# version 05
# speedup vs baseline: 1.1077x; 1.0628x over previous
import math
import contextlib
import numpy as np

import concourse.bass as bass
import concourse.mybir as mybir
from concourse.bass_utils import run_bass_kernel_spmd
from concourse.library_overlay import lower_extended_insts
from concourse import library_config

F32 = mybir.dt.float32
BF16 = mybir.dt.bfloat16
U32 = mybir.dt.uint32
ALU = mybir.AluOpType
AX = mybir.AxisListType
AF = mybir.ActivationFunctionType

ENGINES = ("sync", "vector", "scalar", "tensor", "gpsimd")

LN2PI = 2.0 * math.pi
TWO_PI_HI = np.float32(6.28125)
TWO_PI_LO = np.float32(2.0 * math.pi - 6.28125)


def cdiv(a, b):
    return (a + b - 1) // b


class Sem:
    def __init__(self, handle, name):
        self.h = handle
        self.name = name
        self.n = 0
        self.ndma = 0
        self.issuer = None
        self.fenced = set()

    def bump(self, k=1):
        self.n += k
        return self.n


class PB:

    def __init__(self, nc, resolved=None):
        self.nc = nc
        self.label = "init"
        self.thunks = {e: [] for e in ENGINES}
        self._stack = contextlib.ExitStack()
        self.sems = {}
        self.resolved = resolved or {}
        self.dma_names = {}
        self.inst_dbg = {}

    def _add(self, eng, fn):
        fn._label = self.label
        self.thunks[eng].append(fn)

    def sem(self, name):
        if name not in self.sems:
            h = self._stack.enter_context(self.nc.semaphore(name))
            self.sems[name] = Sem(h, name)
        return self.sems[name]

    def push_scope(self):
        self._scopes = getattr(self, "_scopes", [])
        self._scopes.append(contextlib.ExitStack())

    def pop_scope(self):
        self._scopes.pop().close()

    def _cur(self):
        sc = getattr(self, "_scopes", None)
        return sc[-1] if sc else self._stack

    def sbuf(self, name, shape, dtype):
        return self._cur().enter_context(
            self.nc.sbuf_tensor(name, shape, dtype))

    def psum(self, name, shape, dtype):
        return self._cur().enter_context(
            self.nc.psum_tensor(name, shape, dtype))

    BPCHECK = ("tensor_tensor", "tensor_add", "tensor_sub", "tensor_mult",
               "tensor_tensor_scan", "scalar_tensor_tensor", "select")

    @staticmethod
    def _bp(ap):
        try:
            if not isinstance(ap, bass.AP) or str(ap.space) != \
                    "MemorySpace.SBUF":
                return None
        except Exception:
            return None
        try:
            return ap.base_partition()
        except Exception:
            try:
                return ap.offset // ap.ap[0][0]
            except Exception:
                return None

    def _bpcheck(self, eng, meth, args):
        import traceback
        if eng not in ("vector", "scalar"):
            return
        bps = [b for b in (self._bp(a) for a in args) if b is not None]
        bad = ""
        if meth in self.BPCHECK:
            ins = [b for b in (self._bp(a) for a in args[1:])
                   if b is not None]
            if len(set(ins)) > 1:
                bad = f"input bases differ {ins}"
        for b in bps:
            if b not in (0, 32, 64, 96):
                bad += f" bad base {b}"
        if bad:
            fr = traceback.extract_stack()[-4]
            print(f"BPVIOLATION {meth} {bad} at {fr.filename}:{fr.lineno}")

    def op(self, eng, meth, *args, inc=None, **kw):
        self._bpcheck(eng, meth, args)
        tgt = None
        if inc is not None:
            sem, k = inc
            tgt = sem.bump(k)
            h = sem.h

            def fn(e, meth=meth, args=args, kw=kw, h=h, k=k):
                inst = getattr(e, meth)(*args, **kw)
                inst.then_inc(h, k)
                self.inst_dbg[inst.ins.name] = fn._dbg
        else:
            def fn(e, meth=meth, args=args, kw=kw):
                inst = getattr(e, meth)(*args, **kw)
                try:
                    self.inst_dbg[inst.ins.name] = fn._dbg
                except AttributeError:
                    pass
        import traceback
        fn._dbg = (meth, traceback.extract_stack()[-3:-1])
        self._add(eng, fn)
        if inc is not None and eng in ("vector", "scalar", "gpsimd",
                                       "tensor"):
            self._add(eng, lambda e: e.drain())
        return tgt

    def raw(self, eng, fn):
        self._add(eng, fn)

    def dma(self, eng, out, in_, sem, **kw):
        idx = sem.ndma
        sem.ndma += 1
        sem.n += 16
        h = sem.h
        name = sem.name
        assert sem.issuer in (None, eng), (name, sem.issuer, eng)
        sem.issuer = eng

        def fn(e, out=out, in_=in_, kw=kw, h=h, name=name, idx=idx):
            inst = e.dma_start(out=out, in_=in_, **kw)
            inst.then_inc(h, 16)
            self.dma_names[(name, idx)] = inst.ins.name
        import traceback
        fn._dbg = ("dma", traceback.extract_stack()[-3:-1])
        self._add(eng, fn)
        return idx + 1

    def dma_dyn(self, eng, sem, build_fn):
        idx = sem.ndma
        sem.ndma += 1
        sem.n += 16
        h = sem.h
        name = sem.name

        assert sem.issuer in (None, eng)
        sem.issuer = eng

        def fn(e, h=h, name=name, idx=idx):
            out, in_ = build_fn(e)
            inst = e.dma_start(out=out, in_=in_,
                               allow_slow_non_contiguous=True)
            inst.then_inc(h, 16)
            self.dma_names[(name, idx)] = inst.ins.name
        import traceback
        fn._dbg = ("dma_dyn", traceback.extract_stack()[-3:-1])
        self._add(eng, fn)
        return idx + 1

    def wait_dma(self, eng, sem, token=None):
        t = sem.ndma if token is None else token
        name = sem.name
        h = sem.h
        guess = sem.n

        def fn(e, h=h, name=name, t=t, guess=guess):
            v = self.resolved.get((name, t), guess) if t > 0 else 0
            e.wait_ge(h, v)
        if t > 0 and t not in sem.fenced and sem.issuer is not None:
            sem.fenced.add(t)
            if sem.issuer != eng:
                self.thunks[sem.issuer].append(fn)
        self._add(eng, fn)

    def wait(self, eng, sem, val=None):
        v = sem.n if val is None else val
        h = sem.h
        self._add(eng, lambda e, h=h, v=v: e.wait_ge(h, v))

    def drain(self, eng):
        self._add(eng, lambda e: e.drain())

    def barrier(self):
        s = self.sem("bar")
        tgt = s.n + len(ENGINES)
        for e in ENGINES:
            s.bump(1)
            h = s.h
            self._add(e, lambda eng, h=h: eng.nop().then_inc(h, 1))
            self._add(e, lambda eng, h=h, v=tgt: eng.wait_ge(h, v))




def ln_stats(pb, x_tile, scratch, mean_col, rstd_col, D, eps, sq_scratch):
    s_sum = scratch[:, 0:1]
    s_sq = scratch[:, 1:2]
    s_var = scratch[:, 2:3]
    mean_col = mean_col[:, :]
    rstd_col = rstd_col[:, :]
    x_tile = x_tile[:, :]
    pb.op("vector", "reduce_sum", s_sum, x_tile, axis=AX.X)
    pb.op("vector", "scalar_tensor_tensor", sq_scratch[:, :], x_tile, 1.0,
          x_tile, ALU.bypass, ALU.mult, accum_out=s_sq)
    pb.drain("vector")
    pb.op("vector", "tensor_scalar", mean_col, s_sum, 1.0 / D, None, ALU.mult)
    pb.drain("vector")
    pb.op("vector", "scalar_tensor_tensor", s_var, mean_col, 1.0, mean_col,
          ALU.bypass, ALU.mult)
    pb.drain("vector")
    pb.op("vector", "tensor_scalar", s_var, s_var, -1.0, eps, ALU.mult, ALU.add)
    pb.drain("vector")
    pb.op("vector", "scalar_tensor_tensor", s_var, s_sq, 1.0 / D, s_var,
          ALU.mult, ALU.add)
    pb.drain("vector")
    lnv = pb.sem("ln_v")
    lns = pb.sem("ln_s")
    pb.op("vector", "reciprocal", rstd_col, s_var, inc=(lnv, 1))
    pb.wait("scalar", lnv)
    pb.op("scalar", "activation", rstd_col, rstd_col, AF.Sqrt, inc=(lns, 1))
    pb.wait("vector", lns)


def softplus(pb, in_, tmp, inc=None):
    pb.op("scalar", "activation", tmp, in_, AF.Abs)
    pb.drain("scalar")
    pb.op("scalar", "activation", tmp, tmp, AF.Exp, scale=-1.0)
    pb.drain("scalar")
    pb.op("scalar", "activation", tmp, tmp, AF.Ln, bias=1.0, inc=inc)


def softplus_finish(pb, out, in_, tmp, inc=None):
    pb.op("vector", "scalar_tensor_tensor", out, in_, 0.0, tmp, ALU.max,
          ALU.add, inc=inc)




def build_identity(pb, ident):
    pb.op("gpsimd", "memset", ident[:], 1.0)
    pb.drain("gpsimd")
    pb.op("gpsimd", "affine_select", ident[:], ident[:], pattern=[[1, 128]],
          base=0, channel_multiplier=-1, compare_op=ALU.is_equal, fill=0.0)
    pb.drain("gpsimd")


class Kern:

    def __init__(self, cfg, resolved=None):
        self.cfg = cfg
        nc = bass.Bass("TRN2", target_bir_lowering=False)
        nc.cache_partition_id()
        self.nc = nc
        self.pb = PB(nc, resolved)
        self.inputs = {}
        self.internal = {}

    def dram_in(self, name, shape, dtype=F32):
        if name in self.inputs:
            return self.inputs[name]
        t = self.nc.dram_tensor(name, list(shape), dtype, kind="ExternalInput")
        self.inputs[name] = t.ap()
        return self.inputs[name]

    def dram_tmp(self, name, shape, dtype):
        t = self.nc.dram_tensor(name, list(shape), dtype, kind="Internal")
        self.internal[name] = t.ap()
        return self.internal[name]

    def resolve_dma_totals(self):
        name_of = {v: k for k, v in self.pb.dma_names.items()}
        run = {}
        totals = {}
        for inst in self.nc.all_instructions():
            si = getattr(inst, "sync_info", None)
            upds = si.on_update if si is not None else []
            for u in upds:
                if u.ant_name and u.update_value:
                    run[u.ant_name] = run.get(u.ant_name, 0) + u.update_value
            key = name_of.get(getattr(inst, "name", None))
            if key is not None:
                totals[(key[0], key[1] + 1)] = run.get(key[0], 0)
        by_sem = {}
        for (s, i) in sorted(totals, key=lambda k: k[1]):
            by_sem.setdefault(s, []).append(i)
        for s, idxs in by_sem.items():
            mx = 0
            for i in idxs:
                mx = max(mx, totals[(s, i)])
                totals[(s, i)] = mx
        return totals

    def finish(self):
        nc, pb = self.nc, self.pb
        block = pb._stack.enter_context(nc.Block())
        for ename in ENGINES:
            thunks = pb.thunks[ename]
            section = getattr(block, ename)

            def make(thunks):
                def body(e):
                    cur = None
                    sid = None
                    for i, fn in enumerate(thunks):
                        lab = getattr(fn, "_label", None)
                        if lab != cur:
                            if cur is not None:
                                nc.leave_named_scope(cur, sid, False)
                            cur, sid = lab, None
                            if cur is not None:
                                sid, _ = nc.enter_named_scope(cur, False)
                        try:
                            fn(e)
                        except Exception:
                            import sys
                            print(f"THUNK FAIL eng={e.engine} idx={i} "
                                  f"info={getattr(fn, '_dbg', None)}",
                                  file=sys.stderr)
                            raise
                    if cur is not None:
                        nc.leave_named_scope(cur, sid, False)
                return body
            section(make(thunks))
        pb._stack.close()
        lower_extended_insts(nc)
        return nc



def emit_weight_load(K, dram_ap, sb_tile, rows, cols, sem=None):
    pb = K.pb
    gld = pb.sem("gld")
    v = None
    for r in range(rows // 128):
        v = pb.dma("gpsimd", sb_tile[:, r * cols:(r + 1) * cols],
                   dram_ap[r * 128:(r + 1) * 128, :], gld)
    return (gld, v)


def tile_transpose(K, src_bf16, dst_psum_bf16, inc=None):
    K.pb.op("tensor", "transpose", dst_psum_bf16, src_bf16, K.ident[:, :],
            inc=inc)


def emit_sincos(K, ang, cos_out, sin_out, rows, W, extra_bias=None):
    pb = K.pb
    CW = 1024
    s_v = pb.sem("rr_v")
    s_s = pb.sem("rr_s")
    for out, bias in ((sin_out, 0.0), (cos_out, math.pi / 2)):
        for w0 in range(0, W, CW):
            w1 = min(w0 + CW, W)
            ww = w1 - w0
            sc = K.rr_f32
            a = sc[0:rows, 0:ww]
            f = sc[0:rows, CW:CW + ww]
            cf = sc[0:rows, 2 * CW:2 * CW + ww]
            adj = sc[0:rows, 3 * CW:3 * CW + ww]
            ic = K.rr_i32[0:rows, 0:ww]
            pb.wait("vector", s_s)
            if extra_bias is not None:
                pb.op("vector", "tensor_scalar", a, ang[0:rows, w0:w1],
                      extra_bias, bias, ALU.add, ALU.add)
            else:
                pb.op("vector", "tensor_scalar", a, ang[0:rows, w0:w1],
                      bias, None, ALU.add)
            pb.drain("vector")
            pb.op("vector", "tensor_scalar", f, a, 1.0 / LN2PI, None, ALU.mult)
            pb.drain("vector")
            pb.op("vector", "tensor_copy", ic, f)
            pb.drain("vector")
            pb.op("vector", "tensor_copy", cf, ic)
            pb.drain("vector")
            pb.op("vector", "tensor_sub", f, f, cf)
            pb.drain("vector")
            pb.op("vector", "tensor_scalar", adj, f, 0.5, None, ALU.is_ge)
            pb.drain("vector")
            pb.op("vector", "tensor_add", cf, cf, adj)
            pb.drain("vector")
            pb.op("vector", "tensor_scalar", adj, f, -0.5, None, ALU.is_le)
            pb.drain("vector")
            pb.op("vector", "tensor_sub", cf, cf, adj)
            pb.drain("vector")
            pb.op("vector", "scalar_tensor_tensor", a, cf,
                  -float(TWO_PI_HI), a, ALU.mult, ALU.add)
            pb.drain("vector")
            pb.op("vector", "scalar_tensor_tensor", a, cf,
                  -float(TWO_PI_LO), a, ALU.mult, ALU.add, inc=(s_v, 1))
            pb.wait("scalar", s_v)
            pb.op("scalar", "activation", out[0:rows, w0:w1], a, AF.Sin,
                  inc=(s_s, 1))
    pb.wait("vector", s_s)



def emit_l0_attn(K):
    cfg, pb = K.cfg, K.pb
    T, TL, D, HL, HD, G = (cfg["T"], cfg["TL"], cfg["D"], cfg["HL"],
                           cfg["HD"], cfg["G"])
    NT = TL // 128
    KC = D // 128
    MY = HL * HD
    MYC = MY // 128
    NG = T // G
    GT = G // 128
    VW = HL * (HD + 1)
    eps = 1e-5
    Xd = K.x_dram
    K.dram_in("wqkvT", [D, 3 * MY], F32)
    K.dram_in("wprojT", [MY, D], F32)
    K.dram_in("deltaT", [HD, HL], F32)

    xn_tok = pb.sbuf("l0_xn_tok", [128, D], BF16)
    sq = pb.sbuf("l0_sq", [128, D], F32)
    stats = pb.sbuf("l0_stats", [128, 8], F32)
    mean_c = pb.sbuf("l0_mean", [128, 1], F32)
    rstd_c = pb.sbuf("l0_rstd", [128, 1], F32)
    xfm = pb.sbuf("l0_xfm", [128, KC * 128], BF16)
    wqkv = pb.sbuf("l0_wqkv", [128, KC * 3 * MY], BF16)
    wproj = pb.sbuf("l0_wproj", [128, MYC * D], BF16)
    cpos = pb.sbuf("l0_cpos", [128, G], BF16)
    spos = pb.sbuf("l0_spos", [128, G], BF16)
    ckh = pb.sbuf("l0_ckh", [128, T], BF16)
    skh = pb.sbuf("l0_skh", [128, T], BF16)
    invf = pb.sbuf("l0_invf", [64, 1], F32)
    muq = pb.sbuf("l0_muq", [128, MYC * G], BF16)
    qhat = pb.sbuf("l0_qhat", [128, G], BF16)
    ptile = pb.sbuf("l0_p", [128, G], BF16)
    tri = pb.sbuf("l0_tri", [128, GT * G], BF16)
    ygrp = pb.sbuf("l0_ygrp", [128, MYC * G], BF16)
    denb = pb.sbuf("l0_denb", [HD, G], F32)
    den1 = pb.sbuf("l0_den1", [1, G], F32)
    ostage = pb.sbuf("l0_ostage", [128, (G // 128) * D], BF16)
    obf = pb.sbuf("l0_obf", [128, D], BF16)
    sptmp = pb.sbuf("l0_sptmp", [128, G], F32)
    xn_grp = pb.sbuf("l0_xngrp", [128, KC * G], BF16)
    delta_sb = pb.sbuf("l0_delta", [64, HL], F32)
    otok = pb.sbuf("l0_otok", [128, D], F32)

    ps_t = pb.psum("l0_pst", [128, 1024], BF16)
    ps_a = pb.psum("l0_psa", [128, 512], F32)
    ps_b = pb.psum("l0_psb", [128, 512], F32)
    ps_s0 = pb.psum("l0_pss0", [128, 512], F32)
    ps_s1 = pb.psum("l0_pss1", [128, 512], F32)
    ps_y = pb.psum("l0_psy", [128, 512], F32)

    dr_xn_in = K.dram_tmp("l0_xn_in", [KC * 128, TL], BF16)
    dr_xn_all = K.dram_tmp("l0_xn_all", [2, KC * 128, TL], BF16)
    dr_oa_in = K.dram_tmp("l0_oa_in", [T, D], BF16)
    dr_oa_my = K.dram_tmp("l0_oa_my", [TL, D], BF16)
    dr_ktab = K.dram_tmp("l0_ktab", [HL, 2, 64, T], BF16)
    dr_ptab = K.dram_tmp("l0_ptab", [2, 64, T], BF16)

    s_ld = pb.sem("l0_ld")
    s_v = pb.sem("l0_v")
    s_s = pb.sem("l0_s")
    s_t = pb.sem("l0_t")
    s_g = pb.sem("l0_g")
    s_cc = pb.sem("l0_cc")

    pb.push_scope()
    ang = pb.sbuf("l0_ang", [64, T], F32)
    pb.op("gpsimd", "iota", invf[:, 0:1], pattern=[[1, 1]], base=0,
          channel_multiplier=1, allow_small_or_imprecise_dtypes=True,
          inc=(s_g, 1))
    pb.wait("scalar", s_g)
    pb.op("scalar", "activation", invf[:, 0:1], invf[:, 0:1], AF.Exp,
          scale=-math.log(10000.0) / HD, inc=(s_s, 1))
    pb.op("gpsimd", "iota", ang[:, :], pattern=[[1, T]], base=0,
          channel_multiplier=0, allow_small_or_imprecise_dtypes=True,
          inc=(s_g, 1))
    pb.wait("vector", s_s)
    pb.wait("vector", s_g)
    pb.op("vector", "tensor_scalar", ang[:, :], ang[:, :], invf[:, 0:1], None,
          ALU.mult)
    pb.drain("vector")
    emit_sincos(K, ang, ckh, skh, 64, T)
    pb.wait("sync", pb.sem("rr_s"))
    pb.dma("sync", dr_ptab[0], ckh[0:64, :], s_ld)
    pb.dma("sync", dr_ptab[1], skh[0:64, :], s_ld)
    pb.wait_dma("scalar", s_ld)
    pb.wait_dma("vector", s_ld)
    lv = pb.dma("sync", delta_sb[:, :], K.inputs["deltaT"], s_ld)
    pb.wait_dma("vector", s_ld, lv)
    pb.op("vector", "tensor_scalar", delta_sb[:, :], delta_sb[:, :],
          -LN2PI, 0.0, ALU.max, ALU.min)
    pb.drain("vector")
    pb.op("vector", "nop", inc=(s_v, 1))
    for h in range(HL):
        emit_sincos(K, ang, ckh, skh, 64, T,
                    extra_bias=delta_sb[:, h:h + 1])
        pb.wait("sync", pb.sem("rr_s"))
        for which, buf in ((0, ckh), (1, skh)):
            pb.dma("sync", dr_ktab[h, which], buf[0:64, :], s_ld)
        pb.wait_dma("scalar", s_ld)
        pb.wait_dma("vector", s_ld)
    for j in range(GT):
        sl = tri[:, j * G:(j + 1) * G]
        pb.op("gpsimd", "memset", sl, 1.0)
        pb.drain("gpsimd")
        pb.op("gpsimd", "affine_select", sl, sl, pattern=[[1, G]],
              base=-128 * j, channel_multiplier=-1, compare_op=ALU.is_ge,
              fill=0.0)
        pb.drain("gpsimd")
    pb.op("gpsimd", "nop", inc=(s_g, 1))
    pb.pop_scope()
    muk = pb.sbuf("l0_muk", [128, MYC * T], BF16)
    vtok = pb.sbuf("l0_vtok", [128, (T // 128) * VW], BF16)
    khat = pb.sbuf("l0_khat", [128, T], BF16)
    emit_weight_load(K, K.inputs["wqkvT"], wqkv, D, 3 * MY)
    gld, wv = emit_weight_load(K, K.inputs["wprojT"], wproj, MY, D)

    for t in range(NT):
        lv = pb.dma("sync", sq[:, :], Xd[t * 128:(t + 1) * 128, :], s_ld)
        pb.wait_dma("vector", s_ld, lv)
        ln_stats(pb, sq[:, :], stats, mean_c, rstd_c, D, eps, xn_tok)
        pb.op("vector", "tensor_scalar", xn_tok[:, :], sq[:, :],
              mean_c[:, 0:1], rstd_c[:, 0:1], ALU.subtract, ALU.mult,
              inc=(s_v, 1))
        pb.wait("tensor", s_v)
        for c in range(KC):
            slot = c % 4
            dst = ps_t[:, slot * 128:slot * 128 + 128]
            tile_transpose(K, xn_tok[:, c * 128:(c + 1) * 128], dst,
                           inc=(s_t, 1))
            pb.wait("vector", s_t)
            pb.op("vector", "tensor_copy", xfm[:, c * 128:(c + 1) * 128],
                  dst, inc=(s_v, 1))
            pb.wait("tensor", s_v)
        pb.wait("sync", s_v)
        for c in range(KC):
            pb.dma("sync",
                   dr_xn_in[c * 128:(c + 1) * 128, t * 128:(t + 1) * 128],
                   xfm[:, c * 128:(c + 1) * 128], s_ld)
    pb.raw("gpsimd", lambda g: None)
    pb.wait_dma("gpsimd", s_ld)
    pb.op("gpsimd", "collective_compute", "AllGather", ALU.bypass,
          replica_groups=cfg["groups"], ins=[dr_xn_in], outs=[dr_xn_all],
          inc=(s_cc, 1))

    pb.op("vector", "memset", vtok[:, :], 0.0)
    pb.drain("vector")
    last = (T // 128 - 1, HL - 1)
    for tc in range(T // 128):
        for h in range(HL):
            base = tc * VW + h * (HD + 1) + HD
            pb.op("vector", "memset", vtok[:, base:base + 1], 1.0,
                  inc=(s_v, 1) if (tc, h) == last else None)
    pb.wait("sync", s_cc)
    pb.wait_dma("tensor", gld, wv)
    pb.wait("tensor", s_v)
    for kg in range(NG):
        slot = (kg * G) // TL
        col0 = (kg * G) % TL
        pb.wait("sync", s_t)
        lv = None
        for c in range(KC):
            lv = pb.dma("sync", xn_grp[:, c * G:(c + 1) * G],
                        dr_xn_all[slot, c * 128:(c + 1) * 128,
                                  col0:col0 + G], s_ld)
        pb.wait_dma("tensor", s_ld, lv)
        for which, wofs in (("k", MY), ("v", 2 * MY)):
            for mc in range(MYC):
                ps = ps_a if (mc % 2 == 0) else ps_b
                for c in range(KC):
                    stat = wqkv[:, c * 3 * MY + wofs + mc * 128:
                                c * 3 * MY + wofs + (mc + 1) * 128]
                    pb.op("tensor", "ldweights", stat)
                    pb.op("tensor", "matmul", ps[:, 0:G], stat,
                          xn_grp[:, c * G:(c + 1) * G], start=(c == 0),
                          stop=(c == KC - 1),
                          inc=(s_t, 1) if c == KC - 1 else None)
                if which == "k":
                    pb.wait("scalar", s_t)
                    softplus(pb, ps[:, 0:G], sptmp[:, 0:G], inc=(s_s, 1))
                    pb.wait("vector", s_s)
                    softplus_finish(
                        pb, muk[:, mc * T + kg * G: mc * T + (kg + 1) * G],
                        ps[:, 0:G], sptmp[:, 0:G], inc=(s_v, 1))
                    pb.wait("tensor", s_v)
                else:
                    pb.wait("vector", s_t)
                    pb.op("vector", "tensor_copy", xn_tok[:, 0:G], ps[:, 0:G],
                          inc=(s_v, 1))
                    pb.wait("tensor", s_v)
                    for q in range(GT):
                        slot2 = q % 4
                        dps = ps_t[:, 512 + slot2 * 128:
                                   512 + slot2 * 128 + 128]
                        tile_transpose(K, xn_tok[:, q * 128:(q + 1) * 128],
                                       dps, inc=(s_t, 1))
                        pb.wait("vector", s_t)
                        tc = kg * GT + q
                        for hh in range(128 // HD):
                            h = mc * (128 // HD) + hh
                            pb.op("vector", "tensor_copy",
                                  vtok[:, tc * VW + h * (HD + 1):
                                       tc * VW + h * (HD + 1) + HD],
                                  dps[:, hh * HD:(hh + 1) * HD],
                                  inc=(s_v, 1))
                        pb.wait("tensor", s_v)
    pb.barrier()

    for g in range(NG):
        nkt = (g + 1) * GT
        slot = (g * G) // TL
        col0 = (g * G) % TL
        pb.wait("sync", s_t)
        lv = None
        for c in range(KC):
            lv = pb.dma("sync", xn_grp[:, c * G:(c + 1) * G],
                        dr_xn_all[slot, c * 128:(c + 1) * 128,
                                  col0:col0 + G], s_ld)
        pb.wait_dma("tensor", s_ld, lv)
        pb.wait("sync", s_v)
        for r0d in (0, 64):
            pb.dma("sync", cpos[r0d:r0d + 64, :],
                   dr_ptab[0, :, g * G:(g + 1) * G], s_ld)
            pb.dma("sync", spos[r0d:r0d + 64, :],
                   dr_ptab[1, :, g * G:(g + 1) * G], s_ld)
        pb.wait_dma("vector", s_ld)
        for mc in range(MYC):
            ps = ps_a if (mc % 2 == 0) else ps_b
            for c in range(KC):
                stat = wqkv[:, c * 3 * MY + mc * 128:
                            c * 3 * MY + (mc + 1) * 128]
                pb.op("tensor", "ldweights", stat)
                pb.op("tensor", "matmul", ps[:, 0:G], stat,
                      xn_grp[:, c * G:(c + 1) * G], start=(c == 0),
                      stop=(c == KC - 1),
                      inc=(s_t, 1) if c == KC - 1 else None)
            pb.wait("scalar", s_t)
            softplus(pb, ps[:, 0:G], sptmp[:, 0:G], inc=(s_s, 1))
            pb.wait("vector", s_s)
            softplus_finish(pb, muq[:, mc * G:(mc + 1) * G], ps[:, 0:G],
                            sptmp[:, 0:G], inc=(s_v, 1))
            pb.wait("tensor", s_v)
        for h in range(HL):
            ctx = nkt * 128
            pb.wait("sync", s_v)
            mc0, r0 = divmod(h * HD, 128)
            lv = pb.dma("sync", ckh[r0:r0 + HD, 0:ctx],
                        dr_ktab[h, 0, :, 0:ctx], s_ld)
            lv = pb.dma("sync", skh[r0:r0 + HD, 0:ctx],
                        dr_ktab[h, 1, :, 0:ctx], s_ld)
            pb.wait_dma("vector", s_ld, lv)
            mkv = muk[r0:r0 + HD, mc0 * T:mc0 * T + ctx]
            pb.op("vector", "tensor_tensor", khat[0:HD, 0:ctx], mkv,
                  ckh[r0:r0 + HD, 0:ctx], ALU.mult)
            pb.op("vector", "tensor_tensor", khat[HD:128, 0:ctx], mkv,
                  skh[r0:r0 + HD, 0:ctx], ALU.mult)
            mqv = muq[r0:r0 + HD, mc0 * G:(mc0 + 1) * G]
            pb.op("vector", "tensor_tensor", qhat[0:HD, :], mqv,
                  cpos[r0:r0 + HD, :], ALU.mult)
            pb.op("vector", "tensor_tensor", qhat[HD:128, :], mqv,
                  spos[r0:r0 + HD, :], ALU.mult, inc=(s_v, 1))
            pb.wait("tensor", s_v)
            for kt in range(nkt):
                ps_sc = ps_s0 if kt % 2 == 0 else ps_s1
                pb.op("tensor", "ldweights",
                      khat[:, kt * 128:(kt + 1) * 128])
                pb.op("tensor", "matmul", ps_sc[:, 0:G],
                      khat[:, kt * 128:(kt + 1) * 128], qhat[:, :],
                      start=True, stop=True, inc=(s_t, 1))
                pb.wait("scalar", s_t)
                pb.op("scalar", "activation", ptile[:, :], ps_sc[:, 0:G],
                      AF.Exp, scale=1.0 / math.sqrt(HD), inc=(s_s, 1))
                diag = kt - g * GT
                pb.wait("vector", s_s)
                if diag >= 0:
                    pb.op("vector", "tensor_tensor", ptile[:, :],
                          ptile[:, :], tri[:, diag * G:(diag + 1) * G],
                          ALU.mult, inc=(s_v, 1))
                else:
                    pb.op("vector", "nop", inc=(s_v, 1))
                pb.wait("tensor", s_v)
                pb.op("tensor", "ldweights",
                      vtok[:, kt * VW + h * (HD + 1):
                           kt * VW + (h + 1) * (HD + 1)])
                pb.op("tensor", "matmul", ps_y[0:HD + 1, 0:G],
                      vtok[:, kt * VW + h * (HD + 1):
                           kt * VW + (h + 1) * (HD + 1)],
                      ptile[:, :], start=(kt == 0), stop=(kt == nkt - 1),
                      inc=(s_t, 1))
                pb.wait("scalar", s_t)
            pb.wait("vector", s_t)
            pb.op("vector", "reciprocal", den1[:, :], ps_y[HD:HD + 1, 0:G],
                  inc=(s_v, 1))
            pb.wait("gpsimd", s_v)
            pb.op("gpsimd", "partition_broadcast", denb[:, :], den1[:, :],
                  channels=HD, inc=(s_g, 1))
            pb.wait("vector", s_g)
            pb.op("vector", "tensor_tensor",
                  ygrp[r0:r0 + HD, mc0 * G:(mc0 + 1) * G], ps_y[0:HD, 0:G],
                  denb[:, :], ALU.mult, inc=(s_v, 1))
            pb.wait("tensor", s_v)
        pb.wait_dma("vector", s_ld)
        DH = D // 2
        for j in range(G // 128):
            for c in range(MYC):
                stat = ygrp[:, c * G + j * 128:c * G + j * 128 + 128]
                pb.op("tensor", "ldweights", stat)
                pb.op("tensor", "matmul", ps_a[:, 0:DH], stat,
                      wproj[:, c * D:c * D + DH], start=(c == 0),
                      stop=(c == MYC - 1),
                      inc=(s_t, 1) if c == MYC - 1 else None)
                pb.op("tensor", "matmul", ps_b[:, 0:DH], stat,
                      wproj[:, c * D + DH:(c + 1) * D], start=(c == 0),
                      stop=(c == MYC - 1),
                      inc=(s_t, 1) if c == MYC - 1 else None)
            pb.wait("vector", s_t)
            pb.op("vector", "tensor_copy", ostage[:, j * D:j * D + DH],
                  ps_a[:, 0:DH], inc=(s_v, 1))
            pb.op("vector", "tensor_copy", ostage[:, j * D + DH:(j + 1) * D],
                  ps_b[:, 0:DH], inc=(s_v, 1))
            pb.wait("tensor", s_v)
        pb.wait("sync", s_v)
        for j in range(G // 128):
            pb.dma("sync",
                   dr_oa_in[g * G + j * 128:g * G + (j + 1) * 128, :],
                   ostage[:, j * D:(j + 1) * D], s_ld)
    if cfg.get("dbg_l0"):
        dbs = pb.sem("dbg_st")
        for nm, buf, shp in (("dbg_muk", muk, [128, MYC * T]),
                             ("dbg_vtok", vtok, [128, (T // 128) * VW]),
                             ("dbg_ygrp", ygrp, [128, MYC * G])):
            o = K.nc.dram_tensor(nm, shp, BF16, kind="ExternalOutput").ap()
            pb.dma("sync", o[:, :], buf[:, :], dbs)
        o = K.nc.dram_tensor("dbg_oain", [T, D], BF16,
                             kind="ExternalOutput").ap()
        pb.wait_dma("sync", s_ld)
        pb.dma("sync", o[:, :], dr_oa_in[:, :], dbs)
        o = K.nc.dram_tensor("dbg_xnall", [2, KC * 128, TL], BF16,
                             kind="ExternalOutput").ap()
        pb.dma("sync", o, dr_xn_all, dbs)
        pb.wait_dma("sync", dbs)
    pb.wait_dma("gpsimd", s_ld)
    pb.op("gpsimd", "collective_compute", "ReduceScatter", ALU.add,
          replica_groups=cfg["groups"], ins=[dr_oa_in], outs=[dr_oa_my],
          inc=(s_cc, 1))
    pb.wait("sync", s_cc)
    for t in range(NT):
        pb.wait("sync", s_v)
        pb.wait_dma("sync", s_ld)
        lvo = pb.dma("sync", obf[:, :], dr_oa_my[t * 128:(t + 1) * 128, :],
                     s_ld)
        lvx = pb.dma("sync", sq[:, :], Xd[t * 128:(t + 1) * 128, :], s_ld)
        pb.wait_dma("vector", s_ld, lvx)
        pb.op("vector", "tensor_copy", otok[:, :], obf[:, :],
              inc=(s_v, 1))
        pb.drain("vector")
        ln_stats(pb, otok[:, :], stats, mean_c, rstd_c, D, eps, xn_tok)
        pb.op("vector", "tensor_scalar", otok[:, :], otok[:, :],
              mean_c[:, 0:1], rstd_c[:, 0:1], ALU.subtract, ALU.mult)
        pb.drain("vector")
        pb.op("vector", "tensor_add", sq[:, :], sq[:, :], otok[:, :],
              inc=(s_v, 1))
        pb.wait("sync", s_v)
        pb.dma("sync", Xd[t * 128:(t + 1) * 128, :], sq[:, :], s_ld)
    pb.wait_dma("sync", s_ld)
    pb.barrier()



def setup_common(K):
    pb = K.pb
    cfg = K.cfg

    K.ident = pb.sbuf("c_ident", [128, 128], BF16)
    K.rr_f32 = pb.sbuf("c_rrf", [128, 4 * 1024], F32)
    K.rr_i32 = pb.sbuf("c_rri", [128, 1024], mybir.dt.int32)
    K.x_dram = K.dram_tmp("x_work", [cfg["TL"], cfg["D"]], F32)
    x_in = K.dram_in("x", [cfg["TL"], cfg["D"]], F32)
    s_init = pb.sem("c_init")
    s_initd = pb.sem("c_initd")
    pb.raw("gpsimd", lambda g: g.load_library(library_config.proxy))
    build_identity(pb, K.ident)
    pb.op("gpsimd", "nop", inc=(s_init, 1))
    pb.dma("sync", K.x_dram[:, :], x_in[:, :], s_initd)
    for e in ENGINES:
        pb.wait(e, s_init)
        pb.wait_dma(e, s_initd)


def finalize_out(K):
    pb = K.pb
    cfg = K.cfg
    out = K.nc.dram_tensor("out", [cfg["TL"], cfg["D"]], F32,
                           kind="ExternalOutput").ap()
    s_out = pb.sem("c_out")
    pb.barrier()
    pb.dma("sync", out[:, :], K.x_dram[:, :], s_out)
    pb.wait_dma("sync", s_out)


def _build_once(cfg, resolved):
    K = Kern(cfg, resolved)
    setup_common(K)
    for blk in cfg["blocks"]:
        K.pb.label = blk
        K.pb.push_scope()
        if blk == "l0_attn":
            emit_l0_attn(K)
        elif blk == "l0_ffn":
            emit_l0_ffn(K)
        elif blk == "mamba":
            emit_mamba(K)
        elif blk == "dec":
            emit_l1_ffn(K)
        else:
            raise ValueError(blk)
        K.pb.pop_scope()
    finalize_out(K)
    return K


def build_module_program(cfg):
    K1 = _build_once(cfg, None)
    K1.finish()
    resolved = K1.resolve_dma_totals()
    K2 = _build_once(cfg, resolved)
    nc = K2.finish()
    return K2, nc


FULL_CFG = dict(
    T=4096, TL=2048, D=1024, HL=8, HD=64, G=512,
    groups=[[0, 1], [2, 3], [4, 5], [6, 7]],
    blocks=["l0_attn", "l0_ffn", "mamba", "dec"],
)


def host_shards(cfg, inputs, ncores=8):
    T, TL, D, HL, HD = cfg["T"], cfg["TL"], cfg["D"], cfg["HL"], cfg["HD"]
    MY = HL * HD
    maps = []
    C = np.ascontiguousarray
    for c in range(ncores):
        b, m = c // 2, c % 2
        im = {}
        im["x"] = C(inputs["L0_attn_x"][b, m * TL:(m + 1) * TL, :])
        if "dec" in cfg["blocks"]:
            MY = HL * HD
            KCd = D // 128
            sl = slice(m * MY, (m + 1) * MY)
            for pre, wq, wk, wv, wo in (("1", "L1_ffn_wq1", "L1_ffn_wk1",
                                         "L1_ffn_wv1", "L1_ffn_wo1"),
                                        ("2", "L1_ffn_wq2", "L1_ffn_wk2",
                                         "L1_ffn_wv2", "L1_ffn_wo2")):
                im["d_w" + pre] = C(np.concatenate(
                    [inputs[wq][:, sl], inputs[wk][:, sl],
                     inputs[wv][:, sl]], axis=1))
                im["d_wo" + pre] = C(inputs[wo][sl, :])
            for nm in ("wg1", "wu1", "wg2", "wu2"):
                w = inputs["L1_ffn_" + nm]
                Dd, FFNd = w.shape
                KCd2, FCd = Dd // 128, FFNd // 128
                im["d_" + nm] = C(np.ascontiguousarray(
                    w.reshape(KCd2, 128, FCd, 128).transpose(2, 1, 0, 3)
                    .reshape(FCd, 128, Dd)))
            for nm in ("wd1", "wd2"):
                w = inputs["L1_ffn_" + nm]
                FFNd, Dd = w.shape
                FCd, KCd2 = FFNd // 128, Dd // 128
                im["d_" + nm] = C(np.ascontiguousarray(
                    w.reshape(FCd, 128, KCd2, 128).transpose(2, 1, 0, 3)
                    .reshape(KCd2, 128, FFNd)))
            im["d_ln11"] = C(inputs["L1_ffn_ln1_1"].reshape(1, -1))
            im["d_ln21"] = C(inputs["L1_ffn_ln2_1"].reshape(1, -1))
            im["d_ln12c"] = C(inputs["L1_ffn_ln1_2"].reshape(KCd, 128).T)
            im["d_ln22c"] = C(inputs["L1_ffn_ln2_2"].reshape(KCd, 128).T)
            im["d_pnw"] = C(inputs["L1_ffn_prior_norm_w"].reshape(1, -1))
            im["d_pw"] = C(inputs["L1_ffn_prior_w"])
            im["d_tpos"] = C(np.arange(m * TL, (m + 1) * TL,
                                       dtype=np.float32).reshape(1, -1))
        if "mamba" in cfg["blocks"]:
            DI = cfg.get("DI", 2 * D)
            DIL = DI // 2
            sl = slice(m * DIL, (m + 1) * DIL)
            im["m_Winu"] = C(inputs["L1_attn_W_in"][:, m * DIL:(m + 1) * DIL])
            im["m_Winz"] = C(inputs["L1_attn_W_in"][:, DI + m * DIL:DI + (m + 1) * DIL])
            im["m_convw"] = C(inputs["L1_attn_conv_w"][sl])
            im["m_convb"] = C(inputs["L1_attn_conv_b"][sl].reshape(-1, 1))
            im["m_Wx"] = C(inputs["L1_attn_W_x"][sl])
            im["m_Wdt"] = C(inputs["L1_attn_W_dt"][:, sl])
            im["m_bdt"] = C(inputs["L1_attn_b_dt"][sl].reshape(-1, 1))
            im["m_Alog"] = C(inputs["L1_attn_A_log"][sl])
            im["m_Dskip"] = C(inputs["L1_attn_Dskip"][sl].reshape(-1, 1))
            im["m_Wout"] = C(inputs["L1_attn_W_out"][sl])
            im["m_lng"] = C(inputs["L1_attn_ln_g"].reshape(1, -1))
            im["m_lnb"] = C(inputs["L1_attn_ln_b"].reshape(1, -1))
        if "l0_ffn" in cfg["blocks"]:
            im["rhs"] = C(inputs["L0_ffn_rhs"])
            im["rhsT"] = C(inputs["L0_ffn_rhs"].T)
        if "l0_attn" in cfg["blocks"]:
            wa = inputs["L0_attn_w_attn"]
            D0 = wa.shape[1]
            qkv = np.concatenate([wa[m * MY:(m + 1) * MY, :],
                                  wa[D0 + m * MY:D0 + (m + 1) * MY, :],
                                  wa[2 * D0 + m * MY:2 * D0 + (m + 1) * MY, :]],
                                 axis=0)
            im["wqkvT"] = C(qkv.T)
            im["wprojT"] = C(inputs["L0_attn_w_proj"][:, m * MY:(m + 1) * MY].T)
            im["deltaT"] = C(inputs["L0_attn_delta"][m * HL:(m + 1) * HL, :].T)
        maps.append(im)
    return maps


def kernel(**inputs):
    cfg = FULL_CFG
    K, nc = build_module_program(cfg)
    in_maps = host_shards(cfg, inputs, 8)
    res = run_bass_kernel_spmd(nc, in_maps, core_ids=list(range(8)))
    TL, D = cfg["TL"], cfg["D"]
    out = np.zeros((4, cfg["T"], D), dtype=np.float32)
    for c in range(8):
        b, m = c // 2, c % 2
        out[b, m * TL:(m + 1) * TL, :] = res.results[c]["out"]
    return out



def emit_round_halfaway(pb, K, x_ap, out_ap, itile, ftile, a1, a2):
    itile = itile[:, :]
    ftile = ftile[:, :]
    a1 = a1[:, :]
    a2 = a2[:, :]
    pb.op("vector", "tensor_copy", itile, x_ap)
    pb.drain("vector")
    pb.op("vector", "tensor_copy", ftile, itile)
    pb.drain("vector")
    pb.op("vector", "tensor_sub", a1, x_ap, ftile)
    pb.drain("vector")
    pb.op("vector", "tensor_scalar", a2, a1, 0.5, None, ALU.is_ge)
    pb.drain("vector")
    pb.op("vector", "tensor_add", ftile, ftile, a2)
    pb.drain("vector")
    pb.op("vector", "tensor_scalar", a2, a1, -0.5, None, ALU.is_le)
    pb.drain("vector")
    pb.op("vector", "tensor_sub", out_ap, ftile, a2)
    pb.drain("vector")


def emit_l0_ffn(K):
    cfg, pb = K.cfg, K.pb
    TL, D = cfg["TL"], cfg["D"]
    NT = TL // 128
    KC = D // 128
    eps = 1e-5
    Xd = K.x_dram
    K.dram_in("rhs", [D, D], F32)
    K.dram_in("rhsT", [D, D], F32)

    w_f32 = pb.sbuf("f0_wf", [128, KC * D], F32)
    q_rT = pb.sbuf("f0_qrt", [128, KC * D], BF16)
    q_r = pb.sbuf("f0_qr", [128, KC * D], BF16)
    rs = pb.sbuf("f0_rs", [128, KC], F32)
    itile = pb.sbuf("f0_it", [128, D], mybir.dt.int32)
    ft = pb.sbuf("f0_ft", [128, D], F32)
    a1 = pb.sbuf("f0_a1", [128, D], F32)
    a2 = pb.sbuf("f0_a2", [128, D], F32)
    xn = pb.sbuf("f0_xn", [128, D], F32)
    xq = pb.sbuf("f0_xq", [128, D], BF16)
    xqf = pb.sbuf("f0_xqf", [128, KC * 128], BF16)
    ls = pb.sbuf("f0_ls", [128, 1], F32)
    lsall = pb.sbuf("f0_lsall", [128, NT], F32)
    stats = pb.sbuf("f0_stats", [128, 8], F32)
    mean_c = pb.sbuf("f0_mean", [128, 1], F32)
    rstd_c = pb.sbuf("f0_rstd", [128, 1], F32)
    otok = pb.sbuf("f0_otok", [128, D], F32)
    res = pb.sbuf("f0_res", [128, D], BF16)
    sq = pb.sbuf("f0_sq", [128, D], F32)

    ps_t = pb.psum("f0_pst", [128, 1024], BF16)
    ps_a = pb.psum("f0_psa", [128, 512], F32)
    ps_b = pb.psum("f0_psb", [128, 512], F32)

    s_ld = pb.sem("f0_ld")
    s_v = pb.sem("f0_v")
    s_s = pb.sem("f0_s")
    s_t = pb.sem("f0_t")

    lv = None
    for r in range(KC):
        lv = pb.dma("sync", w_f32[:, r * D:(r + 1) * D],
                    K.inputs["rhsT"][r * 128:(r + 1) * 128, :], s_ld)
    pb.wait_dma("vector", s_ld, lv)
    for r in range(KC):
        sl = w_f32[:, r * D:(r + 1) * D]
        pb.op("vector", "reduce_max", rs[:, r:r + 1], sl, axis=AX.X,
              apply_absolute_value=True)
        pb.drain("vector")
        pb.op("vector", "tensor_scalar_max", rs[:, r:r + 1], rs[:, r:r + 1],
              1e-12)
        pb.drain("vector")
        pb.op("vector", "reciprocal", ls[:, 0:1], rs[:, r:r + 1])
        pb.drain("vector")
        pb.op("vector", "tensor_scalar", ls[:, 0:1], ls[:, 0:1], 127.0, None,
              ALU.mult)
        pb.drain("vector")
        pb.op("vector", "tensor_scalar", rs[:, r:r + 1], rs[:, r:r + 1],
              1.0 / 127.0, None, ALU.mult)
        pb.drain("vector")
        pb.op("vector", "tensor_scalar", sq[:, :], sl, ls[:, 0:1], None,
              ALU.mult)
        pb.drain("vector")
        emit_round_halfaway(pb, K, sq[:, :], sq[:, :], itile, ft, a1, a2)
        pb.op("vector", "tensor_copy", q_rT[:, r * D:(r + 1) * D], sq[:, :],
              inc=(s_v, 1))
    pb.wait("tensor", s_v)
    for r in range(KC):
        for c in range(KC):
            slot = c % 4
            dps = ps_t[:, slot * 128:slot * 128 + 128]
            tile_transpose(K, q_rT[:, r * D + c * 128:r * D + (c + 1) * 128],
                           dps, inc=(s_t, 1))
            pb.wait("vector", s_t)
            pb.op("vector", "tensor_copy",
                  q_r[:, c * D + r * 128:c * D + (r + 1) * 128], dps,
                  inc=(s_v, 1))
            pb.wait("tensor", s_v)
    pb.barrier()

    for t in range(NT):
        pb.wait_dma("sync", s_ld)
        pb.wait("sync", s_v)
        lv = pb.dma("sync", sq[:, :], Xd[t * 128:(t + 1) * 128, :], s_ld)
        pb.wait_dma("vector", s_ld, lv)
        ln_stats(pb, sq[:, :], stats, mean_c, rstd_c, D, eps, xn)
        pb.op("vector", "tensor_scalar", xn[:, :], sq[:, :], mean_c[:, 0:1],
              rstd_c[:, 0:1], ALU.subtract, ALU.mult)
        pb.drain("vector")
        pb.op("vector", "reduce_max", ls[:, 0:1], xn[:, :], axis=AX.X,
              apply_absolute_value=True)
        pb.drain("vector")
        pb.op("vector", "tensor_scalar_max", ls[:, 0:1], ls[:, 0:1], 1e-12)
        pb.drain("vector")
        pb.op("vector", "tensor_copy", lsall[:, t:t + 1], ls[:, 0:1])
        pb.op("vector", "reciprocal", ls[:, 0:1], ls[:, 0:1])
        pb.drain("vector")
        pb.op("vector", "tensor_scalar", xn[:, :], xn[:, :], ls[:, 0:1],
              127.0, ALU.mult, ALU.mult)
        pb.drain("vector")
        emit_round_halfaway(pb, K, xn[:, :], xn[:, :], itile, ft, a1, a2)
        pb.op("vector", "tensor_copy", xq[:, :], xn[:, :], inc=(s_v, 1))
        pb.wait("tensor", s_v)
        for c in range(KC):
            slot = c % 4
            dps = ps_t[:, slot * 128:slot * 128 + 128]
            tile_transpose(K, xq[:, c * 128:(c + 1) * 128], dps, inc=(s_t, 1))
            pb.wait("vector", s_t)
            pb.op("vector", "tensor_copy", xqf[:, c * 128:(c + 1) * 128],
                  dps, inc=(s_v, 1))
            pb.wait("tensor", s_v)
        for oc in range(KC):
            ps = ps_a if oc % 2 == 0 else ps_b
            for c in range(KC):
                stat = q_r[:, c * D + oc * 128:c * D + (oc + 1) * 128]
                pb.op("tensor", "ldweights", stat)
                pb.op("tensor", "matmul", ps[:, 0:128], stat,
                      xqf[:, c * 128:(c + 1) * 128], start=(c == 0),
                      stop=(c == KC - 1),
                      inc=(s_t, 1) if c == KC - 1 else None)
            pb.wait("vector", s_t)
            pb.op("vector", "tensor_scalar", res[:, oc * 128:(oc + 1) * 128]
                  if False else sq[:, oc * 128:(oc + 1) * 128],
                  ps[:, 0:128], rs[:, oc:oc + 1], None, ALU.mult,
                  inc=(s_v, 1))
            pb.wait("tensor", s_v)
        pb.drain("vector")
        pb.op("vector", "tensor_copy", res[:, :], sq[:, :], inc=(s_v, 1))
        pb.wait("tensor", s_v)
        for oc in range(KC):
            slot = oc % 4
            dps = ps_t[:, 512 + slot * 128:512 + slot * 128 + 128]
            tile_transpose(K, res[:, oc * 128:(oc + 1) * 128], dps,
                           inc=(s_t, 1))
            pb.wait("vector", s_t)
            pb.op("vector", "tensor_copy", otok[:, oc * 128:(oc + 1) * 128],
                  dps, inc=(s_v, 1))
            pb.wait("tensor", s_v)
        pb.drain("vector")
        pb.op("vector", "tensor_scalar", otok[:, :], otok[:, :],
              lsall[:, t:t + 1], 1.0 / 127.0, ALU.mult, ALU.mult)
        pb.drain("vector")
        ln_stats(pb, otok[:, :], stats, mean_c, rstd_c, D, eps, xn)
        pb.op("vector", "tensor_scalar", otok[:, :], otok[:, :],
              mean_c[:, 0:1], rstd_c[:, 0:1], ALU.subtract, ALU.mult)
        pb.drain("vector")
        pb.wait("sync", s_v)
        lv = pb.dma("sync", sq[:, :], Xd[t * 128:(t + 1) * 128, :], s_ld)
        pb.wait_dma("vector", s_ld, lv)
        pb.op("vector", "tensor_add", sq[:, :], sq[:, :], otok[:, :],
              inc=(s_v, 1))
        pb.wait("sync", s_v)
        pb.dma("sync", Xd[t * 128:(t + 1) * 128, :], sq[:, :], s_ld)
    pb.wait_dma("sync", s_ld)
    pb.barrier()



def emit_mamba(K):
    cfg, pb = K.cfg, K.pb
    T, TL, D = cfg["T"], cfg["TL"], cfg["D"]
    DI = cfg.get("DI", 2 * D)
    DIL = DI // 2
    MC = DIL // 128
    NT = TL // 128
    KC = D // 128
    DTR = cfg.get("DTR", 64)
    DS = cfg.get("DS", 16)
    W = 512
    TC = T // W
    eps_m = 1e-5
    eps = 1e-5
    Xd = K.x_dram

    K.dram_in("m_Winu", [D, DIL], F32)
    K.dram_in("m_Winz", [D, DIL], F32)
    K.dram_in("m_convw", [DIL, 4], F32)
    K.dram_in("m_convb", [DIL, 1], F32)
    K.dram_in("m_Wx", [DIL, DTR + 2 * DS], F32)
    K.dram_in("m_Wdt", [DTR, DIL], F32)
    K.dram_in("m_bdt", [DIL, 1], F32)
    K.dram_in("m_Alog", [DIL, DS], F32)
    K.dram_in("m_Dskip", [DIL, 1], F32)
    K.dram_in("m_Wout", [DIL, D], F32)
    K.dram_in("m_lng", [1, D], F32)
    K.dram_in("m_lnb", [1, D], F32)

    winu = pb.sbuf("m_winu", [128, KC * DIL], BF16)
    winz = pb.sbuf("m_winz", [128, KC * DIL], BF16)
    wout = pb.sbuf("m_wout", [128, MC * D], BF16)
    wx = pb.sbuf("m_wx", [128, MC * (DTR + 2 * DS)], BF16)
    wdt = pb.sbuf("m_wdt", [DTR, DIL], BF16)
    convw = pb.sbuf("m_cw", [128, MC * 4], F32)
    convb = pb.sbuf("m_cb", [128, MC], F32)
    bdt = pb.sbuf("m_bd", [128, MC], F32)
    Ask = pb.sbuf("m_A", [128, MC * DS], F32)
    dsk = pb.sbuf("m_ds", [128, MC], F32)
    hstate = pb.sbuf("m_h", [128, MC * DS], F32)
    tailb = pb.sbuf("m_tail", [128, MC * 3], BF16)

    sq = pb.sbuf("m_sq", [128, D], F32)
    xp = pb.sbuf("m_xp", [128, D], F32)
    xm_tok = pb.sbuf("m_xmtok", [128, D], BF16)
    stats = pb.sbuf("m_stats", [128, 8], F32)
    mean_c = pb.sbuf("m_mean", [128, 1], F32)
    rstd_c = pb.sbuf("m_rstd", [128, 1], F32)

    ubuf = pb.sbuf("m_ubuf", [128, MC * (W + 3)], BF16)
    ucb = pb.sbuf("m_ucb", [128, MC * W], BF16)
    zb = pb.sbuf("m_zb", [128, MC * W], BF16)
    dtb = pb.sbuf("m_dtb", [128, MC * W], BF16)
    wb = pb.sbuf("m_wb", [128, MC * W], BF16)
    yacc = pb.sbuf("m_yacc", [128, MC * W], F32)
    dbcp = pb.sbuf("m_dbcp", [DTR + 2 * DS, W], F32)
    bbc = pb.sbuf("m_bbc", [128, W], F32)
    cbc = pb.sbuf("m_cbc", [128, W], F32)
    brow0 = pb.sbuf("m_brow0", [1, W], F32)
    crow0 = pb.sbuf("m_crow0", [1, W], F32)
    identf = pb.sbuf("m_identf", [128, 128], F32)
    asc = pb.sbuf("m_asc", [128, 2 * W], BF16)
    bsc = pb.sbuf("m_bsc", [128, W], F32)
    hsc = pb.sbuf("m_hsc", [128, W], F32)
    tsc = pb.sbuf("m_tsc", [128, W], F32)
    xmg = pb.sbuf("m_xmg", [128, KC * W], BF16)
    ydone = pb.sbuf("m_ydone", [128, MC * W], BF16)
    dtr_bf = pb.sbuf("m_dtrbf", [DTR, W], BF16)
    ostage = pb.sbuf("m_ostage", [128, KC * W], BF16)
    sptmp = pb.sbuf("m_sptmp", [128, W], F32)

    ps_t = pb.psum("m_pst", [128, 1024], BF16)
    ps_a = pb.psum("m_psa", [128, 512], F32)
    ps_b = pb.psum("m_psb", [128, 512], F32)
    ps_c = pb.psum("m_psc", [128, 512], F32)

    dr_xm_in = K.dram_tmp("m_xm_in", [KC * 128, TL], BF16)
    dr_xm_all = K.dram_tmp("m_xm_all", [2, KC * 128, TL], BF16)
    dr_dbc_in = K.dram_tmp("m_dbc_in", [DTR + 2 * DS, W], F32)
    dr_dbc_out = K.dram_tmp("m_dbc_out", [DTR + 2 * DS, W], F32)
    dr_xp = K.dram_tmp("m_xp_d", [TL, D], F32)
    dr_mo_in = K.dram_tmp("m_mo_in", [T, D], BF16)
    dr_mo_my = K.dram_tmp("m_mo_my", [TL, D], BF16)

    s_ld = pb.sem("m_ld")
    s_v = pb.sem("m_v")
    s_s = pb.sem("m_s")
    s_t = pb.sem("m_t")
    s_g = pb.sem("m_g")
    s_cc = pb.sem("m_cc")

    pb.op("vector", "tensor_copy", identf[:, :], K.ident[:, :])
    pb.drain("vector")
    emit_weight_load(K, K.inputs["m_Winu"], winu, D, DIL)
    emit_weight_load(K, K.inputs["m_Winz"], winz, D, DIL)
    emit_weight_load(K, K.inputs["m_Wout"], wout, DIL, D)
    emit_weight_load(K, K.inputs["m_Wx"], wx, DIL, DTR + 2 * DS)
    gld = pb.sem("gld")
    pb.dma("gpsimd", wdt[:, :], K.inputs["m_Wdt"], gld)
    lv = None
    for nm, dst, wid in (("m_convw", convw, 4), ("m_convb", convb, 1),
                         ("m_bdt", bdt, 1), ("m_Alog", Ask, DS),
                         ("m_Dskip", dsk, 1)):
        for r in range(MC):
            lv = pb.dma("sync", dst[:, r * wid:(r + 1) * wid],
                        K.inputs[nm][r * 128:(r + 1) * 128, :], s_ld)
    pb.wait_dma("scalar", s_ld, lv)
    pb.wait_dma("vector", s_ld, lv)
    pb.op("scalar", "activation", Ask[:, :], Ask[:, :], AF.Exp, inc=(s_s, 1))
    pb.wait("vector", s_s)
    pb.op("vector", "tensor_scalar", Ask[:, :], Ask[:, :], -1.0, None,
          ALU.mult, inc=(s_v, 1))
    pb.op("vector", "memset", hstate[:, :], 0.0)
    pb.op("vector", "memset", tailb[:, :], 0.0)
    pb.drain("vector")
    pb.push_scope()
    grow = pb.sbuf("m_gr", [128, D], F32)
    brow = pb.sbuf("m_br", [128, D], F32)
    xfm = pb.sbuf("m_xfm", [128, KC * 128], BF16)
    pb.wait("gpsimd", s_v)
    lvg = pb.dma("sync", grow[0:1, :], K.inputs["m_lng"], s_ld)
    lvb = pb.dma("sync", brow[0:1, :], K.inputs["m_lnb"], s_ld)
    pb.wait_dma("gpsimd", s_ld, lvb)
    pb.op("gpsimd", "partition_broadcast", grow[:, :], grow[0:1, :],
          channels=128)
    pb.drain("gpsimd")
    pb.op("gpsimd", "partition_broadcast", brow[:, :], brow[0:1, :],
          channels=128, inc=(s_g, 1))
    pb.wait("vector", s_g)

    for t in range(NT):
        pb.wait_dma("sync", s_ld)
        pb.wait("sync", s_v)
        lv = pb.dma("sync", sq[:, :], Xd[t * 128:(t + 1) * 128, :], s_ld)
        pb.wait_dma("vector", s_ld, lv)
        ln_stats(pb, sq[:, :], stats, mean_c, rstd_c, D, eps, xp)
        pb.op("vector", "tensor_scalar", xp[:, :], sq[:, :], mean_c[:, 0:1],
              rstd_c[:, 0:1], ALU.subtract, ALU.mult, inc=(s_v, 1))
        pb.wait("sync", s_v)
        pb.dma("sync", dr_xp[t * 128:(t + 1) * 128, :], xp[:, :], s_ld)
        pb.drain("vector")
        ln_stats(pb, xp[:, :], stats, mean_c, rstd_c, D, eps_m, sq)
        pb.op("vector", "tensor_scalar", sq[:, :], xp[:, :], mean_c[:, 0:1],
              rstd_c[:, 0:1], ALU.subtract, ALU.mult)
        pb.drain("vector")
        pb.op("vector", "tensor_tensor", sq[:, :], sq[:, :], grow[:, :],
              ALU.mult)
        pb.drain("vector")
        pb.op("vector", "tensor_add", xm_tok[:, :], sq[:, :], brow[:, :],
              inc=(s_v, 1))
        pb.wait("tensor", s_v)
        for c in range(KC):
            slot = c % 4
            dps = ps_t[:, slot * 128:slot * 128 + 128]
            tile_transpose(K, xm_tok[:, c * 128:(c + 1) * 128], dps,
                           inc=(s_t, 1))
            pb.wait("vector", s_t)
            pb.op("vector", "tensor_copy", xfm[:, c * 128:(c + 1) * 128],
                  dps, inc=(s_v, 1))
            pb.wait("tensor", s_v)
        pb.wait("sync", s_v)
        for c in range(KC):
            pb.dma("sync",
                   dr_xm_in[c * 128:(c + 1) * 128, t * 128:(t + 1) * 128],
                   xfm[:, c * 128:(c + 1) * 128], s_ld)
    pb.raw("gpsimd", lambda g: None)
    pb.wait_dma("gpsimd", s_ld)
    pb.op("gpsimd", "collective_compute", "AllGather", ALU.bypass,
          replica_groups=cfg["groups"], ins=[dr_xm_in], outs=[dr_xm_all],
          inc=(s_cc, 1))
    pb.barrier()
    pb.pop_scope()

    pb.wait("sync", s_cc)
    pb.wait_dma("tensor", pb.sem("gld"))
    for tc in range(TC):
        slot = (tc * W) // TL
        col0 = (tc * W) % TL
        pb.wait("sync", s_t)
        pb.wait("sync", s_v)
        lv = None
        for c in range(KC):
            lv = pb.dma("sync", xmg[:, c * W:(c + 1) * W],
                        dr_xm_all[slot, c * 128:(c + 1) * 128,
                                  col0:col0 + W], s_ld)
        pb.wait_dma("tensor", s_ld, lv)
        for which, wsb, dst in (("u", winu, None), ("z", winz, None)):
            for mc in range(MC):
                for half in range(W // 512):
                    ps = ps_a if (mc + half) % 2 == 0 else ps_b
                    cols = slice(half * 512, (half + 1) * 512)
                    for c in range(KC):
                        stat = wsb[:, c * DIL + mc * 128:
                                   c * DIL + (mc + 1) * 128]
                        pb.op("tensor", "ldweights", stat)
                        pb.op("tensor", "matmul", ps[:, 0:512], stat,
                              xmg[:, c * W + half * 512:
                                  c * W + (half + 1) * 512],
                              start=(c == 0), stop=(c == KC - 1),
                              inc=(s_t, 1) if c == KC - 1 else None)
                    if which == "u":
                        pb.wait("vector", s_t)
                        pb.op("vector", "tensor_copy",
                              ubuf[:, mc * (W + 3) + 3 + half * 512:
                                   mc * (W + 3) + 3 + (half + 1) * 512],
                              ps[:, 0:512], inc=(s_v, 1))
                        pb.wait("tensor", s_v)
                    else:
                        pb.wait("scalar", s_t)
                        pb.op("scalar", "activation",
                              zb[:, mc * W + half * 512:
                                 mc * W + (half + 1) * 512],
                              ps[:, 0:512], AF.Silu, inc=(s_s, 1))
                        pb.wait("tensor", s_s)
        pb.wait("vector", s_s)
        for mc in range(MC):
            ub = ubuf[:, mc * (W + 3):mc * (W + 3) + W + 3]
            pb.op("vector", "tensor_copy", ub[:, 0:3],
                  tailb[:, mc * 3:(mc + 1) * 3])
        pb.drain("vector")
        for mc in range(MC):
            ub = ubuf[:, mc * (W + 3):mc * (W + 3) + W + 3]
            pb.op("vector", "tensor_scalar", sptmp[:, 0:W], ub[:, 0:W],
                  convw[:, mc * 4 + 0:mc * 4 + 1], None, ALU.mult)
            pb.drain("vector")
            for j in range(1, 4):
                pb.op("vector", "scalar_tensor_tensor", sptmp[:, 0:W],
                      ub[:, j:j + W], convw[:, mc * 4 + j:mc * 4 + j + 1],
                      sptmp[:, 0:W], ALU.mult, ALU.add)
                pb.drain("vector")
            pb.op("vector", "tensor_scalar", sptmp[:, 0:W], sptmp[:, 0:W],
                  convb[:, mc:mc + 1], None, ALU.add, inc=(s_v, 1))
            pb.wait("scalar", s_v)
            pb.op("scalar", "activation", ucb[:, mc * W:(mc + 1) * W],
                  sptmp[:, 0:W], AF.Silu, inc=(s_s, 1))
            pb.wait("vector", s_s)
            pb.op("vector", "tensor_copy", tailb[:, mc * 3:(mc + 1) * 3],
                  ub[:, W:W + 3])
            pb.drain("vector")
        pb.op("vector", "nop", inc=(s_v, 1))
        pb.wait("tensor", s_v)
        NB = DTR + 2 * DS
        for half in range(W // 512):
            for mc in range(MC):
                stat = wx[:, mc * NB:(mc + 1) * NB]
                pb.op("tensor", "ldweights", stat)
                pb.op("tensor", "matmul", ps_c[0:NB, half * 512 // 1:512]
                      if False else ps_c[0:NB, 0:512], stat,
                      ucb[:, mc * W + half * 512:mc * W + (half + 1) * 512],
                      start=(mc == 0), stop=(mc == MC - 1),
                      inc=(s_t, 1) if mc == MC - 1 else None)
            pb.wait("vector", s_t)
            pb.op("vector", "tensor_copy",
                  dbcp[:, half * 512:(half + 1) * 512], ps_c[0:NB, 0:512],
                  inc=(s_v, 1))
            pb.wait("tensor", s_v)
        pb.wait("sync", s_v)
        pb.dma("sync", dr_dbc_in[:, :], dbcp[:, :], s_ld)
        pb.raw("gpsimd", lambda g: None)
        pb.wait_dma("gpsimd", s_ld)
        pb.op("gpsimd", "collective_compute", "AllReduce", ALU.add,
              replica_groups=cfg["groups"], ins=[dr_dbc_in],
              outs=[dr_dbc_out], inc=(s_cc, 1))
        pb.wait("sync", s_cc)
        lv = pb.dma("sync", dbcp[:, :], dr_dbc_out[:, :], s_ld)
        pb.wait_dma("vector", s_ld, lv)
        pb.wait_dma("tensor", s_ld, lv)
        pb.wait_dma("gpsimd", s_ld, lv)
        pb.op("vector", "tensor_copy", dtr_bf[:, 0:W], dbcp[0:DTR, 0:W],
              inc=(s_v, 1))
        pb.wait("tensor", s_v)
        for mc in range(MC):
            stat = wdt[:, mc * 128:(mc + 1) * 128]
            pb.op("tensor", "ldweights", stat)
            for half in range(W // 512):
                pb.op("tensor", "matmul", ps_a[:, 0:512] if mc % 2 == 0
                      else ps_b[:, 0:512], stat,
                      dtr_bf[:, half * 512:(half + 1) * 512],
                      start=True, stop=True, inc=(s_t, 1))
            ps = ps_a if mc % 2 == 0 else ps_b
            pb.wait("vector", s_t)
            pb.op("vector", "tensor_scalar", sptmp[:, 0:W], ps[:, 0:W],
                  bdt[:, mc:mc + 1], None, ALU.add, inc=(s_v, 1))
            pb.wait("tensor", s_v)
            pb.wait("scalar", s_v)
            softplus(pb, sptmp[:, 0:W], dtb[:, mc * W:(mc + 1) * W],
                     inc=(s_s, 1))
            pb.wait("vector", s_s)
            softplus_finish(pb, dtb[:, mc * W:(mc + 1) * W], sptmp[:, 0:W],
                            dtb[:, mc * W:(mc + 1) * W])
            pb.drain("vector")
            pb.op("vector", "tensor_tensor", wb[:, mc * W:(mc + 1) * W],
                  dtb[:, mc * W:(mc + 1) * W], ucb[:, mc * W:(mc + 1) * W],
                  ALU.mult)
            pb.drain("vector")
        pb.op("vector", "memset", yacc[:, :], 0.0, inc=(s_v, 1))
        for s in range(DS):
            pb.wait("vector", s_g)
            pb.wait("tensor", s_g)
            pb.op("tensor", "matmul", ps_c[0:1, 0:W],
                  identf[0:NB, DTR + s:DTR + s + 1], dbcp[0:NB, 0:W],
                  start=True, stop=True, inc=(s_t, 1))
            pb.wait("vector", s_t)
            pb.op("vector", "tensor_copy", brow0[:, 0:W], ps_c[0:1, 0:W],
                  inc=(s_v, 1))
            pb.wait("tensor", s_v)
            pb.op("tensor", "matmul", ps_c[0:1, 0:W],
                  identf[0:NB, DTR + DS + s:DTR + DS + s + 1],
                  dbcp[0:NB, 0:W], start=True, stop=True, inc=(s_t, 1))
            pb.wait("vector", s_t)
            pb.op("vector", "tensor_copy", crow0[:, 0:W], ps_c[0:1, 0:W],
                  inc=(s_v, 1))
            pb.wait("tensor", s_v)
            pb.wait("gpsimd", s_v)
            pb.op("gpsimd", "partition_broadcast", bbc[:, 0:W],
                  brow0[:, 0:W], channels=128)
            pb.drain("gpsimd")
            pb.op("gpsimd", "partition_broadcast", cbc[:, 0:W],
                  crow0[:, 0:W], channels=128, inc=(s_g, 1))
            pb.wait("vector", s_g)
            pb.wait("scalar", s_g)
            for mc in range(MC):
                par = mc % 2
                aa = asc[:, par * W:(par + 1) * W]
                pb.wait("scalar", s_v)
                pb.op("scalar", "activation", aa,
                      dtb[:, mc * W:(mc + 1) * W], AF.Exp,
                      scale=Ask[:, mc * DS + s:mc * DS + s + 1],
                      inc=(s_s, 1))
                pb.wait("vector", s_s)
                pb.op("vector", "tensor_tensor", bsc[:, 0:W],
                      wb[:, mc * W:(mc + 1) * W], bbc[:, 0:W], ALU.mult)
                pb.drain("vector")
                pb.op("vector", "tensor_tensor_scan", hsc[:, 0:W], aa,
                      bsc[:, 0:W],
                      hstate[:, mc * DS + s:mc * DS + s + 1],
                      ALU.mult, ALU.add, inc=(s_v, 1))
                pb.drain("vector")
                pb.op("vector", "tensor_copy",
                      hstate[:, mc * DS + s:mc * DS + s + 1],
                      hsc[:, W - 1:W])
                pb.drain("vector")
                pb.op("vector", "tensor_tensor", tsc[:, 0:W], hsc[:, 0:W],
                      cbc[:, 0:W], ALU.mult)
                pb.drain("vector")
                pb.op("vector", "tensor_add", yacc[:, mc * W:(mc + 1) * W],
                      yacc[:, mc * W:(mc + 1) * W], tsc[:, 0:W])
                pb.drain("vector")
            pb.op("vector", "nop", inc=(s_v, 1))
        for mc in range(MC):
            pb.op("vector", "scalar_tensor_tensor",
                  yacc[:, mc * W:(mc + 1) * W], ucb[:, mc * W:(mc + 1) * W],
                  dsk[:, mc:mc + 1], yacc[:, mc * W:(mc + 1) * W],
                  ALU.mult, ALU.add)
            pb.drain("vector")
            pb.op("vector", "tensor_tensor", ydone[:, mc * W:(mc + 1) * W],
                  yacc[:, mc * W:(mc + 1) * W], zb[:, mc * W:(mc + 1) * W],
                  ALU.mult)
            pb.drain("vector")
        pb.op("vector", "nop", inc=(s_v, 1))
        pb.wait("tensor", s_v)
        pb.wait_dma("vector", s_ld)
        DH = D // 2
        for j in range(W // 128):
            for mc in range(MC):
                stat = ydone[:, mc * W + j * 128:mc * W + j * 128 + 128]
                pb.op("tensor", "ldweights", stat)
                pb.op("tensor", "matmul", ps_a[:, 0:DH], stat,
                      wout[:, mc * D:mc * D + DH], start=(mc == 0),
                      stop=(mc == MC - 1),
                      inc=(s_t, 1) if mc == MC - 1 else None)
                pb.op("tensor", "matmul", ps_b[:, 0:DH], stat,
                      wout[:, mc * D + DH:(mc + 1) * D], start=(mc == 0),
                      stop=(mc == MC - 1),
                      inc=(s_t, 1) if mc == MC - 1 else None)
            pb.wait("vector", s_t)
            pb.op("vector", "tensor_copy", ostage[:, j * D:j * D + DH],
                  ps_a[:, 0:DH], inc=(s_v, 1))
            pb.op("vector", "tensor_copy", ostage[:, j * D + DH:(j + 1) * D],
                  ps_b[:, 0:DH], inc=(s_v, 1))
            pb.wait("tensor", s_v)
        pb.wait("sync", s_v)
        for j in range(W // 128):
            pb.dma("sync",
                   dr_mo_in[tc * W + j * 128:tc * W + (j + 1) * 128, :],
                   ostage[:, j * D:(j + 1) * D], s_ld)
    obf = pb.sbuf("m_obf", [128, D], BF16)
    otok = pb.sbuf("m_otok", [128, D], F32)
    pb.raw("gpsimd", lambda g: None)
    pb.wait_dma("gpsimd", s_ld)
    pb.op("gpsimd", "collective_compute", "ReduceScatter", ALU.add,
          replica_groups=cfg["groups"], ins=[dr_mo_in], outs=[dr_mo_my],
          inc=(s_cc, 1))
    pb.wait("sync", s_cc)
    for t in range(NT):
        pb.wait("sync", s_v)
        pb.wait_dma("sync", s_ld)
        lvo = pb.dma("sync", obf[:, :], dr_mo_my[t * 128:(t + 1) * 128, :],
                     s_ld)
        lvx = pb.dma("sync", xp[:, :], dr_xp[t * 128:(t + 1) * 128, :], s_ld)
        pb.wait_dma("vector", s_ld, lvx)
        pb.op("vector", "tensor_copy", otok[:, :], obf[:, :], inc=(s_v, 1))
        pb.drain("vector")
        pb.op("vector", "tensor_add", otok[:, :], otok[:, :], xp[:, :])
        pb.drain("vector")
        ln_stats(pb, otok[:, :], stats, mean_c, rstd_c, D, eps, xm_tok)
        pb.op("vector", "tensor_scalar", otok[:, :], otok[:, :],
              mean_c[:, 0:1], rstd_c[:, 0:1], ALU.subtract, ALU.mult)
        pb.drain("vector")
        lvx = pb.dma("sync", sq[:, :], Xd[t * 128:(t + 1) * 128, :], s_ld)
        pb.wait_dma("vector", s_ld, lvx)
        pb.op("vector", "tensor_add", sq[:, :], sq[:, :], otok[:, :],
              inc=(s_v, 1))
        pb.wait("sync", s_v)
        pb.dma("sync", Xd[t * 128:(t + 1) * 128, :], sq[:, :], s_ld)
    pb.wait_dma("sync", s_ld)
    pb.barrier()



def rmsnorm_tile(pb, K, x_ap, out_ap, wrow_bcast, stats, sq_scr, D, s_v, lns,
                 eps=1e-6, inc=None):
    s_sq = stats[:, 1:2]
    rst = stats[:, 3:4]
    pb.op("vector", "scalar_tensor_tensor", sq_scr[:, :], x_ap, 1.0, x_ap,
          ALU.bypass, ALU.mult, accum_out=s_sq)
    pb.drain("vector")
    pb.op("vector", "tensor_scalar", rst, s_sq, 1.0 / D, eps, ALU.mult,
          ALU.add)
    pb.drain("vector")
    pb.op("vector", "reciprocal", rst, rst, inc=(s_v, 1))
    pb.wait("scalar", s_v)
    pb.op("scalar", "activation", rst, rst, AF.Sqrt, inc=(lns, 1))
    pb.wait("vector", lns)
    if wrow_bcast is not None:
        pb.op("vector", "tensor_scalar", sq_scr[:, :], x_ap, rst, None,
              ALU.mult)
        pb.drain("vector")
        pb.op("vector", "tensor_tensor", out_ap, sq_scr[:, :], wrow_bcast,
              ALU.mult, inc=inc)
        pb.drain("vector")
    else:
        pb.op("vector", "tensor_scalar", out_ap, x_ap, rst, None, ALU.mult,
              inc=inc)
        pb.drain("vector")


def emit_l1_ffn(K):
    cfg, pb = K.cfg, K.pb
    T, TL, D, HL, HD, G = (cfg["T"], cfg["TL"], cfg["D"], cfg["HL"],
                           cfg["HD"], cfg["G"])
    FFN = cfg.get("FFN", 4 * D)
    S2 = T // 4
    NT = TL // 128
    KC = D // 128
    MY = HL * HD
    MYC = MY // 128
    NG = T // G
    GT = G // 128
    VW = HL * (HD + 1)
    HALF = HD // 2
    eps = 1e-5
    Xd = K.x_dram

    K.dram_in("d_w1", [D, 3 * MY], F32)
    K.dram_in("d_wo1", [MY, D], F32)
    K.dram_in("d_w2", [D, 3 * MY], F32)
    K.dram_in("d_wo2", [MY, D], F32)
    K.dram_in("d_wg1", [FFN // 128, 128, D], F32)
    K.dram_in("d_wu1", [FFN // 128, 128, D], F32)
    K.dram_in("d_wd1", [D // 128, 128, FFN], F32)
    K.dram_in("d_wg2", [FFN // 128, 128, D], F32)
    K.dram_in("d_wu2", [FFN // 128, 128, D], F32)
    K.dram_in("d_wd2", [D // 128, 128, FFN], F32)
    K.dram_in("d_ln11", [1, D], F32)
    K.dram_in("d_ln21", [1, D], F32)
    K.dram_in("d_ln12c", [128, KC], F32)
    K.dram_in("d_ln22c", [128, KC], F32)
    K.dram_in("d_pnw", [1, D], F32)
    K.dram_in("d_pw", [D, 2 * D], F32)
    K.dram_in("d_tpos", [1, TL], F32)

    dr_hs = K.dram_tmp("d_hs", [TL, D], BF16)
    K.d_hs = dr_hs
    pb.push_scope()
    sq = pb.sbuf("d_sq", [128, D], F32)
    sq2 = pb.sbuf("d_sq2", [128, D], F32)
    xn_tok = pb.sbuf("d_xntok", [128, D], BF16)
    xfm = pb.sbuf("d_xfm", [128, KC * 128], BF16)
    stats = pb.sbuf("d_stats", [128, 8], F32)
    mean_c = pb.sbuf("d_mean", [128, 1], F32)
    rstd_c = pb.sbuf("d_rstd", [128, 1], F32)
    lnrow = pb.sbuf("d_lnrow", [128, D], F32)
    ln2row = pb.sbuf("d_ln2row", [128, D], F32)
    w1 = pb.sbuf("d_w1sb", [128, KC * 3 * MY], BF16)
    wo = pb.sbuf("d_wosb", [128, MYC * D], BF16)
    kst = pb.sbuf("d_kst", [128, MYC * T], BF16)
    vtok = pb.sbuf("d_vtok", [128, (T // 128) * VW], BF16)
    cpos = pb.sbuf("d_cpos", [128, T], BF16)
    spos = pb.sbuf("d_spos", [128, T], BF16)
    invf = pb.sbuf("d_invf", [HALF, 1], F32)
    qg = pb.sbuf("d_qg", [128, MYC * G], BF16)
    qrope = pb.sbuf("d_qrope", [128, MYC * G], BF16)
    qpad = pb.sbuf("d_qpad", [128, 2 * G], BF16)
    ptile = pb.sbuf("d_p", [128, G], BF16)
    tri = pb.sbuf("d_tri", [128, GT * G], BF16)
    ygrp = pb.sbuf("d_ygrp", [128, MYC * G], BF16)
    denb = pb.sbuf("d_denb", [HD, G], F32)
    den1 = pb.sbuf("d_den1", [1, G], F32)
    ostage = pb.sbuf("d_ostage", [128, KC * G], BF16)
    obf = pb.sbuf("d_obf", [128, D], BF16)
    otok = pb.sbuf("d_otok", [128, D], F32)
    xn_grp = pb.sbuf("d_xngrp", [128, KC * G], BF16)

    ps_t = pb.psum("d_pst", [128, 1024], BF16)
    ps_a = pb.psum("d_psa", [128, 512], F32)
    ps_b = pb.psum("d_psb", [128, 512], F32)
    ps_s0 = pb.psum("d_pss0", [128, 512], F32)
    ps_s1 = pb.psum("d_pss1", [128, 512], F32)
    ps_y = pb.psum("d_psy", [128, 512], F32)

    dr_h_in = K.dram_tmp("d_h_in", [KC * 128, TL], BF16)
    dr_h_all = K.dram_tmp("d_h_all", [2, KC * 128, TL], BF16)
    dr_oa_in = K.dram_tmp("d_oa_in", [T, D], BF16)
    dr_oa_my = K.dram_tmp("d_oa_my", [TL, D], BF16)
    dr_proc = K.dram_tmp("d_proc", [TL, D], F32)
    K.d_proc = dr_proc

    s_ld = pb.sem("d_ld")
    s_v = pb.sem("d_v")
    s_s = pb.sem("d_s")
    s_t = pb.sem("d_t")
    s_g = pb.sem("d_g")
    s_cc = pb.sem("d_cc")
    lns = pb.sem("ln_s")

    pb.push_scope()
    ang = pb.sbuf("d_ang", [HALF, 1024], F32)
    pb.op("gpsimd", "iota", invf[:, 0:1], pattern=[[1, 1]], base=0,
          channel_multiplier=1, allow_small_or_imprecise_dtypes=True,
          inc=(s_g, 1))
    pb.wait("scalar", s_g)
    pb.op("scalar", "activation", invf[:, 0:1], invf[:, 0:1], AF.Exp,
          scale=-math.log(10000.0) / HALF, inc=(s_s, 1))
    pb.wait("vector", s_s)
    for t0 in range(0, T, 1024):
        pb.op("gpsimd", "iota", ang[:, :], pattern=[[1, 1024]], base=t0,
              channel_multiplier=0, allow_small_or_imprecise_dtypes=True,
              inc=(s_g, 1))
        pb.wait("vector", s_g)
        pb.op("vector", "tensor_scalar", ang[:, :], ang[:, :], invf[:, 0:1],
              None, ALU.mult)
        pb.drain("vector")
        emit_sincos(K, ang, cpos[0:HALF, t0:t0 + 1024],
                    spos[0:HALF, t0:t0 + 1024], HALF, 1024)
        pb.wait("gpsimd", pb.sem("rr_s"))
    pb.op("vector", "nop", inc=(s_v, 1))
    pb.wait("sync", s_v)
    for rr in (32, 64, 96):
        pb.dma("sync", cpos[rr:rr + HALF, :], cpos[0:HALF, :], s_ld)
        lv = pb.dma("sync", spos[rr:rr + HALF, :], spos[0:HALF, :], s_ld)
    pb.wait_dma("vector", s_ld, lv)
    pb.pop_scope()
    for j in range(GT):
        sl = tri[:, j * G:(j + 1) * G]
        pb.op("gpsimd", "memset", sl, 1.0)
        pb.drain("gpsimd")
        pb.op("gpsimd", "affine_select", sl, sl, pattern=[[1, G]],
              base=-128 * j, channel_multiplier=-1, compare_op=ALU.is_ge,
              fill=0.0)
        pb.drain("gpsimd")
    pb.op("gpsimd", "nop", inc=(s_g, 1))
    lv = pb.dma("sync", lnrow[0:1, :], K.inputs["d_ln11"], s_ld)
    pb.wait_dma("gpsimd", s_ld, lv)
    pb.op("gpsimd", "partition_broadcast", lnrow[:, :], lnrow[0:1, :],
          channels=128, inc=(s_g, 1))
    pb.wait("vector", s_g)
    emit_weight_load(K, K.inputs["d_w1"], w1, D, 3 * MY)
    gld, wv = emit_weight_load(K, K.inputs["d_wo1"], wo, MY, D)

    for t in range(NT):
        pb.wait_dma("sync", s_ld)
        pb.wait("sync", s_v)
        lv = pb.dma("sync", sq[:, :], Xd[t * 128:(t + 1) * 128, :], s_ld)
        pb.wait_dma("vector", s_ld, lv)
        ln_stats(pb, sq[:, :], stats, mean_c, rstd_c, D, eps, sq2)
        pb.op("vector", "tensor_scalar", sq[:, :], sq[:, :], mean_c[:, 0:1],
              rstd_c[:, 0:1], ALU.subtract, ALU.mult)
        pb.drain("vector")
        pb.op("vector", "tensor_copy", xn_tok[:, :], sq[:, :], inc=(s_v, 1))
        pb.wait("sync", s_v)
        pb.dma("sync", dr_proc[t * 128:(t + 1) * 128, :], sq[:, :], s_ld)
        lvh = pb.dma("sync", dr_hs[t * 128:(t + 1) * 128, :], xn_tok[:, :],
                     s_ld)
        pb.wait_dma("vector", s_ld, lvh)
        rmsnorm_tile(pb, K, sq[:, :], xn_tok[:, :], lnrow[:, :], stats, sq2,
                     D, s_v, lns, inc=(s_v, 1))
        pb.wait("tensor", s_v)
        for c in range(KC):
            slot = c % 4
            dps = ps_t[:, slot * 128:slot * 128 + 128]
            tile_transpose(K, xn_tok[:, c * 128:(c + 1) * 128], dps,
                           inc=(s_t, 1))
            pb.wait("vector", s_t)
            pb.op("vector", "tensor_copy", xfm[:, c * 128:(c + 1) * 128],
                  dps, inc=(s_v, 1))
            pb.wait("tensor", s_v)
        pb.wait("sync", s_v)
        for c in range(KC):
            pb.dma("sync",
                   dr_h_in[c * 128:(c + 1) * 128, t * 128:(t + 1) * 128],
                   xfm[:, c * 128:(c + 1) * 128], s_ld)
    pb.raw("gpsimd", lambda g: None)
    pb.wait_dma("gpsimd", s_ld)
    pb.op("gpsimd", "collective_compute", "AllGather", ALU.bypass,
          replica_groups=cfg["groups"], ins=[dr_h_in], outs=[dr_h_all],
          inc=(s_cc, 1))

    pb.op("vector", "memset", vtok[:, :], 0.0)
    pb.drain("vector")
    last = (T // 128 - 1, HL - 1)
    for tc in range(T // 128):
        for h in range(HL):
            base = tc * VW + h * (HD + 1) + HD
            pb.op("vector", "memset", vtok[:, base:base + 1], 1.0,
                  inc=(s_v, 1) if (tc, h) == last else None)
    pb.wait("sync", s_cc)
    pb.wait_dma("tensor", gld, wv)
    pb.wait("tensor", s_v)

    def rope_chunk(dst, src, gcol0, glen, sgn_q=True):
        for r in (0, 64):
            x1 = src[r:r + HALF, 0:glen]
            x2 = src[r + HALF:r + 2 * HALF, 0:glen]
            cc1 = cpos[r:r + HALF, gcol0:gcol0 + glen]
            cc2 = cpos[r + HALF:r + 2 * HALF, gcol0:gcol0 + glen]
            ss1 = spos[r:r + HALF, gcol0:gcol0 + glen]
            ss2 = spos[r + HALF:r + 2 * HALF, gcol0:gcol0 + glen]
            d1 = dst[r:r + HALF, 0:glen]
            d2 = dst[r + HALF:r + 2 * HALF, 0:glen]
            pb.op("vector", "tensor_tensor", d1, x1, cc1, ALU.mult)
            pb.op("vector", "tensor_tensor", d2, x2, cc2, ALU.mult)
            pb.drain("vector")
            pb.op("vector", "tensor_tensor", sq[r:r + HALF, 0:glen], x2,
                  ss2, ALU.mult)
            pb.op("vector", "tensor_tensor",
                  sq[r + HALF:r + 2 * HALF, 0:glen], x1, ss1, ALU.mult)
            pb.drain("vector")
            pb.op("vector", "tensor_sub", d1, d1, sq[r:r + HALF, 0:glen])
            pb.op("vector", "tensor_add", d2, d2,
                  sq[r + HALF:r + 2 * HALF, 0:glen])
            pb.drain("vector")

    for kg in range(NG):
        slot = (kg * G) // TL
        col0 = (kg * G) % TL
        pb.wait("sync", s_t)
        lv = None
        for c in range(KC):
            lv = pb.dma("sync", xn_grp[:, c * G:(c + 1) * G],
                        dr_h_all[slot, c * 128:(c + 1) * 128,
                                 col0:col0 + G], s_ld)
        pb.wait_dma("tensor", s_ld, lv)
        for which, wofs in (("k", MY), ("v", 2 * MY)):
            for mc in range(MYC):
                ps = ps_a if (mc % 2 == 0) else ps_b
                for c in range(KC):
                    stat = w1[:, c * 3 * MY + wofs + mc * 128:
                              c * 3 * MY + wofs + (mc + 1) * 128]
                    pb.op("tensor", "ldweights", stat)
                    pb.op("tensor", "matmul", ps[:, 0:G], stat,
                          xn_grp[:, c * G:(c + 1) * G], start=(c == 0),
                          stop=(c == KC - 1),
                          inc=(s_t, 1) if c == KC - 1 else None)
                if which == "k":
                    pb.wait("vector", s_t)
                    pb.op("vector", "tensor_copy", xn_tok[:, 0:G],
                          ps[:, 0:G], inc=(s_v, 1))
                    pb.wait("tensor", s_v)
                    pb.drain("vector")
                    rope_chunk(kst[:, mc * T + kg * G:mc * T + (kg + 1) * G],
                               xn_tok[:, 0:G], kg * G, G)
                    pb.op("vector", "nop", inc=(s_v, 1))
                else:
                    pb.wait("vector", s_t)
                    pb.op("vector", "tensor_copy", xn_tok[:, 0:G],
                          ps[:, 0:G], inc=(s_v, 1))
                    pb.wait("tensor", s_v)
                    for q in range(GT):
                        slot2 = q % 4
                        dps = ps_t[:, 512 + slot2 * 128:
                                   512 + slot2 * 128 + 128]
                        tile_transpose(K, xn_tok[:, q * 128:(q + 1) * 128],
                                       dps, inc=(s_t, 1))
                        pb.wait("vector", s_t)
                        tc = kg * GT + q
                        for hh in range(128 // HD):
                            h = mc * (128 // HD) + hh
                            pb.op("vector", "tensor_copy",
                                  vtok[:, tc * VW + h * (HD + 1):
                                       tc * VW + h * (HD + 1) + HD],
                                  dps[:, hh * HD:(hh + 1) * HD],
                                  inc=(s_v, 1))
                        pb.wait("tensor", s_v)
    pb.barrier()

    pb.op("vector", "memset", qpad[:, :], 0.0)
    pb.drain("vector")
    for g in range(NG):
        nkt = (g + 1) * GT
        slot = (g * G) // TL
        col0 = (g * G) % TL
        pb.wait("sync", s_t)
        lv = None
        for c in range(KC):
            lv = pb.dma("sync", xn_grp[:, c * G:(c + 1) * G],
                        dr_h_all[slot, c * 128:(c + 1) * 128,
                                 col0:col0 + G], s_ld)
        pb.wait_dma("tensor", s_ld, lv)
        for mc in range(MYC):
            ps = ps_a if (mc % 2 == 0) else ps_b
            for c in range(KC):
                stat = w1[:, c * 3 * MY + mc * 128:c * 3 * MY + (mc + 1) * 128]
                pb.op("tensor", "ldweights", stat)
                pb.op("tensor", "matmul", ps[:, 0:G], stat,
                      xn_grp[:, c * G:(c + 1) * G], start=(c == 0),
                      stop=(c == KC - 1),
                      inc=(s_t, 1) if c == KC - 1 else None)
            pb.wait("vector", s_t)
            pb.op("vector", "tensor_copy", qg[:, mc * G:(mc + 1) * G],
                  ps[:, 0:G], inc=(s_v, 1))
            pb.wait("tensor", s_v)
            pb.drain("vector")
            rope_chunk(qrope[:, mc * G:(mc + 1) * G], qg[:, mc * G:(mc + 1) * G],
                       g * G, G)
            pb.op("vector", "nop", inc=(s_v, 1))
        pb.wait("tensor", s_v)
        for h in range(HL):
            mc0, r0 = divmod(h * HD, 128)
            sl = r0 // 64
            pb.wait("vector", s_t)
            pb.op("vector", "tensor_copy",
                  qpad[r0:r0 + HD, sl * G:(sl + 1) * G],
                  qrope[r0:r0 + HD, mc0 * G:(mc0 + 1) * G], inc=(s_v, 1))
            pb.wait("tensor", s_v)
            for kt in range(nkt):
                ps_sc = ps_s0 if kt % 2 == 0 else ps_s1
                stat = kst[:, mc0 * T + kt * 128:mc0 * T + (kt + 1) * 128]
                pb.op("tensor", "ldweights", stat)
                pb.op("tensor", "matmul", ps_sc[:, 0:G], stat,
                      qpad[:, sl * G:(sl + 1) * G],
                      start=True, stop=True, inc=(s_t, 1))
                pb.wait("scalar", s_t)
                pb.op("scalar", "activation", ptile[:, :], ps_sc[:, 0:G],
                      AF.Exp, scale=1.0 / math.sqrt(HD), inc=(s_s, 1))
                diag = kt - g * GT
                pb.wait("vector", s_s)
                if diag >= 0:
                    pb.op("vector", "tensor_tensor", ptile[:, :],
                          ptile[:, :], tri[:, diag * G:(diag + 1) * G],
                          ALU.mult, inc=(s_v, 1))
                else:
                    pb.op("vector", "nop", inc=(s_v, 1))
                pb.wait("tensor", s_v)
                pb.op("tensor", "ldweights",
                      vtok[:, kt * VW + h * (HD + 1):
                           kt * VW + (h + 1) * (HD + 1)])
                pb.op("tensor", "matmul", ps_y[0:HD + 1, 0:G],
                      vtok[:, kt * VW + h * (HD + 1):
                           kt * VW + (h + 1) * (HD + 1)],
                      ptile[:, :], start=(kt == 0), stop=(kt == nkt - 1),
                      inc=(s_t, 1))
                pb.wait("scalar", s_t)
            pb.wait("vector", s_t)
            pb.op("vector", "reciprocal", den1[:, :], ps_y[HD:HD + 1, 0:G],
                  inc=(s_v, 1))
            pb.wait("gpsimd", s_v)
            pb.op("gpsimd", "partition_broadcast", denb[:, :], den1[:, :],
                  channels=HD, inc=(s_g, 1))
            pb.wait("vector", s_g)
            pb.op("vector", "tensor_tensor",
                  ygrp[r0:r0 + HD, mc0 * G:(mc0 + 1) * G], ps_y[0:HD, 0:G],
                  denb[:, :], ALU.mult, inc=(s_v, 1))
            pb.wait("tensor", s_v)
        pb.wait_dma("vector", s_ld)
        DH = D // 2
        for j in range(G // 128):
            for c in range(MYC):
                stat = ygrp[:, c * G + j * 128:c * G + j * 128 + 128]
                pb.op("tensor", "ldweights", stat)
                pb.op("tensor", "matmul", ps_a[:, 0:DH], stat,
                      wo[:, c * D:c * D + DH], start=(c == 0),
                      stop=(c == MYC - 1),
                      inc=(s_t, 1) if c == MYC - 1 else None)
                pb.op("tensor", "matmul", ps_b[:, 0:DH], stat,
                      wo[:, c * D + DH:(c + 1) * D], start=(c == 0),
                      stop=(c == MYC - 1),
                      inc=(s_t, 1) if c == MYC - 1 else None)
            pb.wait("vector", s_t)
            pb.op("vector", "tensor_copy", ostage[:, j * D:j * D + DH],
                  ps_a[:, 0:DH], inc=(s_v, 1))
            pb.op("vector", "tensor_copy", ostage[:, j * D + DH:(j + 1) * D],
                  ps_b[:, 0:DH], inc=(s_v, 1))
            pb.wait("tensor", s_v)
        pb.wait("sync", s_v)
        for j in range(G // 128):
            pb.dma("sync",
                   dr_oa_in[g * G + j * 128:g * G + (j + 1) * 128, :],
                   ostage[:, j * D:(j + 1) * D], s_ld)
    pb.raw("gpsimd", lambda g: None)
    pb.wait_dma("gpsimd", s_ld)
    pb.op("gpsimd", "collective_compute", "ReduceScatter", ALU.add,
          replica_groups=cfg["groups"], ins=[dr_oa_in], outs=[dr_oa_my],
          inc=(s_cc, 1))
    pb.wait("sync", s_cc)
    for t in range(NT):
        pb.wait("sync", s_v)
        pb.wait_dma("sync", s_ld)
        lvo = pb.dma("sync", obf[:, :], dr_oa_my[t * 128:(t + 1) * 128, :],
                     s_ld)
        lvx = pb.dma("sync", sq[:, :], dr_proc[t * 128:(t + 1) * 128, :],
                     s_ld)
        pb.wait_dma("vector", s_ld, lvx)
        pb.op("vector", "tensor_copy", otok[:, :], obf[:, :], inc=(s_v, 1))
        pb.drain("vector")
        pb.op("vector", "tensor_add", sq[:, :], sq[:, :], otok[:, :],
              inc=(s_v, 1))
        pb.wait("sync", s_v)
        pb.dma("sync", dr_proc[t * 128:(t + 1) * 128, :], sq[:, :], s_ld)
    pb.wait_dma("sync", s_ld)
    pb.barrier()
    pb.pop_scope()
    pb.push_scope()
    K.pb.label = "dec_ffn1"
    emit_dec_ffn(K, dr_proc, "d_ln21", "d_wg1", "d_wu1", "d_wd1")
    pb.pop_scope()
    pb.push_scope()
    K.pb.label = "router"
    emit_router_dec2(K)
    pb.pop_scope()


def emit_dec_ffn(K, dr_x, ln_name, wg_name, wu_name, wd_name,
                 pre="f1"):
    cfg, pb = K.cfg, K.pb
    TL, D = cfg["TL"], cfg["D"]
    FFN = cfg.get("FFN", 4 * D)
    NT = TL // 128
    KC = D // 128
    FC = FFN // 128
    G2 = 512
    NGL = TL // G2
    GT2 = G2 // 128

    sq = pb.sbuf(pre + "_sq", [128, D], F32)
    sq2 = pb.sbuf(pre + "_sq2", [128, D], F32)
    stats = pb.sbuf(pre + "_stats", [128, 8], F32)
    lnrow = pb.sbuf(pre + "_lnrow", [128, D], F32)
    h2tok = pb.sbuf(pre + "_h2", [128, D], BF16)
    hfm = pb.sbuf(pre + "_hfm", [128, KC * G2], BF16)
    act = pb.sbuf(pre + "_act", [128, FC * G2], BF16)
    wgc = pb.sbuf(pre + "_wgc", [128, KC * 128 * 2 * 2], BF16)
    wdc = pb.sbuf(pre + "_wdc", [128, 2 * FFN], BF16)
    wstg = pb.sbuf(pre + "_wstg", [128, KC * 128 * 2], F32)
    wstgd = pb.sbuf(pre + "_wstgd", [128, FFN], F32)
    gtile = pb.sbuf(pre + "_gt", [128, G2], F32)
    otok = pb.sbuf(pre + "_otok", [128, D], F32)
    ofm = pb.sbuf(pre + "_ofm", [128, KC * G2], BF16)

    ps_t = pb.psum(pre + "_pst", [128, 1024], BF16)
    ps_a = pb.psum(pre + "_psa", [128, 512], F32)
    ps_b = pb.psum(pre + "_psb", [128, 512], F32)
    ps_c = pb.psum(pre + "_psc", [128, 512], F32)

    s_ld = pb.sem(pre + "_ld")
    s_v = pb.sem(pre + "_v")
    s_s = pb.sem(pre + "_s")
    s_t = pb.sem(pre + "_t")
    s_g = pb.sem(pre + "_g")
    gld = pb.sem("gld")
    lns = pb.sem("ln_s")

    lv = pb.dma("sync", lnrow[0:1, :], K.inputs[ln_name], s_ld)
    pb.wait_dma("gpsimd", s_ld, lv)
    pb.op("gpsimd", "partition_broadcast", lnrow[:, :], lnrow[0:1, :],
          channels=128, inc=(s_g, 1))
    pb.wait("vector", s_g)

    s_wl = pb.sem(pre + "_wl")
    s_ca = pb.sem(pre + "_ca")
    CH = KC * 128 * 2
    guards = []
    ldtok = {}
    caval_prev = {}
    dtok = {}
    dcav = {}
    dguards = []

    def issue_gu(gq):
        gi2, fc = divmod(gq, FC)
        if gi2 >= NGL:
            return
        if gq >= 1:
            pb.wait("sync", s_ca, caval_prev[gq - 1])
        pb.dma("sync", wstg[:, 0:KC * 128], K.inputs[wg_name][fc], s_wl)
        ldtok[gq] = pb.dma("sync", wstg[:, KC * 128:2 * KC * 128],
                           K.inputs[wu_name][fc], s_wl)

    def issue_wd(gq):
        gi2, oc = divmod(gq, KC)
        if gi2 >= NGL:
            return
        if gq >= 1:
            pb.wait("sync", s_ca, dcav[gq - 1])
        dtok[gq] = pb.dma("sync", wstgd[:, 0:FFN],
                          K.inputs[wd_name][oc], s_wl)

    issue_gu(0)
    for gi in range(NGL):
        for q in range(GT2):
            t = gi * GT2 + q
            pb.wait_dma("sync", s_ld)
            pb.wait("sync", s_v)
            lv = pb.dma("sync", sq[:, :], dr_x[t * 128:(t + 1) * 128, :],
                        s_ld)
            pb.wait_dma("vector", s_ld, lv)
            rmsnorm_tile(pb, K, sq[:, :], h2tok[:, :], lnrow[:, :], stats,
                         sq2, D, s_v, lns, inc=(s_v, 1))
            pb.wait("tensor", s_v)
            for c in range(KC):
                slot = c % 4
                dps = ps_t[:, slot * 128:slot * 128 + 128]
                tile_transpose(K, h2tok[:, c * 128:(c + 1) * 128], dps,
                               inc=(s_t, 1))
                pb.wait("vector", s_t)
                pb.op("vector", "tensor_copy",
                      hfm[:, c * G2 + q * 128:c * G2 + q * 128 + 128], dps,
                      inc=(s_v, 1))
                pb.wait("tensor", s_v)
        for fc in range(FC):
            gq = gi * FC + fc
            slot = gq % 2
            pb.wait_dma("scalar", s_wl, ldtok[gq])
            if gq >= 2:
                gsem, gval = guards[gq - 2]
                pb.wait("scalar", gsem, gval)
            caval_prev[gq] = pb.op(
                "scalar", "activation", wgc[:, slot * CH:(slot + 1) * CH],
                wstg[:, 0:CH], AF.Copy, inc=(s_ca, 1))
            pb.wait("tensor", s_ca, caval_prev[gq])
            issue_gu(gq + 1)
            if fc == FC - 1:
                issue_wd(gi * KC)
            for half in range(GT2 * 128 // 512):
                ps = ps_a
                psu = ps_b
                for c in range(KC):
                    stat = wgc[:, slot * CH + c * 128:
                               slot * CH + (c + 1) * 128]
                    pb.op("tensor", "ldweights", stat)
                    pb.op("tensor", "matmul", ps[:, 0:512], stat,
                          hfm[:, c * G2 + half * 512:c * G2 + (half + 1) * 512],
                          start=(c == 0), stop=(c == KC - 1),
                          inc=(s_t, 1) if c == KC - 1 else None)
                for c in range(KC):
                    stat = wgc[:, slot * CH + KC * 128 + c * 128:
                               slot * CH + KC * 128 + (c + 1) * 128]
                    pb.op("tensor", "ldweights", stat)
                    pb.op("tensor", "matmul", psu[:, 0:512], stat,
                          hfm[:, c * G2 + half * 512:c * G2 + (half + 1) * 512],
                          start=(c == 0), stop=(c == KC - 1),
                          inc=(s_t, 1) if c == KC - 1 else None)
                pb.wait("scalar", s_t)
                pb.op("scalar", "activation", gtile[:, 0:512], ps[:, 0:512],
                      AF.Silu, inc=(s_s, 1))
                pb.wait("vector", s_s)
                pb.op("vector", "tensor_tensor",
                      act[:, fc * G2 + half * 512:fc * G2 + (half + 1) * 512],
                      gtile[:, 0:512], psu[:, 0:512], ALU.mult, inc=(s_v, 1))
                pb.wait("tensor", s_v)
            guards.append((s_t, s_t.n))
        for oc in range(KC):
            gq = gi * KC + oc
            slot = gq % 2
            pb.wait_dma("scalar", s_wl, dtok[gq])
            if gq >= 2:
                gsem, gval = dguards[gq - 2]
                pb.wait("scalar", gsem, gval)
            dcav[gq] = pb.op(
                "scalar", "activation", wdc[:, slot * FFN:(slot + 1) * FFN],
                wstgd[:, 0:FFN], AF.Copy, inc=(s_ca, 1))
            pb.wait("tensor", s_ca, dcav[gq])
            if oc < KC - 1:
                issue_wd(gq + 1)
            for half in range(GT2 * 128 // 512):
                ps = ps_c
                for fc in range(FC):
                    stat = wdc[:, slot * FFN + fc * 128:
                               slot * FFN + (fc + 1) * 128]
                    pb.op("tensor", "ldweights", stat)
                    pb.op("tensor", "matmul", ps[:, 0:512], stat,
                          act[:, fc * G2 + half * 512:fc * G2 + (half + 1) * 512],
                          start=(fc == 0), stop=(fc == FC - 1),
                          inc=(s_t, 1) if fc == FC - 1 else None)
                pb.wait("vector", s_t)
                pb.op("vector", "tensor_copy",
                      ofm[:, oc * G2 + half * 512:oc * G2 + (half + 1) * 512],
                      ps[:, 0:512], inc=(s_v, 1))
                pb.wait("tensor", s_v)
            dguards.append((s_t, s_t.n))
        for q in range(GT2):
            t = gi * GT2 + q
            pb.wait("tensor", s_v)
            for c in range(KC):
                slot = c % 4
                dps = ps_t[:, 512 + slot * 128:512 + slot * 128 + 128]
                tile_transpose(K, ofm[:, c * G2 + q * 128:c * G2 + q * 128 + 128],
                               dps, inc=(s_t, 1))
                pb.wait("vector", s_t)
                pb.op("vector", "tensor_copy",
                      otok[:, c * 128:(c + 1) * 128], dps, inc=(s_v, 1))
                pb.wait("tensor", s_v)
            pb.wait_dma("sync", s_ld)
            pb.wait("sync", s_v)
            lv = pb.dma("sync", sq[:, :], dr_x[t * 128:(t + 1) * 128, :],
                        s_ld)
            pb.wait_dma("vector", s_ld, lv)
            pb.op("vector", "tensor_add", sq[:, :], sq[:, :], otok[:, :],
                  inc=(s_v, 1))
            pb.wait("sync", s_v)
            pb.dma("sync", dr_x[t * 128:(t + 1) * 128, :], sq[:, :], s_ld)
    pb.wait_dma("sync", s_ld)
    pb.barrier()


def psum_sel(K, fc):
    return K._gather_ps[fc % 4]


def emit_router_dec2(K):
    cfg, pb = K.cfg, K.pb
    T, TL, D, HL, HD, G = (cfg["T"], cfg["TL"], cfg["D"], cfg["HL"],
                           cfg["HD"], cfg["G"])
    FFN = cfg.get("FFN", 4 * D)
    S2 = T // 4
    SC = S2 // 128
    W2H = min(512, S2)
    NT = TL // 128
    KC = D // 128
    MY = HL * HD
    MYC = MY // 128
    HALF = HD // 2
    VW = HL * (HD + 1)
    G2 = min(512, S2)
    NG2 = S2 // G2
    GT2 = G2 // 128
    FC = FFN // 128
    eps = 1e-5
    Xd = K.x_dram
    dr_proc = K.d_proc
    dr_hs = K.d_hs

    hs_t = pb.sbuf("r_hst", [128, D], BF16)
    sq = pb.sbuf("r_sq", [128, D], F32)
    sq2 = pb.sbuf("r_sq2", [128, D], F32)
    stats = pb.sbuf("r_stats", [128, 8], F32)
    mean_c = pb.sbuf("r_mean", [128, 1], F32)
    rstd_c = pb.sbuf("r_rstd", [128, 1], F32)
    onescol = pb.sbuf("r_ones", [128, 1], F32)
    packC = pb.sbuf("r_packC", [128, TL], F32)
    myslot = packC[0:1, 0:TL]
    sc1 = pb.sbuf("r_sc1", [1, 8], F32)
    gth = pb.sbuf("r_gth", [1, 4], F32)
    slotcols = pb.sbuf("r_scol", [128, NT], F32)
    gcols = pb.sbuf("r_gcol", [128, NT], BF16)
    basecols = pb.sbuf("r_bcol", [128, NT], BF16)
    iotacol = pb.sbuf("r_iota", [128, 1], BF16)
    icf = pb.sbuf("r_icf", [128, 1], F32)
    gcf = pb.sbuf("r_gcf", [128, NT], F32)
    tpf = pb.sbuf("r_tpf", [128, NT], F32)
    selfm = pb.sbuf("r_selfm", [128, KC * S2], BF16)
    x2fm = pb.sbuf("r_x2fm", [128, KC * S2], BF16)
    h2fm = pb.sbuf("r_h2fm", [128, KC * S2], BF16)
    scb = pb.sbuf("r_scb", [128, S2], F32)
    scores_row = packC[64:65, 0:S2]
    pos_row = packC[96:97, 0:S2]
    otok = pb.sbuf("r_otok", [128, D], F32)
    rbf = pb.sbuf("r_rbf", [128, D], BF16)
    f32a = pb.sbuf("r_f32a", [128, 512], F32)
    f32b = pb.sbuf("r_f32b", [128, 512], F32)

    ps_t = pb.psum("r_pst", [128, 1024], BF16)
    ps_a = pb.psum("r_psa", [128, 512], F32)
    ps_b = pb.psum("r_psb", [128, 512], F32)
    ps_s0 = pb.psum("r_pss0", [128, 512], F32)
    ps_s1 = pb.psum("r_pss1", [128, 512], F32)
    ps_y = pb.psum("r_psy", [128, 512], F32)
    K._gather_ps = [ps_a, ps_b, ps_s0, ps_s1]

    dr_rt_in = K.dram_tmp("r_rt_in", [1, 2 * TL], F32)
    dr_rt_all = K.dram_tmp("r_rt_all", [2, 1, 2 * TL], F32)
    dr_slot = K.dram_tmp("r_slotd", [1, T], F32)
    dr_g = K.dram_tmp("r_g", [1, T], F32)
    dr_sel_in = K.dram_tmp("r_sel_in", [D, S2], BF16)
    dr_sel_out = K.dram_tmp("r_sel_out", [D, S2], BF16)
    dr_rows_in = K.dram_tmp("r_rows_in", [4, S2], F32)
    dr_rows_out = K.dram_tmp("r_rows_out", [4, S2], F32)
    dr_o2_in = K.dram_tmp("r_o2_in", [D, S2], BF16)
    dr_o2_out = K.dram_tmp("r_o2_out", [D, S2], BF16)
    dr_procb = K.dram_tmp("r_procb", [TL, D], BF16)

    s_ld = pb.sem("r_ld")
    s_v = pb.sem("r_v")
    s_s = pb.sem("r_s")
    s_t = pb.sem("r_t")
    s_g = pb.sem("r_g")
    s_cc = pb.sem("r_cc")
    lns = pb.sem("ln_s")
    gld = pb.sem("gld")

    pb.push_scope()
    packA = pb.sbuf("r_packA", [128, T], F32)
    packB = pb.sbuf("r_packB", [128, T], F32)
    grow = packA[0:1, 0:T]
    mrow = packA[32:33, 0:T]
    slotrow = packA[64:65, 0:T]
    zerorow = packA[96:97, 0:T]
    zrow = packB[0:1, 0:T]
    onesrow = packB[32:33, 0:T]
    tmprow = packB[64:65, 0:T]
    dstrow = packB[64:65, 0:TL]
    dchrow = packB[96:97, 0:TL]
    pb.op("vector", "memset", onescol[:, :], 1.0)
    pb.op("vector", "memset", zerorow[:, :], 0.0)
    pb.op("vector", "memset", onesrow[0:1, :], 1.0)
    pb.drain("vector")
    tpin = K.inputs["d_tpos"]
    lv = pb.dma("sync", tpf[:, :],
                tpin.rearrange("o (c p) -> (o p) c", p=128), s_ld,
                allow_slow_non_contiguous=True)
    pb.wait_dma("vector", s_ld, lv)

    pb.push_scope()
    pnrow = pb.sbuf("r_pnrow", [128, D], F32)
    pw = pb.sbuf("r_pw", [128, KC * 2 * D], BF16)
    res_fm = pb.sbuf("r_resfm", [128, KC * 512], BF16)
    ph_fm = pb.sbuf("r_phfm", [128, KC * 512], BF16)
    mlv = pb.sbuf("r_mlv", [128, 2 * KC * 512], BF16)

    lv = pb.dma("sync", pnrow[0:1, :], K.inputs["d_pnw"], s_ld)
    pb.wait_dma("gpsimd", s_ld, lv)
    pb.op("gpsimd", "partition_broadcast", pnrow[:, :], pnrow[0:1, :],
          channels=128, inc=(s_g, 1))
    pb.wait("vector", s_g)
    emit_weight_load(K, K.inputs["d_pw"], pw, D, 2 * D)
    pb.raw("tensor", lambda e: None)
    pb.wait_dma("tensor", gld)

    for gi in range(TL // 512):
        for q in range(4):
            t = gi * 4 + q
            pb.wait_dma("sync", s_ld)
            pb.wait("sync", s_v)
            lv = pb.dma("sync", sq[:, :], dr_proc[t * 128:(t + 1) * 128, :],
                        s_ld)
            lv = pb.dma("sync", hs_t[:, :], dr_hs[t * 128:(t + 1) * 128, :],
                        s_ld)
            pb.wait_dma("vector", s_ld, lv)
            pb.op("vector", "tensor_copy", rbf[:, :], sq[:, :], inc=(s_v, 1))
            pb.wait("sync", s_v)
            lvpb = pb.dma("sync", dr_procb[t * 128:(t + 1) * 128, :],
                          rbf[:, :], s_ld)
            pb.op("vector", "tensor_sub", sq[:, :], sq[:, :], hs_t[:, :])
            pb.drain("vector")
            pb.wait_dma("vector", s_ld, lvpb)
            pb.op("vector", "tensor_copy", rbf[:, :], sq[:, :], inc=(s_v, 1))
            pb.wait("tensor", s_v)
            for c in range(KC):
                slot = c % 4
                dps = ps_t[:, slot * 128:slot * 128 + 128]
                tile_transpose(K, rbf[:, c * 128:(c + 1) * 128], dps,
                               inc=(s_t, 1))
                pb.wait("vector", s_t)
                pb.op("vector", "tensor_copy",
                      res_fm[:, c * 512 + q * 128:c * 512 + q * 128 + 128],
                      dps, inc=(s_v, 1))
                pb.wait("tensor", s_v)
            pb.op("vector", "tensor_copy", sq[:, :], hs_t[:, :])
            pb.drain("vector")
            rmsnorm_tile(pb, K, sq[:, :], rbf[:, :], pnrow[:, :], stats,
                         otok, D, s_v, lns, eps=1e-6, inc=(s_v, 1))
            pb.wait("tensor", s_v)
            for c in range(KC):
                slot = c % 4
                dps = ps_t[:, 512 + slot * 128:512 + slot * 128 + 128]
                tile_transpose(K, rbf[:, c * 128:(c + 1) * 128], dps,
                               inc=(s_t, 1))
                pb.wait("vector", s_t)
                pb.op("vector", "tensor_copy",
                      ph_fm[:, c * 512 + q * 128:c * 512 + q * 128 + 128],
                      dps, inc=(s_v, 1))
                pb.wait("tensor", s_v)
        for mc in range(2 * KC):
            ps = ps_a if mc % 2 == 0 else ps_b
            for c in range(KC):
                stat = pw[:, c * 2 * D + mc * 128:c * 2 * D + (mc + 1) * 128]
                pb.op("tensor", "ldweights", stat)
                pb.op("tensor", "matmul", ps[:, 0:512], stat,
                      ph_fm[:, c * 512:(c + 1) * 512], start=(c == 0),
                      stop=(c == KC - 1),
                      inc=(s_t, 1) if c == KC - 1 else None)
            pb.wait("vector", s_t)
            pb.op("vector", "tensor_copy", mlv[:, mc * 512:(mc + 1) * 512],
                  ps[:, 0:512], inc=(s_v, 1))
            pb.wait("tensor", s_v)
        pb.wait("scalar", s_v)
        for c in range(KC):
            pb.op("scalar", "activation", f32a[:, :],
                  res_fm[:, c * 512:(c + 1) * 512], AF.Square, inc=(s_s, 1))
            pb.wait("tensor", s_s)
            pb.op("tensor", "matmul", ps_y[0:1, 0:512], onescol[:, 0:1],
                  f32a[:, :], start=(c == 0), stop=(c == KC - 1),
                  inc=(s_t, 1))
            pb.wait("scalar", s_t)
        pb.wait("vector", s_t)
        pb.op("vector", "tensor_scalar",
              dstrow[0:1, gi * 512:(gi + 1) * 512], ps_y[0:1, 0:512],
              1.0 / D, None, ALU.mult, inc=(s_v, 1))
        pb.wait("tensor", s_v)
        for c in range(KC):
            pb.wait("vector", s_s)
            pb.op("vector", "tensor_sub", f32a[:, :],
                  res_fm[:, c * 512:(c + 1) * 512],
                  mlv[:, c * 512:(c + 1) * 512], inc=(s_v, 1))
            pb.wait("scalar", s_v)
            pb.op("scalar", "activation", f32a[:, :], f32a[:, :], AF.Square,
                  inc=(s_s, 1))
            pb.op("scalar", "activation", f32b[:, :],
                  mlv[:, (KC + c) * 512:(KC + c + 1) * 512], AF.Exp,
                  scale=-1.0, inc=(s_s, 1))
            pb.wait("vector", s_s)
            pb.op("vector", "tensor_scalar", f32a[:, :], f32a[:, :], 1.0,
                  None, ALU.add)
            pb.drain("vector")
            pb.op("vector", "tensor_tensor", f32a[:, :], f32a[:, :],
                  f32b[:, :], ALU.mult)
            pb.drain("vector")
            pb.op("vector", "tensor_add", f32a[:, :], f32a[:, :],
                  mlv[:, (KC + c) * 512:(KC + c + 1) * 512])
            pb.drain("vector")
            pb.op("vector", "tensor_scalar", f32a[:, :], f32a[:, :], -1.0,
                  None, ALU.add, inc=(s_v, 1))
            pb.wait("tensor", s_v)
            pb.op("tensor", "matmul", ps_y[0:1, 0:512], onescol[:, 0:1],
                  f32a[:, :], start=(c == 0), stop=(c == KC - 1),
                  inc=(s_t, 1))
            pb.wait("vector", s_t)
        pb.op("vector", "tensor_scalar",
              dchrow[0:1, gi * 512:(gi + 1) * 512], ps_y[0:1, 0:512],
              0.5 / D, None, ALU.mult, inc=(s_v, 1))
        pb.wait("tensor", s_v)
        pb.wait("scalar", s_v)
    pb.pop_scope()

    pb.op("vector", "nop", inc=(s_v, 1))
    pb.wait("sync", s_v)
    pb.dma("sync", dr_rt_in[0:1, 0:TL], dstrow[:, :], s_ld)
    pb.dma("sync", dr_rt_in[0:1, TL:2 * TL], dchrow[:, :], s_ld)
    pb.raw("gpsimd", lambda g: None)
    pb.wait_dma("gpsimd", s_ld)
    pb.op("gpsimd", "collective_compute", "AllGather", ALU.bypass,
          replica_groups=cfg["groups"], ins=[dr_rt_in], outs=[dr_rt_all],
          inc=(s_cc, 1))
    pb.wait("sync", s_cc)
    for sl in range(2):
        pb.dma("sync", zrow[0:1, sl * TL:(sl + 1) * TL],
               dr_rt_all[sl, 0:1, 0:TL], s_ld)
        lv = pb.dma("sync", grow[0:1, sl * TL:(sl + 1) * TL],
                    dr_rt_all[sl, 0:1, TL:2 * TL], s_ld)
    pb.wait_dma("vector", s_ld, lv)
    pb.op("vector", "reduce_sum", sc1[0:1, 0:1], zrow[0:1, :], axis=AX.X)
    pb.op("vector", "reduce_sum", sc1[0:1, 1:2], grow[0:1, :], axis=AX.X)
    pb.drain("vector")
    pb.op("vector", "tensor_scalar", sc1[0:1, 0:1], sc1[0:1, 0:1], 1.0 / T,
          None, ALU.mult)
    pb.op("vector", "tensor_scalar", sc1[0:1, 1:2], sc1[0:1, 1:2], 1.0 / T,
          None, ALU.mult)
    pb.drain("vector")
    pb.op("vector", "tensor_add", zrow[0:1, :], zrow[0:1, :], grow[0:1, :])
    pb.drain("vector")
    pb.op("vector", "tensor_scalar", zrow[0:1, :], zrow[0:1, :],
          sc1[0:1, 0:1], None, ALU.subtract)
    pb.drain("vector")
    pb.op("vector", "tensor_scalar", zrow[0:1, :], zrow[0:1, :],
          sc1[0:1, 1:2], None, ALU.subtract, inc=(s_v, 1))
    pb.wait("scalar", s_v)
    pb.op("scalar", "activation", grow[0:1, :], zrow[0:1, :], AF.Sigmoid,
          inc=(s_s, 1))
    pb.wait("vector", s_s)
    pb.op("vector", "memset", gth[0:1, 0:1], 0.0)
    pb.op("vector", "memset", gth[0:1, 1:2], 1.0)
    pb.drain("vector")
    for it in range(42):
        pb.op("vector", "tensor_add", gth[0:1, 2:3], gth[0:1, 0:1],
              gth[0:1, 1:2])
        pb.drain("vector")
        pb.op("vector", "tensor_scalar", gth[0:1, 2:3], gth[0:1, 2:3], 0.5,
              None, ALU.mult)
        pb.drain("vector")
        pb.op("vector", "tensor_scalar", zrow[0:1, :], grow[0:1, :],
              gth[0:1, 2:3], 0.0, ALU.is_ge, ALU.add,
              accum_out=sc1[0:1, 2:3])
        pb.drain("vector")
        pb.op("vector", "tensor_scalar", gth[0:1, 3:4], sc1[0:1, 2:3],
              float(S2), None, ALU.is_ge)
        pb.drain("vector")
        pb.op("vector", "tensor_sub", sc1[0:1, 3:4], gth[0:1, 2:3],
              gth[0:1, 0:1])
        pb.drain("vector")
        pb.op("vector", "tensor_tensor", sc1[0:1, 3:4], sc1[0:1, 3:4],
              gth[0:1, 3:4], ALU.mult)
        pb.drain("vector")
        pb.op("vector", "tensor_add", gth[0:1, 0:1], gth[0:1, 0:1],
              sc1[0:1, 3:4])
        pb.drain("vector")
        pb.op("vector", "tensor_sub", sc1[0:1, 3:4], gth[0:1, 2:3],
              gth[0:1, 1:2])
        pb.drain("vector")
        pb.op("vector", "tensor_scalar", sc1[0:1, 4:5], gth[0:1, 3:4], -1.0,
              1.0, ALU.mult, ALU.add)
        pb.drain("vector")
        pb.op("vector", "tensor_tensor", sc1[0:1, 3:4], sc1[0:1, 3:4],
              sc1[0:1, 4:5], ALU.mult)
        pb.drain("vector")
        pb.op("vector", "tensor_add", gth[0:1, 1:2], gth[0:1, 1:2],
              sc1[0:1, 3:4])
        pb.drain("vector")
    pb.op("vector", "tensor_scalar", mrow[0:1, :], grow[0:1, :],
          gth[0:1, 0:1], None, ALU.is_ge)
    pb.drain("vector")
    pb.op("vector", "tensor_tensor_scan", slotrow[0:1, :], onesrow[0:1, :],
          mrow[0:1, :], 0.0, ALU.mult, ALU.add)
    pb.drain("vector")
    pb.op("vector", "tensor_scalar", tmprow[0:1, :], mrow[0:1, :], 1.0e5,
          -1.0e5, ALU.mult, ALU.add)
    pb.drain("vector")
    pb.op("vector", "tensor_tensor", slotrow[0:1, :], slotrow[0:1, :],
          tmprow[0:1, :], ALU.add)
    pb.drain("vector")
    pb.op("vector", "tensor_scalar", slotrow[0:1, :], slotrow[0:1, :], -1.0,
          None, ALU.add, inc=(s_v, 1))
    pb.wait("sync", s_v)
    pb.dma("sync", dr_slot[:, :], slotrow[0:1, :], s_ld)
    pb.dma("sync", dr_g[:, :], grow[0:1, :], s_ld)
    pb.wait_dma("sync", s_ld)

    def bfr(buf, dr, cols=False):
        def bf(e, buf=buf, dr=dr, cols=cols):
            mm = e.partition_id() % 2
            if cols:
                src = dr[0:1, bass.ts(mm, TL)].rearrange(
                    "o (c p) -> (o p) c", p=128)
            else:
                src = dr[0:1, bass.ts(mm, TL)]
            return (buf, src)
        return bf
    pb.dma_dyn("sync", s_ld, bfr(myslot[0:1, :], dr_slot))
    pb.dma_dyn("sync", s_ld, bfr(slotcols[:, :], dr_slot, cols=True))
    lvv = pb.dma_dyn("sync", s_ld, bfr(gcf[:, :], dr_g, cols=True))
    pb.wait_dma("vector", s_ld, lvv)
    pb.wait_dma("gpsimd", s_ld, lvv)
    pb.op("vector", "tensor_copy", gcols[:, :], gcf[:, :])
    pb.drain("vector")
    pb.op("gpsimd", "iota", icf[:, 0:1], pattern=[[1, 1]], base=0,
          channel_multiplier=1, allow_small_or_imprecise_dtypes=True,
          inc=(s_g, 1))
    pb.wait("vector", s_g)
    pb.op("vector", "tensor_scalar", tpf[:, :], tpf[:, :], icf[:, 0:1],
          None, ALU.subtract)
    pb.drain("vector")
    pb.op("vector", "tensor_copy", basecols[:, :], tpf[:, :])
    pb.drain("vector")
    pb.op("vector", "tensor_copy", iotacol[:, 0:1], icf[:, 0:1])
    pb.drain("vector")

    if cfg.get("dbg_d"):
        dbs = pb.sem("dbgd_st")
        o = K.nc.dram_tensor("dbg_proc", [TL, D], F32,
                             kind="ExternalOutput").ap()
        pb.dma("sync", o[:, :], dr_proc[:, :], dbs)
        for nm, buf in (("dbg_g", grow), ("dbg_m", mrow),
                        ("dbg_slot", slotrow)):
            o = K.nc.dram_tensor(nm, [1, T], F32, kind="ExternalOutput").ap()
            pb.dma("sync", o[0:1, :], buf[0:1, :], dbs)
        o = K.nc.dram_tensor("dbg_th", [1, 4], F32,
                             kind="ExternalOutput").ap()
        pb.dma("sync", o[0:1, :], gth[0:1, :], dbs)
        pb.wait_dma("sync", dbs)
    pb.push_scope()
    jbc = pb.sbuf("r_jbc", [128, S2], F32)
    jrow = pb.sbuf("r_jrow", [1, S2], F32)
    pt = pb.sbuf("r_pt", [128, NT * S2], BF16)
    ptile_b = pb.sbuf("r_ptileb", [128, 1024], BF16)
    selp = pb.sbuf("r_selp", [128, KC * S2], BF16)

    pb.op("gpsimd", "iota", jrow[0:1, :], pattern=[[1, S2]], base=0,
          channel_multiplier=0, allow_small_or_imprecise_dtypes=True)
    pb.drain("gpsimd")
    pb.op("gpsimd", "partition_broadcast", jbc[:, :], jrow[0:1, :],
          channels=128, inc=(s_g, 1))
    pb.wait("vector", s_g)
    for tc in range(NT):
        pb.op("vector", "tensor_scalar", pt[:, tc * S2:(tc + 1) * S2],
              jbc[:, :], slotcols[:, tc:tc + 1], None, ALU.is_equal)
        pb.drain("vector")
    pb.op("vector", "nop", inc=(s_v, 1))
    pb.wait("tensor", s_v)
    NFG = max(1, KC // 4)
    s_ptb = pb.sem("r_ptb")
    it = 0
    guards = []
    for half in range(S2 // W2H):
        for fcg in range(NFG):
            fcs = [fcg * 4 + i for i in range(min(4, KC))]
            for tc in range(NT):
                slot = it % 2
                if it >= 2:
                    gsem, gval = guards[it - 2]
                    pb.wait("sync", gsem, gval)
                lv = pb.dma("sync",
                            ptile_b[:, slot * 512:slot * 512 +
                                    len(fcs) * 128],
                            dr_procb[tc * 128:(tc + 1) * 128,
                                     fcs[0] * 128:(fcs[-1] + 1) * 128],
                            s_ld)
                pb.wait_dma("tensor", s_ld, lv)
                for i, fc in enumerate(fcs):
                    stat = ptile_b[:, slot * 512 + i * 128:
                                   slot * 512 + (i + 1) * 128]
                    last = (i == len(fcs) - 1)
                    pb.op("tensor", "ldweights", stat)
                    pb.op("tensor", "matmul", psum_sel(K, fc)[:, 0:W2H],
                          stat,
                          pt[:, tc * S2 + half * W2H:
                             tc * S2 + (half + 1) * W2H],
                          start=(tc == 0), stop=(tc == NT - 1),
                          inc=(s_t, 1) if tc == NT - 1 else
                          ((s_ptb, 1) if last else None))
                sem = s_t if tc == NT - 1 else s_ptb
                guards.append((sem, sem.n))
                it += 1
            for fc in fcs:
                pb.wait("vector", s_t)
                pb.op("vector", "tensor_copy",
                      selp[:, fc * S2 + half * W2H:
                           fc * S2 + (half + 1) * W2H],
                      psum_sel(K, fc)[:, 0:W2H], inc=(s_v, 1))
            pb.wait("tensor", s_v)
    for half in range(S2 // W2H):
        for tc in range(NT):
            pb.op("tensor", "ldweights", gcols[:, tc:tc + 1])
            pb.op("tensor", "matmul", ps_y[0:1, 0:W2H], gcols[:, tc:tc + 1],
                  pt[:, tc * S2 + half * W2H:tc * S2 + (half + 1) * W2H],
                  start=(tc == 0), stop=(tc == NT - 1),
                  inc=(s_t, 1) if tc == NT - 1 else None)
        pb.wait("vector", s_t)
        pb.op("vector", "tensor_copy",
              scores_row[0:1, half * W2H:(half + 1) * W2H],
              ps_y[0:1, 0:W2H], inc=(s_v, 1))
        pb.wait("tensor", s_v)
    for half in range(S2 // W2H):
        for tc in range(NT):
            pb.op("tensor", "ldweights", basecols[:, tc:tc + 1])
            pb.op("tensor", "matmul", ps_y[0:1, 0:W2H],
                  basecols[:, tc:tc + 1],
                  pt[:, tc * S2 + half * W2H:tc * S2 + (half + 1) * W2H],
                  start=(tc == 0), stop=False)
        for tc in range(NT):
            pb.op("tensor", "ldweights", iotacol[:, 0:1])
            pb.op("tensor", "matmul", ps_y[0:1, 0:W2H], iotacol[:, 0:1],
                  pt[:, tc * S2 + half * W2H:tc * S2 + (half + 1) * W2H],
                  start=False, stop=(tc == NT - 1),
                  inc=(s_t, 1) if tc == NT - 1 else None)
        pb.wait("vector", s_t)
        pb.op("vector", "tensor_copy",
              pos_row[0:1, half * W2H:(half + 1) * W2H],
              ps_y[0:1, 0:W2H], inc=(s_v, 1))
        pb.wait("tensor", s_v)
    pb.wait("sync", s_v)
    for fc in range(KC):
        pb.dma("sync", dr_sel_in[fc * 128:(fc + 1) * 128, :],
               selp[:, fc * S2:(fc + 1) * S2], s_ld)
    pb.dma("sync", dr_rows_in[0:1, :], scores_row[0:1, :], s_ld)
    pb.dma("sync", dr_rows_in[1:2, :], pos_row[0:1, :], s_ld)
    pb.dma("sync", dr_rows_in[2:3, :], zerorow[0:1, 0:S2], s_ld)
    pb.dma("sync", dr_rows_in[3:4, :], zerorow[0:1, 0:S2], s_ld)
    pb.raw("gpsimd", lambda g: None)
    pb.wait_dma("gpsimd", s_ld)
    pb.op("gpsimd", "collective_compute", "AllReduce", ALU.add,
          replica_groups=cfg["groups"], ins=[dr_sel_in], outs=[dr_sel_out],
          inc=(s_cc, 1))
    pb.op("gpsimd", "collective_compute", "AllReduce", ALU.add,
          replica_groups=cfg["groups"], ins=[dr_rows_in],
          outs=[dr_rows_out], inc=(s_cc, 1))
    pb.wait("sync", s_cc)
    for fc in range(KC):
        pb.dma("sync", selfm[:, fc * S2:(fc + 1) * S2],
               dr_sel_out[fc * 128:(fc + 1) * 128, :], s_ld)
    pb.dma("sync", scores_row[0:1, :], dr_rows_out[0:1, :], s_ld)
    lv = pb.dma("sync", pos_row[0:1, :], dr_rows_out[1:2, :], s_ld)
    pb.wait_dma("vector", s_ld, lv)
    pb.wait_dma("scalar", s_ld, lv)
    pb.wait_dma("gpsimd", s_ld, lv)
    pb.wait_dma("tensor", s_ld, lv)
    if cfg.get("dbg_d"):
        dbs = pb.sem("dbgd_st")
        o = K.nc.dram_tensor("dbg_pt", [128, NT * S2], BF16,
                             kind="ExternalOutput").ap()
        pb.dma("sync", o[:, :], pt[:, :], dbs)
        o = K.nc.dram_tensor("dbg_scol", [128, NT], F32,
                             kind="ExternalOutput").ap()
        pb.dma("sync", o[:, :], slotcols[:, :], dbs)
        o = K.nc.dram_tensor("dbg_sel", [KC * 128, S2], BF16,
                             kind="ExternalOutput").ap()
        for fc in range(KC):
            pb.dma("sync", o[fc * 128:(fc + 1) * 128, :],
                   selfm[:, fc * S2:(fc + 1) * S2], dbs)
        o = K.nc.dram_tensor("dbg_rows", [2, S2], F32,
                             kind="ExternalOutput").ap()
        pb.dma("sync", o[0:1, :], scores_row[0:1, :], dbs)
        pb.dma("sync", o[1:2, :], pos_row[0:1, :], dbs)
        pb.wait_dma("sync", dbs)
    pb.pop_scope()
    pb.pop_scope()
    pb.barrier()
    K.pb.label = "dec2"
    emit_dec2(K, selfm, (scores_row, pos_row), locals())


def emit_dec2(K, selfm, rows_p, env):
    cfg, pb = K.cfg, K.pb
    T, TL, D, HL, HD = cfg["T"], cfg["TL"], cfg["D"], cfg["HL"], cfg["HD"]
    FFN = cfg.get("FFN", 4 * D)
    S2 = T // 4
    SC = S2 // 128
    NT = TL // 128
    KC = D // 128
    MY = HL * HD
    MYC = MY // 128
    HALF = HD // 2
    VW = HL * (HD + 1)
    G2 = min(512, S2)
    W2H = G2
    NG2 = S2 // G2
    GT2 = G2 // 128
    FC = FFN // 128
    eps = 1e-5
    Xd = K.x_dram
    dr_proc = K.d_proc
    ps_t, ps_a, ps_b = env["ps_t"], env["ps_a"], env["ps_b"]
    ps_s0, ps_s1, ps_y = env["ps_s0"], env["ps_s1"], env["ps_y"]
    s_ld, s_v, s_s, s_t = env["s_ld"], env["s_v"], env["s_s"], env["s_t"]
    s_g, s_cc, lns, gld = env["s_g"], env["s_cc"], env["lns"], env["gld"]
    sq, sq2, onescol = env["sq"], env["sq2"], env["onescol"]
    myslot, otok, stats = env["myslot"], env["otok"], env["stats"]
    mean_c, rstd_c = env["mean_c"], env["rstd_c"]
    dr_o2_in, dr_o2_out = env["dr_o2_in"], env["dr_o2_out"]
    zerorow = env["zerorow"]

    pb.push_scope()
    w2 = pb.sbuf("r2_w2", [128, KC * 3 * MY], BF16)
    wo2 = pb.sbuf("r2_wo2", [128, MYC * D], BF16)
    x2fm = env["x2fm"]
    h2fm = env["h2fm"]
    q2 = pb.sbuf("r2_q2", [128, MYC * S2], BF16)
    qpad2 = pb.sbuf("r2_qpad", [128, 2 * G2], BF16)
    k2 = pb.sbuf("r2_k2", [128, MYC * S2], BF16)
    v2tok = pb.sbuf("r2_v2t", [128, SC * VW], BF16)
    ang2 = pb.sbuf("r2_ang2", [HALF, S2], F32)
    c2 = pb.sbuf("r2_c2", [128, S2], BF16)
    s2t = pb.sbuf("r2_s2", [128, S2], BF16)
    invf2 = pb.sbuf("r2_invf2", [HALF, 1], F32)
    rr_row = pb.sbuf("r2_rrrow", [1, S2], F32)
    rrb = pb.sbuf("r2_rrb", [128, min(512, S2)], F32)
    ln12c = pb.sbuf("r2_ln12", [128, KC], F32)
    ln22c = pb.sbuf("r2_ln22", [128, KC], F32)
    ptile = pb.sbuf("r2_p", [128, G2], BF16)
    tri2 = pb.sbuf("r2_tri", [128, GT2 * G2], BF16)
    ygrp = pb.sbuf("r2_ygrp", [128, MYC * G2], BF16)
    denb = pb.sbuf("r2_denb", [HD, G2], F32)
    den1 = pb.sbuf("r2_den1", [1, G2], F32)
    o2st = pb.sbuf("r2_o2st", [128, 2 * S2], BF16)
    scb = env["scb"]
    ropesc = pb.sbuf("r2_ropesc", [128, S2], BF16)

    lv = pb.dma("sync", ln12c[:, :], K.inputs["d_ln12c"], s_ld)
    lv = pb.dma("sync", ln22c[:, :], K.inputs["d_ln22c"], s_ld)
    emit_weight_load(K, K.inputs["d_w2"], w2, D, 3 * MY)
    emit_weight_load(K, K.inputs["d_wo2"], wo2, MY, D)
    pb.wait_dma("vector", s_ld, lv)

    pb.op("vector", "tensor_copy", rr_row[0:1, 0:S2], rows_p[1][0:1, :],
          inc=(s_v, 1))
    pb.op("gpsimd", "iota", invf2[:, 0:1], pattern=[[1, 1]], base=0,
          channel_multiplier=1, allow_small_or_imprecise_dtypes=True)
    pb.drain("gpsimd")
    pb.raw("gpsimd", lambda g: None)
    pb.wait("gpsimd", s_v)
    pb.op("gpsimd", "partition_broadcast", ang2[:, :], rr_row[0:1, 0:S2],
          channels=HALF, inc=(s_g, 1))
    pb.wait("vector", s_g)
    pb.wait("scalar", s_g)
    pb.op("scalar", "activation", invf2[:, 0:1], invf2[:, 0:1], AF.Exp,
          scale=-math.log(10000.0) / HALF, inc=(s_s, 1))
    pb.wait("vector", s_s)
    pb.wait("vector", s_g)
    pb.op("vector", "tensor_scalar", ang2[:, :], ang2[:, :], invf2[:, 0:1],
          None, ALU.mult)
    pb.drain("vector")
    emit_sincos(K, ang2[0:HALF, :], c2[0:HALF, :], s2t[0:HALF, :], HALF,
                S2)
    pb.op("vector", "nop", inc=(s_v, 1))
    pb.wait("sync", s_v)
    for rr in (32, 64, 96):
        pb.dma("sync", c2[rr:rr + HALF, :], c2[0:HALF, :], s_ld)
        lvc2 = pb.dma("sync", s2t[rr:rr + HALF, :], s2t[0:HALF, :], s_ld)
    pb.wait_dma("vector", s_ld, lvc2)
    for j in range(GT2):
        sl = tri2[:, j * G2:(j + 1) * G2]
        pb.op("gpsimd", "memset", sl, 1.0)
        pb.drain("gpsimd")
        pb.op("gpsimd", "affine_select", sl, sl, pattern=[[1, G2]],
              base=-128 * j, channel_multiplier=-1, compare_op=ALU.is_ge,
              fill=0.0)
        pb.drain("gpsimd")
    pb.op("gpsimd", "nop", inc=(s_g, 1))
    pb.wait("vector", s_g)
    pb.wait("vector", s_g)
    pb.op("vector", "tensor_copy", rr_row[0:1, 0:S2], rows_p[0][0:1, :],
          inc=(s_v, 1))
    pb.raw("gpsimd", lambda g: None)
    pb.wait("gpsimd", s_v)
    pb.op("gpsimd", "partition_broadcast", scb[:, :], rr_row[0:1, 0:S2],
          channels=128, inc=(s_g, 1))
    pb.wait("vector", s_g)

    for half in range(S2 // W2H):
        for c in range(KC):
            pb.wait("scalar", s_v)
            pb.op("scalar", "activation", env["f32a"][:, 0:W2H],
                  selfm[:, c * S2 + half * W2H:c * S2 + (half + 1) * W2H],
                  AF.Square, inc=(s_s, 1))
            pb.wait("tensor", s_s)
            pb.op("tensor", "matmul", ps_y[0:1, 0:W2H], onescol[:, 0:1],
                  env["f32a"][:, 0:W2H], start=(c == 0), stop=(c == KC - 1),
                  inc=(s_t, 1))
            pb.wait("scalar", s_t)
        pb.wait("vector", s_t)
        pb.op("vector", "tensor_scalar", rr_row[0:1, 0:W2H],
              ps_y[0:1, 0:W2H], 1.0 / D, 1e-6, ALU.mult, ALU.add)
        pb.drain("vector")
        pb.op("vector", "reciprocal", rr_row[0:1, 0:W2H],
              rr_row[0:1, 0:W2H], inc=(s_v, 1))
        pb.wait("scalar", s_v)
        pb.op("scalar", "activation", rr_row[0:1, 0:W2H],
              rr_row[0:1, 0:W2H], AF.Sqrt, inc=(lns, 1))
        pb.wait("gpsimd", lns)
        pb.op("gpsimd", "partition_broadcast", rrb[:, 0:W2H],
              rr_row[0:1, 0:W2H], channels=128, inc=(s_g, 1))
        pb.wait("vector", s_g)
        for c in range(KC):
            pb.op("vector", "tensor_tensor", env["f32a"][:, 0:W2H],
                  selfm[:, c * S2 + half * W2H:c * S2 + (half + 1) * W2H],
                  rrb[:, 0:W2H], ALU.mult)
            pb.drain("vector")
            pb.op("vector", "tensor_scalar",
                  h2fm[:, c * S2 + half * W2H:c * S2 + (half + 1) * W2H],
                  env["f32a"][:, 0:W2H], ln12c[:, c:c + 1], None, ALU.mult)
            pb.drain("vector")
        pb.op("vector", "nop", inc=(s_v, 1))
    pb.wait("tensor", s_v)
    pb.wait_dma("tensor", gld)

    def rope_fm(dst, src, mc, width):
        for r in (0, 64):
            x1 = src[r:r + HALF, 0:width]
            x2 = src[r + HALF:r + 2 * HALF, 0:width]
            cc1 = c2[r:r + HALF, 0:width]
            cc2 = c2[r + HALF:r + 2 * HALF, 0:width]
            ss1 = s2t[r:r + HALF, 0:width]
            ss2 = s2t[r + HALF:r + 2 * HALF, 0:width]
            d1 = dst[r:r + HALF, 0:width]
            d2 = dst[r + HALF:r + 2 * HALF, 0:width]
            pb.op("vector", "tensor_tensor", d1, x1, cc1, ALU.mult)
            pb.op("vector", "tensor_tensor", d2, x2, cc2, ALU.mult)
            pb.drain("vector")
            pb.op("vector", "tensor_tensor", ropesc[r:r + HALF, 0:width],
                  x2, ss2, ALU.mult)
            pb.op("vector", "tensor_tensor",
                  ropesc[r + HALF:r + 2 * HALF, 0:width], x1, ss1, ALU.mult)
            pb.drain("vector")
            pb.op("vector", "tensor_sub", d1, d1,
                  ropesc[r:r + HALF, 0:width])
            pb.op("vector", "tensor_add", d2, d2,
                  ropesc[r + HALF:r + 2 * HALF, 0:width])
            pb.drain("vector")

    pb.op("vector", "memset", v2tok[:, :], 0.0)
    pb.drain("vector")
    for tc in range(SC):
        for h in range(HL):
            base = tc * VW + h * (HD + 1) + HD
            pb.op("vector", "memset", v2tok[:, base:base + 1], 1.0)
    pb.drain("vector")
    pb.op("vector", "nop", inc=(s_v, 1))
    pb.wait("tensor", s_v)
    for which, wofs, dstb in (("q", 0, q2), ("k", MY, k2), ("v", 2 * MY, None)):
        for mc in range(MYC):
            for half in range(S2 // W2H):
                ps = ps_a if (mc + half) % 2 == 0 else ps_b
                for c in range(KC):
                    stat = w2[:, c * 3 * MY + wofs + mc * 128:
                              c * 3 * MY + wofs + (mc + 1) * 128]
                    pb.op("tensor", "ldweights", stat)
                    pb.op("tensor", "matmul", ps[:, 0:W2H], stat,
                          h2fm[:, c * S2 + half * W2H:
                               c * S2 + (half + 1) * W2H],
                          start=(c == 0), stop=(c == KC - 1),
                          inc=(s_t, 1) if c == KC - 1 else None)
                pb.wait("vector", s_t)
                if which in ("q", "k"):
                    pb.op("vector", "tensor_copy",
                          sq[:, 0:W2H], ps[:, 0:W2H], inc=(s_v, 1))
                    pb.wait("tensor", s_v)
                    pb.drain("vector")
                    dst = dstb[:, mc * S2 + half * W2H:
                               mc * S2 + (half + 1) * W2H]
                    for r in (0, 64):
                        x1 = sq[r:r + HALF, 0:W2H]
                        x2 = sq[r + HALF:r + 2 * HALF, 0:W2H]
                        cc1 = c2[r:r + HALF, half * W2H:(half + 1) * W2H]
                        cc2 = c2[r + HALF:r + 2 * HALF,
                                 half * W2H:(half + 1) * W2H]
                        ss1 = s2t[r:r + HALF, half * W2H:(half + 1) * W2H]
                        ss2 = s2t[r + HALF:r + 2 * HALF,
                                  half * W2H:(half + 1) * W2H]
                        d1 = dst[r:r + HALF, 0:W2H]
                        d2 = dst[r + HALF:r + 2 * HALF, 0:W2H]
                        pb.op("vector", "tensor_tensor", d1, x1, cc1,
                              ALU.mult)
                        pb.op("vector", "tensor_tensor", d2, x2, cc2,
                              ALU.mult)
                        pb.drain("vector")
                        pb.op("vector", "tensor_tensor",
                              ropesc[r:r + HALF, 0:W2H], x2, ss2, ALU.mult)
                        pb.op("vector", "tensor_tensor",
                              ropesc[r + HALF:r + 2 * HALF, 0:W2H], x1, ss1,
                              ALU.mult)
                        pb.drain("vector")
                        pb.op("vector", "tensor_sub", d1, d1,
                              ropesc[r:r + HALF, 0:W2H])
                        pb.op("vector", "tensor_add", d2, d2,
                              ropesc[r + HALF:r + 2 * HALF, 0:W2H])
                        pb.drain("vector")
                    pb.op("vector", "nop", inc=(s_v, 1))
                    pb.wait("tensor", s_v)
                else:
                    pb.wait("vector", s_t)
                    pb.op("vector", "tensor_copy", ropesc[:, 0:W2H],
                          ps[:, 0:W2H], inc=(s_v, 1))
                    pb.wait("tensor", s_v)
                    for q in range(W2H // 128):
                        slot = q % 4
                        dps = ps_t[:, 512 + slot * 128:512 + slot * 128 + 128]
                        tile_transpose(K, ropesc[:, q * 128:(q + 1) * 128],
                                       dps, inc=(s_t, 1))
                        pb.wait("vector", s_t)
                        tc = half * 4 + q
                        for hh in range(128 // HD):
                            h = mc * (128 // HD) + hh
                            pb.op("vector", "tensor_copy",
                                  v2tok[:, tc * VW + h * (HD + 1):
                                       tc * VW + h * (HD + 1) + HD],
                                  dps[:, hh * HD:(hh + 1) * HD],
                                  inc=(s_v, 1))
                        pb.wait("tensor", s_v)
    pb.barrier()

    pb.op("vector", "memset", qpad2[:, :], 0.0)
    pb.drain("vector")
    for g in range(NG2):
        nkt = (g + 1) * GT2
        for h in range(HL):
            mc0, r0 = divmod(h * HD, 128)
            sl = r0 // 64
            pb.wait("vector", s_t)
            pb.op("vector", "tensor_copy",
                  qpad2[r0:r0 + HD, sl * G2:(sl + 1) * G2],
                  q2[r0:r0 + HD, mc0 * S2 + g * G2:mc0 * S2 + (g + 1) * G2],
                  inc=(s_v, 1))
            pb.wait("tensor", s_v)
            for kt in range(nkt):
                ps_sc = ps_s0 if kt % 2 == 0 else ps_s1
                stat = k2[:, mc0 * S2 + kt * 128:mc0 * S2 + (kt + 1) * 128]
                pb.op("tensor", "ldweights", stat)
                pb.op("tensor", "matmul", ps_sc[:, 0:G2], stat,
                      qpad2[:, sl * G2:(sl + 1) * G2],
                      start=True, stop=True, inc=(s_t, 1))
                pb.wait("scalar", s_t)
                pb.op("scalar", "activation", ptile[:, 0:G2],
                      ps_sc[:, 0:G2], AF.Exp, scale=1.0 / math.sqrt(HD),
                      inc=(s_s, 1))
                diag = kt - g * GT2
                pb.wait("vector", s_s)
                if diag >= 0:
                    pb.op("vector", "tensor_tensor", ptile[:, 0:G2],
                          ptile[:, 0:G2], tri2[:, diag * G2:(diag + 1) * G2],
                          ALU.mult, inc=(s_v, 1))
                else:
                    pb.op("vector", "nop", inc=(s_v, 1))
                pb.wait("tensor", s_v)
                pb.op("tensor", "ldweights",
                      v2tok[:, kt * VW + h * (HD + 1):
                            kt * VW + (h + 1) * (HD + 1)])
                pb.op("tensor", "matmul", ps_y[0:HD + 1, 0:G2],
                      v2tok[:, kt * VW + h * (HD + 1):
                            kt * VW + (h + 1) * (HD + 1)],
                      ptile[:, 0:G2], start=(kt == 0), stop=(kt == nkt - 1),
                      inc=(s_t, 1))
                pb.wait("scalar", s_t)
            pb.wait("vector", s_t)
            pb.op("vector", "reciprocal", den1[0:1, 0:G2],
                  ps_y[HD:HD + 1, 0:G2], inc=(s_v, 1))
            pb.wait("gpsimd", s_v)
            pb.op("gpsimd", "partition_broadcast", denb[:, 0:G2],
                  den1[0:1, 0:G2], channels=HD, inc=(s_g, 1))
            pb.wait("vector", s_g)
            pb.op("vector", "tensor_tensor",
                  ygrp[r0:r0 + HD, mc0 * G2:(mc0 + 1) * G2],
                  ps_y[0:HD, 0:G2], denb[:, 0:G2], ALU.mult, inc=(s_v, 1))
            pb.wait("tensor", s_v)
        for oc in range(KC):
            ps = ps_a if oc % 2 == 0 else ps_b
            for c in range(MYC):
                stat = wo2[:, c * D + oc * 128:c * D + (oc + 1) * 128]
                pb.op("tensor", "ldweights", stat)
                pb.op("tensor", "matmul", ps[:, 0:G2], stat,
                      ygrp[:, c * G2:(c + 1) * G2], start=(c == 0),
                      stop=(c == MYC - 1),
                      inc=(s_t, 1) if c == MYC - 1 else None)
            pb.wait("vector", s_t)
            if g > 0 or oc >= 2:
                pb.wait_dma("vector", s_ld)
            osl = o2st[:, (oc % 2) * S2:(oc % 2) * S2 + G2]
            pb.op("vector", "tensor_copy", osl, ps[:, 0:G2], inc=(s_v, 1))
            pb.wait("sync", s_v)
            pb.dma("sync",
                   dr_o2_in[oc * 128:(oc + 1) * 128, g * G2:(g + 1) * G2],
                   osl, s_ld)
            pb.wait("tensor", s_v)
    pb.raw("gpsimd", lambda g: None)
    pb.wait_dma("gpsimd", s_ld)
    pb.op("gpsimd", "collective_compute", "AllReduce", ALU.add,
          replica_groups=cfg["groups"], ins=[dr_o2_in], outs=[dr_o2_out],
          inc=(s_cc, 1))
    pb.wait("sync", s_cc)
    for oc in range(KC):
        if oc >= 2:
            pb.wait("sync", s_v)
        osl = o2st[:, (oc % 2) * S2:(oc % 2 + 1) * S2]
        lv = pb.dma("sync", osl, dr_o2_out[oc * 128:(oc + 1) * 128, :], s_ld)
        pb.wait_dma("vector", s_ld, lv)
        pb.op("vector", "tensor_add", x2fm[:, oc * S2:(oc + 1) * S2],
              selfm[:, oc * S2:(oc + 1) * S2], osl, inc=(s_v, 1))
    pb.op("vector", "nop", inc=(s_v, 1))
    if cfg.get("dbg_d"):
        dbs2 = pb.sem("dbgd2_st")
        o = K.nc.dram_tensor("dbg_x2", [KC * 128, S2], BF16,
                             kind="ExternalOutput").ap()
        pb.wait("sync", s_v)
        for fc in range(KC):
            pb.dma("sync", o[fc * 128:(fc + 1) * 128, :],
                   x2fm[:, fc * S2:(fc + 1) * S2], dbs2)
        o = K.nc.dram_tensor("dbg_q2", [128, 4 * S2], BF16,
                             kind="ExternalOutput").ap()
        pb.dma("sync", o[:, :], q2[:, 0:4 * S2], dbs2)
        o = K.nc.dram_tensor("dbg_k2", [128, 4 * S2], BF16,
                             kind="ExternalOutput").ap()
        pb.dma("sync", o[:, :], k2[:, 0:4 * S2], dbs2)
        o = K.nc.dram_tensor("dbg_c2", [128, S2], BF16,
                             kind="ExternalOutput").ap()
        pb.dma("sync", o[:, :], c2[:, :], dbs2)
        o = K.nc.dram_tensor("dbg_s2t", [128, S2], BF16,
                             kind="ExternalOutput").ap()
        pb.dma("sync", o[:, :], s2t[:, :], dbs2)
        o = K.nc.dram_tensor("dbg_h1", [KC * 128, S2], BF16,
                             kind="ExternalOutput").ap()
        for fc in range(KC):
            pb.dma("sync", o[fc * 128:(fc + 1) * 128, :],
                   h2fm[:, fc * S2:(fc + 1) * S2], dbs2)
        pb.wait_dma("sync", dbs2)
    for half in range(S2 // W2H):
        for c in range(KC):
            pb.wait("scalar", s_v)
            pb.op("scalar", "activation", env["f32a"][:, 0:W2H],
                  x2fm[:, c * S2 + half * W2H:c * S2 + (half + 1) * W2H],
                  AF.Square, inc=(s_s, 1))
            pb.wait("tensor", s_s)
            pb.op("tensor", "matmul", ps_y[0:1, 0:W2H], onescol[:, 0:1],
                  env["f32a"][:, 0:W2H], start=(c == 0), stop=(c == KC - 1),
                  inc=(s_t, 1))
            pb.wait("scalar", s_t)
        pb.wait("vector", s_t)
        pb.op("vector", "tensor_scalar", rr_row[0:1, 0:W2H],
              ps_y[0:1, 0:W2H], 1.0 / D, 1e-6, ALU.mult, ALU.add)
        pb.drain("vector")
        pb.op("vector", "reciprocal", rr_row[0:1, 0:W2H], rr_row[0:1, 0:W2H],
              inc=(s_v, 1))
        pb.wait("scalar", s_v)
        pb.op("scalar", "activation", rr_row[0:1, 0:W2H], rr_row[0:1, 0:W2H],
              AF.Sqrt, inc=(lns, 1))
        pb.wait("gpsimd", lns)
        pb.op("gpsimd", "partition_broadcast", rrb[:, 0:W2H],
              rr_row[0:1, 0:W2H], channels=128, inc=(s_g, 1))
        pb.wait("vector", s_g)
        for c in range(KC):
            pb.op("vector", "tensor_tensor", env["f32a"][:, 0:W2H],
                  x2fm[:, c * S2 + half * W2H:c * S2 + (half + 1) * W2H],
                  rrb[:, 0:W2H], ALU.mult)
            pb.drain("vector")
            pb.op("vector", "tensor_scalar",
                  h2fm[:, c * S2 + half * W2H:c * S2 + (half + 1) * W2H],
                  env["f32a"][:, 0:W2H], ln22c[:, c:c + 1], None, ALU.mult)
            pb.drain("vector")
        pb.op("vector", "nop", inc=(s_v, 1))
    pb.wait("tensor", s_v)
    pb.pop_scope()
    K.pb.label = "dec2_ffn"
    emit_dec2_ffn_scatter(K, env)


def emit_dec2_ffn_scatter(K, env):
    cfg, pb = K.cfg, K.pb
    T, TL, D = cfg["T"], cfg["TL"], cfg["D"]
    FFN = cfg.get("FFN", 4 * D)
    S2 = T // 4
    SC = S2 // 128
    W2H = min(512, S2)
    NT = TL // 128
    KC = D // 128
    FC = FFN // 128
    eps = 1e-5
    Xd = K.x_dram
    dr_proc = K.d_proc
    ps_t, ps_a, ps_b = env["ps_t"], env["ps_a"], env["ps_b"]
    ps_y = env["ps_y"]
    s_ld, s_v, s_s, s_t = env["s_ld"], env["s_v"], env["s_s"], env["s_t"]
    s_g, lns, gld = env["s_g"], env["lns"], env["gld"]
    sq, sq2 = env["sq"], env["sq2"]
    stats, mean_c, rstd_c = env["stats"], env["mean_c"], env["rstd_c"]
    otok, myslot = env["otok"], env["myslot"]
    selfm = env["selfm"]
    x2fm = env["x2fm"]
    h2fm = env["h2fm"]
    scb = env["scb"]

    pb.push_scope()
    act2 = pb.sbuf("r3_act2", [128, FC * W2H], BF16)
    wgc2 = pb.sbuf("r3_wgc2", [128, KC * 128 * 2 * 2], BF16)
    wdc2 = pb.sbuf("r3_wdc2", [128, 2 * FFN], BF16)
    wstg2 = pb.sbuf("r3_wstg2", [128, KC * 128 * 2], F32)
    wstgd2 = pb.sbuf("r3_wstgd2", [128, FFN], F32)
    gtile2 = pb.sbuf("r3_gt2", [128, W2H], F32)
    NG3 = S2 // W2H
    CH = KC * 128 * 2
    s_wl = pb.sem("r3_wl")
    s_ca = pb.sem("r3_ca")
    guards = []
    ldtok = {}
    caval = {}
    dtok = {}
    dcav = {}
    dguards = []

    def issue_gu(gq):
        gi2, fc = divmod(gq, FC)
        if gi2 >= NG3:
            return
        if gq >= 1:
            pb.wait("sync", s_ca, caval[gq - 1])
        pb.dma("sync", wstg2[:, 0:KC * 128], K.inputs["d_wg2"][fc], s_wl)
        ldtok[gq] = pb.dma("sync", wstg2[:, KC * 128:2 * KC * 128],
                           K.inputs["d_wu2"][fc], s_wl)

    def issue_wd(gq):
        gi2, oc = divmod(gq, KC)
        if gi2 >= NG3:
            return
        if gq >= 1:
            pb.wait("sync", s_ca, dcav[gq - 1])
        dtok[gq] = pb.dma("sync", wstgd2[:, 0:FFN],
                          K.inputs["d_wd2"][oc], s_wl)

    issue_gu(0)
    for g2 in range(NG3):
        for fc in range(FC):
            gq = g2 * FC + fc
            slot = gq % 2
            pb.wait_dma("scalar", s_wl, ldtok[gq])
            if gq >= 2:
                gsem, gval = guards[gq - 2]
                pb.wait("scalar", gsem, gval)
            caval[gq] = pb.op(
                "scalar", "activation", wgc2[:, slot * CH:(slot + 1) * CH],
                wstg2[:, 0:CH], AF.Copy, inc=(s_ca, 1))
            pb.wait("tensor", s_ca, caval[gq])
            issue_gu(gq + 1)
            if fc == FC - 1:
                issue_wd(g2 * KC)
            for c in range(KC):
                stat = wgc2[:, slot * CH + c * 128:slot * CH + (c + 1) * 128]
                pb.op("tensor", "ldweights", stat)
                pb.op("tensor", "matmul", ps_a[:, 0:W2H], stat,
                      h2fm[:, c * S2 + g2 * W2H:c * S2 + (g2 + 1) * W2H],
                      start=(c == 0), stop=(c == KC - 1),
                      inc=(s_t, 1) if c == KC - 1 else None)
            for c in range(KC):
                stat = wgc2[:, slot * CH + KC * 128 + c * 128:
                            slot * CH + KC * 128 + (c + 1) * 128]
                pb.op("tensor", "ldweights", stat)
                pb.op("tensor", "matmul", ps_b[:, 0:W2H], stat,
                      h2fm[:, c * S2 + g2 * W2H:c * S2 + (g2 + 1) * W2H],
                      start=(c == 0), stop=(c == KC - 1),
                      inc=(s_t, 1) if c == KC - 1 else None)
            guards.append((s_t, s_t.n))
            pb.wait("scalar", s_t)
            pb.op("scalar", "activation", gtile2[:, 0:W2H], ps_a[:, 0:W2H],
                  AF.Silu, inc=(s_s, 1))
            pb.wait("vector", s_s)
            pb.op("vector", "tensor_tensor", act2[:, fc * W2H:(fc + 1) * W2H],
                  gtile2[:, 0:W2H], ps_b[:, 0:W2H], ALU.mult, inc=(s_v, 1))
            pb.wait("tensor", s_v)
        for oc in range(KC):
            gq = g2 * KC + oc
            slot = gq % 2
            pb.wait_dma("scalar", s_wl, dtok[gq])
            if gq >= 2:
                gsem, gval = dguards[gq - 2]
                pb.wait("scalar", gsem, gval)
            dcav[gq] = pb.op(
                "scalar", "activation", wdc2[:, slot * FFN:(slot + 1) * FFN],
                wstgd2[:, 0:FFN], AF.Copy, inc=(s_ca, 1))
            pb.wait("tensor", s_ca, dcav[gq])
            if oc < KC - 1:
                issue_wd(gq + 1)
            for fc in range(FC):
                stat = wdc2[:, slot * FFN + fc * 128:
                            slot * FFN + (fc + 1) * 128]
                pb.op("tensor", "ldweights", stat)
                pb.op("tensor", "matmul", ps_a[:, 0:W2H], stat,
                      act2[:, fc * W2H:(fc + 1) * W2H], start=(fc == 0),
                      stop=(fc == FC - 1),
                      inc=(s_t, 1) if fc == FC - 1 else None)
            dguards.append((s_t, s_t.n))
            pb.wait("vector", s_t)
            colr = slice(oc * S2 + g2 * W2H, oc * S2 + (g2 + 1) * W2H)
            pb.op("vector", "tensor_add", env["f32a"][:, 0:W2H],
                  ps_a[:, 0:W2H], x2fm[:, colr])
            pb.drain("vector")
            pb.op("vector", "tensor_sub", env["f32a"][:, 0:W2H],
                  env["f32a"][:, 0:W2H], selfm[:, colr])
            pb.drain("vector")
            pb.op("vector", "tensor_tensor", x2fm[:, colr],
                  env["f32a"][:, 0:W2H],
                  scb[:, g2 * W2H:(g2 + 1) * W2H], ALU.mult, inc=(s_v, 1))
            pb.wait("tensor", s_v)
    pb.pop_scope()

    pb.push_scope()
    slotbc = pb.sbuf("r4_slbc", [128, TL], F32)
    pmat = pb.sbuf("r4_pmat", [128, SC * TL], BF16)
    deltT = pb.sbuf("r4_dT", [128, SC * D], BF16)
    oadd = pb.sbuf("r4_oadd", [128, KC * TL], BF16)
    jcol = pb.sbuf("r4_jcol", [128, 1], F32)

    pb.raw("gpsimd", lambda g: None)
    pb.wait("gpsimd", s_v)
    pb.op("gpsimd", "partition_broadcast", slotbc[:, :], myslot[0:1, :],
          channels=128, inc=(s_g, 1))
    pb.wait("vector", s_g)
    for jc in range(SC):
        pb.wait("vector", s_g)
        pb.op("gpsimd", "iota", jcol[:, 0:1], pattern=[[1, 1]],
              base=jc * 128, channel_multiplier=1,
              allow_small_or_imprecise_dtypes=True, inc=(s_g, 1))
        pb.wait("vector", s_g)
        pb.op("vector", "tensor_scalar", pmat[:, jc * TL:(jc + 1) * TL],
              slotbc[:, :], jcol[:, 0:1], None, ALU.is_equal, inc=(s_v, 1))
        pb.wait("gpsimd", s_v)
    pb.wait("tensor", s_v)
    for jc in range(SC):
        for c in range(KC):
            slot = c % 4
            dps = ps_t[:, slot * 128:slot * 128 + 128]
            tile_transpose(K, x2fm[:, c * S2 + jc * 128:
                                   c * S2 + (jc + 1) * 128], dps,
                           inc=(s_t, 1))
            pb.wait("vector", s_t)
            pb.op("vector", "tensor_copy",
                  deltT[:, jc * D + c * 128:jc * D + (c + 1) * 128], dps,
                  inc=(s_v, 1))
            pb.wait("tensor", s_v)
    for c in range(KC):
        for half in range(TL // 512):
            ps = ps_a if half % 2 == 0 else ps_b
            for jc in range(SC):
                stat = deltT[:, jc * D + c * 128:jc * D + (c + 1) * 128]
                pb.op("tensor", "ldweights", stat)
                pb.op("tensor", "matmul", ps[:, 0:512], stat,
                      pmat[:, jc * TL + half * 512:jc * TL + (half + 1) * 512],
                      start=(jc == 0), stop=(jc == SC - 1),
                      inc=(s_t, 1) if jc == SC - 1 else None)
            pb.wait("vector", s_t)
            pb.op("vector", "tensor_copy",
                  oadd[:, c * TL + half * 512:c * TL + (half + 1) * 512],
                  ps[:, 0:512], inc=(s_v, 1))
            pb.wait("tensor", s_v)
    for t in range(NT):
        pb.wait("tensor", s_v)
        for c in range(KC):
            slot = c % 4
            dps = ps_t[:, 512 + slot * 128:512 + slot * 128 + 128]
            tile_transpose(K, oadd[:, c * TL + t * 128:c * TL + t * 128 + 128],
                           dps, inc=(s_t, 1))
            pb.wait("vector", s_t)
            pb.op("vector", "tensor_copy", otok[:, c * 128:(c + 1) * 128],
                  dps, inc=(s_v, 1))
            pb.wait("tensor", s_v)
        pb.wait_dma("sync", s_ld)
        lv = pb.dma("sync", sq[:, :], dr_proc[t * 128:(t + 1) * 128, :],
                    s_ld)
        pb.wait_dma("vector", s_ld, lv)
        pb.op("vector", "tensor_add", otok[:, :], otok[:, :], sq[:, :])
        pb.drain("vector")
        ln_stats(pb, otok[:, :], stats, mean_c, rstd_c, D, eps, sq)
        pb.op("vector", "tensor_scalar", otok[:, :], otok[:, :],
              mean_c[:, 0:1], rstd_c[:, 0:1], ALU.subtract, ALU.mult,
              inc=(s_v, 1))
        pb.wait("sync", s_v)
        lv = pb.dma("sync", sq[:, :], Xd[t * 128:(t + 1) * 128, :], s_ld)
        pb.wait_dma("vector", s_ld, lv)
        pb.op("vector", "tensor_add", sq[:, :], sq[:, :], otok[:, :],
              inc=(s_v, 1))
        pb.wait("sync", s_v)
        pb.dma("sync", Xd[t * 128:(t + 1) * 128, :], sq[:, :], s_ld)
    pb.wait_dma("sync", s_ld)
    pb.pop_scope()
    pb.barrier()


# version 06
# speedup vs baseline: 1.2100x; 1.0923x over previous
import math
import contextlib
import numpy as np

import concourse.bass as bass
import concourse.mybir as mybir
from concourse.bass_utils import run_bass_kernel_spmd
from concourse.library_overlay import lower_extended_insts
from concourse import library_config

F32 = mybir.dt.float32
BF16 = mybir.dt.bfloat16
U32 = mybir.dt.uint32
ALU = mybir.AluOpType
AX = mybir.AxisListType
AF = mybir.ActivationFunctionType

ENGINES = ("sync", "vector", "scalar", "tensor", "gpsimd")

LN2PI = 2.0 * math.pi
TWO_PI_HI = np.float32(6.28125)
TWO_PI_LO = np.float32(2.0 * math.pi - 6.28125)


def cdiv(a, b):
    return (a + b - 1) // b


class Sem:
    def __init__(self, handle, name):
        self.h = handle
        self.name = name
        self.n = 0
        self.ndma = 0
        self.issuer = None
        self.fenced = set()

    def bump(self, k=1):
        self.n += k
        return self.n


class PB:

    def __init__(self, nc, resolved=None):
        self.nc = nc
        self.label = "init"
        self.thunks = {e: [] for e in ENGINES}
        self._stack = contextlib.ExitStack()
        self.sems = {}
        self.resolved = resolved or {}
        self.dma_names = {}
        self.inst_dbg = {}

    def _add(self, eng, fn):
        fn._label = self.label
        self.thunks[eng].append(fn)

    def sem(self, name):
        if name not in self.sems:
            h = self._stack.enter_context(self.nc.semaphore(name))
            self.sems[name] = Sem(h, name)
        return self.sems[name]

    def push_scope(self):
        self._scopes = getattr(self, "_scopes", [])
        self._scopes.append(contextlib.ExitStack())

    def pop_scope(self):
        self._scopes.pop().close()

    def _cur(self):
        sc = getattr(self, "_scopes", None)
        return sc[-1] if sc else self._stack

    def sbuf(self, name, shape, dtype):
        return self._cur().enter_context(
            self.nc.sbuf_tensor(name, shape, dtype))

    def psum(self, name, shape, dtype):
        return self._cur().enter_context(
            self.nc.psum_tensor(name, shape, dtype))

    BPCHECK = ("tensor_tensor", "tensor_add", "tensor_sub", "tensor_mult",
               "tensor_tensor_scan", "scalar_tensor_tensor", "select")

    @staticmethod
    def _bp(ap):
        try:
            if not isinstance(ap, bass.AP) or str(ap.space) != \
                    "MemorySpace.SBUF":
                return None
        except Exception:
            return None
        try:
            return ap.base_partition()
        except Exception:
            try:
                return ap.offset // ap.ap[0][0]
            except Exception:
                return None

    def _bpcheck(self, eng, meth, args):
        import traceback
        if eng not in ("vector", "scalar"):
            return
        bps = [b for b in (self._bp(a) for a in args) if b is not None]
        bad = ""
        if meth in self.BPCHECK:
            ins = [b for b in (self._bp(a) for a in args[1:])
                   if b is not None]
            if len(set(ins)) > 1:
                bad = f"input bases differ {ins}"
        for b in bps:
            if b not in (0, 32, 64, 96):
                bad += f" bad base {b}"
        if bad:
            fr = traceback.extract_stack()[-4]
            print(f"BPVIOLATION {meth} {bad} at {fr.filename}:{fr.lineno}")

    def op(self, eng, meth, *args, inc=None, **kw):
        self._bpcheck(eng, meth, args)
        tgt = None
        if inc is not None:
            sem, k = inc
            tgt = sem.bump(k)
            h = sem.h

            def fn(e, meth=meth, args=args, kw=kw, h=h, k=k):
                inst = getattr(e, meth)(*args, **kw)
                inst.then_inc(h, k)
                self.inst_dbg[inst.ins.name] = fn._dbg
        else:
            def fn(e, meth=meth, args=args, kw=kw):
                inst = getattr(e, meth)(*args, **kw)
                try:
                    self.inst_dbg[inst.ins.name] = fn._dbg
                except AttributeError:
                    pass
        import traceback
        fn._dbg = (meth, traceback.extract_stack()[-3:-1])
        self._add(eng, fn)
        if inc is not None and eng in ("vector", "scalar", "gpsimd",
                                       "tensor"):
            self._add(eng, lambda e: e.drain())
        return tgt

    def raw(self, eng, fn):
        self._add(eng, fn)

    def dma(self, eng, out, in_, sem, **kw):
        idx = sem.ndma
        sem.ndma += 1
        sem.n += 16
        h = sem.h
        name = sem.name
        assert sem.issuer in (None, eng), (name, sem.issuer, eng)
        sem.issuer = eng

        def fn(e, out=out, in_=in_, kw=kw, h=h, name=name, idx=idx):
            inst = e.dma_start(out=out, in_=in_, **kw)
            inst.then_inc(h, 16)
            self.dma_names[(name, idx)] = inst.ins.name
        import traceback
        fn._dbg = ("dma", traceback.extract_stack()[-3:-1])
        self._add(eng, fn)
        return idx + 1

    def dma_dyn(self, eng, sem, build_fn):
        idx = sem.ndma
        sem.ndma += 1
        sem.n += 16
        h = sem.h
        name = sem.name

        assert sem.issuer in (None, eng)
        sem.issuer = eng

        def fn(e, h=h, name=name, idx=idx):
            out, in_ = build_fn(e)
            inst = e.dma_start(out=out, in_=in_,
                               allow_slow_non_contiguous=True)
            inst.then_inc(h, 16)
            self.dma_names[(name, idx)] = inst.ins.name
        import traceback
        fn._dbg = ("dma_dyn", traceback.extract_stack()[-3:-1])
        self._add(eng, fn)
        return idx + 1

    def wait_dma(self, eng, sem, token=None):
        t = sem.ndma if token is None else token
        name = sem.name
        h = sem.h
        guess = sem.n

        def fn(e, h=h, name=name, t=t, guess=guess):
            v = self.resolved.get((name, t), guess) if t > 0 else 0
            e.wait_ge(h, v)
        if t > 0 and t not in sem.fenced and sem.issuer is not None:
            sem.fenced.add(t)
            if sem.issuer != eng:
                self.thunks[sem.issuer].append(fn)
        self._add(eng, fn)

    def wait(self, eng, sem, val=None):
        v = sem.n if val is None else val
        h = sem.h
        self._add(eng, lambda e, h=h, v=v: e.wait_ge(h, v))

    def drain(self, eng):
        self._add(eng, lambda e: e.drain())

    def barrier(self):
        s = self.sem("bar")
        tgt = s.n + len(ENGINES)
        for e in ENGINES:
            s.bump(1)
            h = s.h
            self._add(e, lambda eng, h=h: eng.nop().then_inc(h, 1))
            self._add(e, lambda eng, h=h, v=tgt: eng.wait_ge(h, v))




def ln_stats(pb, x_tile, scratch, mean_col, rstd_col, D, eps, sq_scratch):
    s_sum = scratch[:, 0:1]
    s_sq = scratch[:, 1:2]
    s_var = scratch[:, 2:3]
    mean_col = mean_col[:, :]
    rstd_col = rstd_col[:, :]
    x_tile = x_tile[:, :]
    pb.op("vector", "reduce_sum", s_sum, x_tile, axis=AX.X)
    pb.op("vector", "scalar_tensor_tensor", sq_scratch[:, :], x_tile, 1.0,
          x_tile, ALU.bypass, ALU.mult, accum_out=s_sq)
    pb.drain("vector")
    pb.op("vector", "tensor_scalar", mean_col, s_sum, 1.0 / D, None, ALU.mult)
    pb.drain("vector")
    pb.op("vector", "scalar_tensor_tensor", s_var, mean_col, 1.0, mean_col,
          ALU.bypass, ALU.mult)
    pb.drain("vector")
    pb.op("vector", "tensor_scalar", s_var, s_var, -1.0, eps, ALU.mult, ALU.add)
    pb.drain("vector")
    pb.op("vector", "scalar_tensor_tensor", s_var, s_sq, 1.0 / D, s_var,
          ALU.mult, ALU.add)
    pb.drain("vector")
    lnv = pb.sem("ln_v")
    lns = pb.sem("ln_s")
    pb.op("vector", "reciprocal", rstd_col, s_var, inc=(lnv, 1))
    pb.wait("scalar", lnv)
    pb.op("scalar", "activation", rstd_col, rstd_col, AF.Sqrt, inc=(lns, 1))
    pb.wait("vector", lns)


def softplus(pb, in_, tmp, inc=None):
    pb.op("scalar", "activation", tmp, in_, AF.Abs)
    pb.drain("scalar")
    pb.op("scalar", "activation", tmp, tmp, AF.Exp, scale=-1.0)
    pb.drain("scalar")
    pb.op("scalar", "activation", tmp, tmp, AF.Ln, bias=1.0, inc=inc)


def softplus_finish(pb, out, in_, tmp, inc=None):
    pb.op("vector", "scalar_tensor_tensor", out, in_, 0.0, tmp, ALU.max,
          ALU.add, inc=inc)




def build_identity(pb, ident):
    pb.op("gpsimd", "memset", ident[:], 1.0)
    pb.drain("gpsimd")
    pb.op("gpsimd", "affine_select", ident[:], ident[:], pattern=[[1, 128]],
          base=0, channel_multiplier=-1, compare_op=ALU.is_equal, fill=0.0)
    pb.drain("gpsimd")


class Kern:

    def __init__(self, cfg, resolved=None):
        self.cfg = cfg
        nc = bass.Bass("TRN2", target_bir_lowering=False)
        nc.cache_partition_id()
        self.nc = nc
        self.pb = PB(nc, resolved)
        self.inputs = {}
        self.internal = {}

    def dram_in(self, name, shape, dtype=F32):
        if name in self.inputs:
            return self.inputs[name]
        t = self.nc.dram_tensor(name, list(shape), dtype, kind="ExternalInput")
        self.inputs[name] = t.ap()
        return self.inputs[name]

    def dram_tmp(self, name, shape, dtype):
        t = self.nc.dram_tensor(name, list(shape), dtype, kind="Internal")
        self.internal[name] = t.ap()
        return self.internal[name]

    def resolve_dma_totals(self):
        name_of = {v: k for k, v in self.pb.dma_names.items()}
        run = {}
        totals = {}
        for inst in self.nc.all_instructions():
            si = getattr(inst, "sync_info", None)
            upds = si.on_update if si is not None else []
            for u in upds:
                if u.ant_name and u.update_value:
                    run[u.ant_name] = run.get(u.ant_name, 0) + u.update_value
            key = name_of.get(getattr(inst, "name", None))
            if key is not None:
                totals[(key[0], key[1] + 1)] = run.get(key[0], 0)
        by_sem = {}
        for (s, i) in sorted(totals, key=lambda k: k[1]):
            by_sem.setdefault(s, []).append(i)
        for s, idxs in by_sem.items():
            mx = 0
            for i in idxs:
                mx = max(mx, totals[(s, i)])
                totals[(s, i)] = mx
        return totals

    def finish(self):
        nc, pb = self.nc, self.pb
        block = pb._stack.enter_context(nc.Block())
        for ename in ENGINES:
            thunks = pb.thunks[ename]
            section = getattr(block, ename)

            def make(thunks):
                def body(e):
                    cur = None
                    sid = None
                    for i, fn in enumerate(thunks):
                        lab = getattr(fn, "_label", None)
                        if lab != cur:
                            if cur is not None:
                                nc.leave_named_scope(cur, sid, False)
                            cur, sid = lab, None
                            if cur is not None:
                                sid, _ = nc.enter_named_scope(cur, False)
                        try:
                            fn(e)
                        except Exception:
                            import sys
                            print(f"THUNK FAIL eng={e.engine} idx={i} "
                                  f"info={getattr(fn, '_dbg', None)}",
                                  file=sys.stderr)
                            raise
                    if cur is not None:
                        nc.leave_named_scope(cur, sid, False)
                return body
            section(make(thunks))
        pb._stack.close()
        lower_extended_insts(nc)
        return nc



def emit_weight_load(K, dram_ap, sb_tile, rows, cols, sem=None):
    pb = K.pb
    gld = pb.sem("gld")
    v = None
    for r in range(rows // 128):
        v = pb.dma("gpsimd", sb_tile[:, r * cols:(r + 1) * cols],
                   dram_ap[r * 128:(r + 1) * 128, :], gld)
    return (gld, v)


def tile_transpose(K, src_bf16, dst_psum_bf16, inc=None):
    K.pb.op("tensor", "transpose", dst_psum_bf16, src_bf16, K.ident[:, :],
            inc=inc)


def emit_sincos(K, ang, cos_out, sin_out, rows, W, extra_bias=None):
    pb = K.pb
    CW = 1024
    s_v = pb.sem("rr_v")
    s_s = pb.sem("rr_s")
    for out, bias in ((sin_out, 0.0), (cos_out, math.pi / 2)):
        for w0 in range(0, W, CW):
            w1 = min(w0 + CW, W)
            ww = w1 - w0
            sc = K.rr_f32
            a = sc[0:rows, 0:ww]
            f = sc[0:rows, CW:CW + ww]
            cf = sc[0:rows, 2 * CW:2 * CW + ww]
            adj = sc[0:rows, 3 * CW:3 * CW + ww]
            ic = K.rr_i32[0:rows, 0:ww]
            pb.wait("vector", s_s)
            if extra_bias is not None:
                pb.op("vector", "tensor_scalar", a, ang[0:rows, w0:w1],
                      extra_bias, bias, ALU.add, ALU.add)
            else:
                pb.op("vector", "tensor_scalar", a, ang[0:rows, w0:w1],
                      bias, None, ALU.add)
            pb.drain("vector")
            pb.op("vector", "tensor_scalar", f, a, 1.0 / LN2PI, None, ALU.mult)
            pb.drain("vector")
            pb.op("vector", "tensor_copy", ic, f)
            pb.drain("vector")
            pb.op("vector", "tensor_copy", cf, ic)
            pb.drain("vector")
            pb.op("vector", "tensor_sub", f, f, cf)
            pb.drain("vector")
            pb.op("vector", "tensor_scalar", adj, f, 0.5, None, ALU.is_ge)
            pb.drain("vector")
            pb.op("vector", "tensor_add", cf, cf, adj)
            pb.drain("vector")
            pb.op("vector", "tensor_scalar", adj, f, -0.5, None, ALU.is_le)
            pb.drain("vector")
            pb.op("vector", "tensor_sub", cf, cf, adj)
            pb.drain("vector")
            pb.op("vector", "scalar_tensor_tensor", a, cf,
                  -float(TWO_PI_HI), a, ALU.mult, ALU.add)
            pb.drain("vector")
            pb.op("vector", "scalar_tensor_tensor", a, cf,
                  -float(TWO_PI_LO), a, ALU.mult, ALU.add, inc=(s_v, 1))
            pb.wait("scalar", s_v)
            pb.op("scalar", "activation", out[0:rows, w0:w1], a, AF.Sin,
                  inc=(s_s, 1))
    pb.wait("vector", s_s)



def emit_l0_attn(K):
    cfg, pb = K.cfg, K.pb
    T, TL, D, HL, HD, G = (cfg["T"], cfg["TL"], cfg["D"], cfg["HL"],
                           cfg["HD"], cfg["G"])
    NT = TL // 128
    KC = D // 128
    MY = HL * HD
    MYC = MY // 128
    NG = T // G
    GT = G // 128
    VW = HL * (HD + 1)
    eps = 1e-5
    Xd = K.x_dram
    K.dram_in("wqkvT", [D, 3 * MY], F32)
    K.dram_in("wprojT", [MY, D], F32)
    K.dram_in("deltaT", [HD, HL], F32)

    xn_tok = pb.sbuf("l0_xn_tok", [128, D], BF16)
    sq = pb.sbuf("l0_sq", [128, D], F32)
    stats = pb.sbuf("l0_stats", [128, 8], F32)
    mean_c = pb.sbuf("l0_mean", [128, 1], F32)
    rstd_c = pb.sbuf("l0_rstd", [128, 1], F32)
    xfm = pb.sbuf("l0_xfm", [128, KC * 128], BF16)
    wqkv = pb.sbuf("l0_wqkv", [128, KC * 3 * MY], BF16)
    wproj = pb.sbuf("l0_wproj", [128, MYC * D], BF16)
    cpos = pb.sbuf("l0_cpos", [128, G], BF16)
    spos = pb.sbuf("l0_spos", [128, G], BF16)
    ckh = pb.sbuf("l0_ckh", [128, T], BF16)
    skh = pb.sbuf("l0_skh", [128, T], BF16)
    invf = pb.sbuf("l0_invf", [64, 1], F32)
    muq = pb.sbuf("l0_muq", [128, MYC * G], BF16)
    qhat = pb.sbuf("l0_qhat", [128, 2 * G], BF16)
    ptile = pb.sbuf("l0_p", [128, 2 * G], BF16)
    tri = pb.sbuf("l0_tri", [128, GT * G], BF16)
    ygrp = pb.sbuf("l0_ygrp", [128, MYC * G], BF16)
    denb = pb.sbuf("l0_denb", [HD, G], F32)
    den1 = pb.sbuf("l0_den1", [1, G], F32)
    ostage = pb.sbuf("l0_ostage", [128, (G // 128) * D], BF16)
    obf = pb.sbuf("l0_obf", [128, D], BF16)
    sptmp = pb.sbuf("l0_sptmp", [128, G], F32)
    xn_grp = pb.sbuf("l0_xngrp", [128, KC * G], BF16)
    delta_sb = pb.sbuf("l0_delta", [64, HL], F32)
    otok = pb.sbuf("l0_otok", [128, D], F32)

    ps_t = pb.psum("l0_pst", [128, 1024], BF16)
    ps_a = pb.psum("l0_psa", [128, 512], F32)
    ps_b = pb.psum("l0_psb", [128, 512], F32)
    ps_s0 = pb.psum("l0_pss0", [128, 512], F32)
    ps_s1 = pb.psum("l0_pss1", [128, 512], F32)
    ps_y = pb.psum("l0_psy", [128, 512], F32)
    ps_y2 = pb.psum("l0_psy2", [128, 512], F32)

    dr_xn_in = K.dram_tmp("l0_xn_in", [KC * 128, TL], BF16)
    dr_xn_all = K.dram_tmp("l0_xn_all", [2, KC * 128, TL], BF16)
    dr_oa_in = K.dram_tmp("l0_oa_in", [T, D], BF16)
    dr_oa_my = K.dram_tmp("l0_oa_my", [TL, D], BF16)
    dr_ktab = K.dram_tmp("l0_ktab", [HL, 2, 64, T], BF16)
    dr_ptab = K.dram_tmp("l0_ptab", [2, 64, T], BF16)

    s_ld = pb.sem("l0_ld")
    s_v = pb.sem("l0_v")
    s_s = pb.sem("l0_s")
    s_t = pb.sem("l0_t")
    s_g = pb.sem("l0_g")
    s_cc = pb.sem("l0_cc")

    pb.push_scope()
    ang = pb.sbuf("l0_ang", [64, T], F32)
    pb.op("gpsimd", "iota", invf[:, 0:1], pattern=[[1, 1]], base=0,
          channel_multiplier=1, allow_small_or_imprecise_dtypes=True,
          inc=(s_g, 1))
    pb.wait("scalar", s_g)
    pb.op("scalar", "activation", invf[:, 0:1], invf[:, 0:1], AF.Exp,
          scale=-math.log(10000.0) / HD, inc=(s_s, 1))
    pb.op("gpsimd", "iota", ang[:, :], pattern=[[1, T]], base=0,
          channel_multiplier=0, allow_small_or_imprecise_dtypes=True,
          inc=(s_g, 1))
    pb.wait("vector", s_s)
    pb.wait("vector", s_g)
    pb.op("vector", "tensor_scalar", ang[:, :], ang[:, :], invf[:, 0:1], None,
          ALU.mult)
    pb.drain("vector")
    emit_sincos(K, ang, ckh, skh, 64, T)
    pb.wait("sync", pb.sem("rr_s"))
    pb.dma("sync", dr_ptab[0], ckh[0:64, :], s_ld)
    pb.dma("sync", dr_ptab[1], skh[0:64, :], s_ld)
    pb.wait_dma("scalar", s_ld)
    pb.wait_dma("vector", s_ld)
    lv = pb.dma("sync", delta_sb[:, :], K.inputs["deltaT"], s_ld)
    pb.wait_dma("vector", s_ld, lv)
    pb.op("vector", "tensor_scalar", delta_sb[:, :], delta_sb[:, :],
          -LN2PI, 0.0, ALU.max, ALU.min)
    pb.drain("vector")
    pb.op("vector", "nop", inc=(s_v, 1))
    for h in range(HL):
        emit_sincos(K, ang, ckh, skh, 64, T,
                    extra_bias=delta_sb[:, h:h + 1])
        pb.wait("sync", pb.sem("rr_s"))
        for which, buf in ((0, ckh), (1, skh)):
            pb.dma("sync", dr_ktab[h, which], buf[0:64, :], s_ld)
        pb.wait_dma("scalar", s_ld)
        pb.wait_dma("vector", s_ld)
    for j in range(GT):
        sl = tri[:, j * G:(j + 1) * G]
        pb.op("gpsimd", "memset", sl, 1.0)
        pb.drain("gpsimd")
        pb.op("gpsimd", "affine_select", sl, sl, pattern=[[1, G]],
              base=-128 * j, channel_multiplier=-1, compare_op=ALU.is_ge,
              fill=0.0)
        pb.drain("gpsimd")
    pb.op("gpsimd", "nop", inc=(s_g, 1))
    pb.pop_scope()
    muk = pb.sbuf("l0_muk", [128, MYC * T], BF16)
    vtok = pb.sbuf("l0_vtok", [128, (T // 128) * VW], BF16)
    khat = pb.sbuf("l0_khat", [128, 2 * T], BF16)
    emit_weight_load(K, K.inputs["wqkvT"], wqkv, D, 3 * MY)
    gld, wv = emit_weight_load(K, K.inputs["wprojT"], wproj, MY, D)

    for t in range(NT):
        lv = pb.dma("sync", sq[:, :], Xd[t * 128:(t + 1) * 128, :], s_ld)
        pb.wait_dma("vector", s_ld, lv)
        ln_stats(pb, sq[:, :], stats, mean_c, rstd_c, D, eps, xn_tok)
        pb.op("vector", "tensor_scalar", xn_tok[:, :], sq[:, :],
              mean_c[:, 0:1], rstd_c[:, 0:1], ALU.subtract, ALU.mult,
              inc=(s_v, 1))
        pb.wait("tensor", s_v)
        for c in range(KC):
            slot = c % 4
            dst = ps_t[:, slot * 128:slot * 128 + 128]
            tile_transpose(K, xn_tok[:, c * 128:(c + 1) * 128], dst,
                           inc=(s_t, 1))
            pb.wait("vector", s_t)
            pb.op("vector", "tensor_copy", xfm[:, c * 128:(c + 1) * 128],
                  dst, inc=(s_v, 1))
            pb.wait("tensor", s_v)
        pb.wait("sync", s_v)
        for c in range(KC):
            pb.dma("sync",
                   dr_xn_in[c * 128:(c + 1) * 128, t * 128:(t + 1) * 128],
                   xfm[:, c * 128:(c + 1) * 128], s_ld)
    pb.raw("gpsimd", lambda g: None)
    pb.wait_dma("gpsimd", s_ld)
    pb.op("gpsimd", "collective_compute", "AllGather", ALU.bypass,
          replica_groups=cfg["groups"], ins=[dr_xn_in], outs=[dr_xn_all],
          inc=(s_cc, 1))

    pb.op("vector", "memset", vtok[:, :], 0.0)
    pb.drain("vector")
    last = (T // 128 - 1, HL - 1)
    for tc in range(T // 128):
        for h in range(HL):
            base = tc * VW + h * (HD + 1) + HD
            pb.op("vector", "memset", vtok[:, base:base + 1], 1.0,
                  inc=(s_v, 1) if (tc, h) == last else None)
    pb.wait("sync", s_cc)
    pb.wait_dma("tensor", gld, wv)
    pb.wait("tensor", s_v)
    for kg in range(NG):
        slot = (kg * G) // TL
        col0 = (kg * G) % TL
        pb.wait("sync", s_t)
        lv = None
        for c in range(KC):
            lv = pb.dma("sync", xn_grp[:, c * G:(c + 1) * G],
                        dr_xn_all[slot, c * 128:(c + 1) * 128,
                                  col0:col0 + G], s_ld)
        pb.wait_dma("tensor", s_ld, lv)
        for which, wofs in (("k", MY), ("v", 2 * MY)):
            for mc in range(MYC):
                ps = ps_a if (mc % 2 == 0) else ps_b
                for c in range(KC):
                    stat = wqkv[:, c * 3 * MY + wofs + mc * 128:
                                c * 3 * MY + wofs + (mc + 1) * 128]
                    pb.op("tensor", "ldweights", stat)
                    pb.op("tensor", "matmul", ps[:, 0:G], stat,
                          xn_grp[:, c * G:(c + 1) * G], start=(c == 0),
                          stop=(c == KC - 1),
                          inc=(s_t, 1) if c == KC - 1 else None)
                if which == "k":
                    pb.wait("scalar", s_t)
                    softplus(pb, ps[:, 0:G], sptmp[:, 0:G], inc=(s_s, 1))
                    pb.wait("vector", s_s)
                    softplus_finish(
                        pb, muk[:, mc * T + kg * G: mc * T + (kg + 1) * G],
                        ps[:, 0:G], sptmp[:, 0:G], inc=(s_v, 1))
                    pb.wait("tensor", s_v)
                else:
                    pb.wait("vector", s_t)
                    pb.op("vector", "tensor_copy", xn_tok[:, 0:G], ps[:, 0:G],
                          inc=(s_v, 1))
                    pb.wait("tensor", s_v)
                    for q in range(GT):
                        slot2 = q % 4
                        dps = ps_t[:, 512 + slot2 * 128:
                                   512 + slot2 * 128 + 128]
                        tile_transpose(K, xn_tok[:, q * 128:(q + 1) * 128],
                                       dps, inc=(s_t, 1))
                        pb.wait("vector", s_t)
                        tc = kg * GT + q
                        for hh in range(128 // HD):
                            h = mc * (128 // HD) + hh
                            pb.op("vector", "tensor_copy",
                                  vtok[:, tc * VW + h * (HD + 1):
                                       tc * VW + h * (HD + 1) + HD],
                                  dps[:, hh * HD:(hh + 1) * HD],
                                  inc=(s_v, 1))
                        pb.wait("tensor", s_v)
    pb.barrier()

    K.att_khg = {0: None, 1: None}
    K.att_pe_gate = {0: None, 1: None}
    K.att_exp_gate = {0: None, 1: None}
    K.att_ygate = {0: None, 1: None}
    for g in range(NG):
        nkt = (g + 1) * GT
        slot = (g * G) // TL
        col0 = (g * G) % TL
        pb.wait("sync", s_t)
        lv = None
        for c in range(KC):
            lv = pb.dma("sync", xn_grp[:, c * G:(c + 1) * G],
                        dr_xn_all[slot, c * 128:(c + 1) * 128,
                                  col0:col0 + G], s_ld)
        pb.wait_dma("tensor", s_ld, lv)
        pb.wait("sync", s_v)
        for r0d in (0, 64):
            pb.dma("sync", cpos[r0d:r0d + 64, :],
                   dr_ptab[0, :, g * G:(g + 1) * G], s_ld)
            pb.dma("sync", spos[r0d:r0d + 64, :],
                   dr_ptab[1, :, g * G:(g + 1) * G], s_ld)
        pb.wait_dma("vector", s_ld)
        for mc in range(MYC):
            ps = ps_a if (mc % 2 == 0) else ps_b
            for c in range(KC):
                stat = wqkv[:, c * 3 * MY + mc * 128:
                            c * 3 * MY + (mc + 1) * 128]
                pb.op("tensor", "ldweights", stat)
                pb.op("tensor", "matmul", ps[:, 0:G], stat,
                      xn_grp[:, c * G:(c + 1) * G], start=(c == 0),
                      stop=(c == KC - 1),
                      inc=(s_t, 1) if c == KC - 1 else None)
            pb.wait("scalar", s_t)
            softplus(pb, ps[:, 0:G], sptmp[:, 0:G], inc=(s_s, 1))
            pb.wait("vector", s_s)
            softplus_finish(pb, muq[:, mc * G:(mc + 1) * G], ps[:, 0:G],
                            sptmp[:, 0:G], inc=(s_v, 1))
            pb.wait("tensor", s_v)
        for h in range(HL):
            ctx = nkt * 128
            mc0, r0 = divmod(h * HD, 128)
            hb = h % 2
            yslot = h % 2
            psy = ps_y if yslot == 0 else ps_y2
            if K.att_khg[hb] is not None:
                pb.wait("sync", s_v, K.att_khg[hb])
            lv = pb.dma("sync", ckh[r0:r0 + HD, 0:ctx],
                        dr_ktab[h, 0, :, 0:ctx], s_ld)
            lv = pb.dma("sync", skh[r0:r0 + HD, 0:ctx],
                        dr_ktab[h, 1, :, 0:ctx], s_ld)
            pb.wait_dma("vector", s_ld, lv)
            if K.att_pe_gate[hb] is not None:
                pb.wait("vector", s_t, K.att_pe_gate[hb])
            mkv = muk[r0:r0 + HD, mc0 * T:mc0 * T + ctx]
            kh0 = hb * T
            qh0 = hb * G
            pb.op("vector", "tensor_tensor", khat[0:HD, kh0:kh0 + ctx], mkv,
                  ckh[r0:r0 + HD, 0:ctx], ALU.mult)
            pb.op("vector", "tensor_tensor", khat[HD:128, kh0:kh0 + ctx],
                  mkv, skh[r0:r0 + HD, 0:ctx], ALU.mult)
            mqv = muq[r0:r0 + HD, mc0 * G:(mc0 + 1) * G]
            pb.op("vector", "tensor_tensor", qhat[0:HD, qh0:qh0 + G], mqv,
                  cpos[r0:r0 + HD, :], ALU.mult)
            khg = pb.op("vector", "tensor_tensor", qhat[HD:128, qh0:qh0 + G],
                        mqv, spos[r0:r0 + HD, :], ALU.mult, inc=(s_v, 1))
            K.att_khg[hb] = khg
            scd = {}
            expd = {}
            mskd = {}
            pvd = {}

            def emit_scores(kt):
                slot = kt % 2
                ps_sc = ps_s0 if slot == 0 else ps_s1
                if kt == 0:
                    vgate = khg
                    if K.att_ygate[yslot] is not None:
                        vgate = max(vgate, K.att_ygate[yslot])
                    pb.wait("tensor", s_v, vgate)
                if K.att_exp_gate[slot] is not None:
                    pb.wait("tensor", s_s, K.att_exp_gate[slot])
                pb.op("tensor", "ldweights",
                      khat[:, kh0 + kt * 128:kh0 + (kt + 1) * 128])
                scd[kt] = pb.op("tensor", "matmul", ps_sc[:, 0:G],
                                khat[:, kh0 + kt * 128:kh0 + (kt + 1) * 128],
                                qhat[:, qh0:qh0 + G], start=True, stop=True,
                                inc=(s_t, 1))

            def emit_exp_mask(kt):
                slot = kt % 2
                ps_sc = ps_s0 if slot == 0 else ps_s1
                pb.wait("scalar", s_t, scd[kt])
                expd[kt] = pb.op("scalar", "activation",
                                 ptile[:, slot * G:(slot + 1) * G],
                                 ps_sc[:, 0:G], AF.Exp,
                                 scale=1.0 / math.sqrt(HD), inc=(s_s, 1))
                K.att_exp_gate[slot] = expd[kt]
                diag = kt - g * GT
                if diag >= 0:
                    pb.wait("vector", s_s, expd[kt])
                    mskd[kt] = pb.op("vector", "tensor_tensor",
                                     ptile[:, slot * G:(slot + 1) * G],
                                     ptile[:, slot * G:(slot + 1) * G],
                                     tri[:, diag * G:(diag + 1) * G],
                                     ALU.mult, inc=(s_v, 1))
                else:
                    mskd[kt] = None

            def emit_pv(kt):
                slot = kt % 2
                if mskd[kt] is not None:
                    pb.wait("tensor", s_v, mskd[kt])
                else:
                    pb.wait("tensor", s_s, expd[kt])
                pb.op("tensor", "ldweights",
                      vtok[:, kt * VW + h * (HD + 1):
                           kt * VW + (h + 1) * (HD + 1)])
                pvd[kt] = pb.op("tensor", "matmul", psy[0:HD + 1, 0:G],
                                vtok[:, kt * VW + h * (HD + 1):
                                     kt * VW + (h + 1) * (HD + 1)],
                                ptile[:, slot * G:(slot + 1) * G],
                                start=(kt == 0), stop=(kt == nkt - 1),
                                inc=(s_t, 1))

            for kt in range(nkt):
                emit_scores(kt)
                emit_exp_mask(kt)
                if kt >= 1:
                    emit_pv(kt - 1)
            emit_pv(nkt - 1)
            K.att_pe_gate[hb] = scd[nkt - 1]
            pb.wait("vector", s_t, pvd[nkt - 1])
            rv = pb.op("vector", "reciprocal", den1[:, :],
                       psy[HD:HD + 1, 0:G], inc=(s_v, 1))
            pb.wait("gpsimd", s_v, rv)
            gv = pb.op("gpsimd", "partition_broadcast", denb[:, :],
                       den1[:, :], channels=HD, inc=(s_g, 1))
            pb.wait("vector", s_g, gv)
            K.att_ygate[yslot] = pb.op(
                "vector", "tensor_tensor",
                ygrp[r0:r0 + HD, mc0 * G:(mc0 + 1) * G], psy[0:HD, 0:G],
                denb[:, :], ALU.mult, inc=(s_v, 1))
        pb.wait("tensor", s_v)
        pb.wait_dma("vector", s_ld)
        DH = D // 2
        for j in range(G // 128):
            for c in range(MYC):
                stat = ygrp[:, c * G + j * 128:c * G + j * 128 + 128]
                pb.op("tensor", "ldweights", stat)
                pb.op("tensor", "matmul", ps_a[:, 0:DH], stat,
                      wproj[:, c * D:c * D + DH], start=(c == 0),
                      stop=(c == MYC - 1),
                      inc=(s_t, 1) if c == MYC - 1 else None)
                pb.op("tensor", "matmul", ps_b[:, 0:DH], stat,
                      wproj[:, c * D + DH:(c + 1) * D], start=(c == 0),
                      stop=(c == MYC - 1),
                      inc=(s_t, 1) if c == MYC - 1 else None)
            pb.wait("vector", s_t)
            pb.op("vector", "tensor_copy", ostage[:, j * D:j * D + DH],
                  ps_a[:, 0:DH], inc=(s_v, 1))
            pb.op("vector", "tensor_copy", ostage[:, j * D + DH:(j + 1) * D],
                  ps_b[:, 0:DH], inc=(s_v, 1))
            pb.wait("tensor", s_v)
        pb.wait("sync", s_v)
        for j in range(G // 128):
            pb.dma("sync",
                   dr_oa_in[g * G + j * 128:g * G + (j + 1) * 128, :],
                   ostage[:, j * D:(j + 1) * D], s_ld)
    if cfg.get("dbg_l0"):
        dbs = pb.sem("dbg_st")
        for nm, buf, shp in (("dbg_muk", muk, [128, MYC * T]),
                             ("dbg_vtok", vtok, [128, (T // 128) * VW]),
                             ("dbg_ygrp", ygrp, [128, MYC * G])):
            o = K.nc.dram_tensor(nm, shp, BF16, kind="ExternalOutput").ap()
            pb.dma("sync", o[:, :], buf[:, :], dbs)
        o = K.nc.dram_tensor("dbg_oain", [T, D], BF16,
                             kind="ExternalOutput").ap()
        pb.wait_dma("sync", s_ld)
        pb.dma("sync", o[:, :], dr_oa_in[:, :], dbs)
        o = K.nc.dram_tensor("dbg_xnall", [2, KC * 128, TL], BF16,
                             kind="ExternalOutput").ap()
        pb.dma("sync", o, dr_xn_all, dbs)
        pb.wait_dma("sync", dbs)
    pb.wait_dma("gpsimd", s_ld)
    pb.op("gpsimd", "collective_compute", "ReduceScatter", ALU.add,
          replica_groups=cfg["groups"], ins=[dr_oa_in], outs=[dr_oa_my],
          inc=(s_cc, 1))
    pb.wait("sync", s_cc)
    for t in range(NT):
        pb.wait("sync", s_v)
        pb.wait_dma("sync", s_ld)
        lvo = pb.dma("sync", obf[:, :], dr_oa_my[t * 128:(t + 1) * 128, :],
                     s_ld)
        lvx = pb.dma("sync", sq[:, :], Xd[t * 128:(t + 1) * 128, :], s_ld)
        pb.wait_dma("vector", s_ld, lvx)
        pb.op("vector", "tensor_copy", otok[:, :], obf[:, :],
              inc=(s_v, 1))
        pb.drain("vector")
        ln_stats(pb, otok[:, :], stats, mean_c, rstd_c, D, eps, xn_tok)
        pb.op("vector", "tensor_scalar", otok[:, :], otok[:, :],
              mean_c[:, 0:1], rstd_c[:, 0:1], ALU.subtract, ALU.mult)
        pb.drain("vector")
        pb.op("vector", "tensor_add", sq[:, :], sq[:, :], otok[:, :],
              inc=(s_v, 1))
        pb.wait("sync", s_v)
        pb.dma("sync", Xd[t * 128:(t + 1) * 128, :], sq[:, :], s_ld)
    pb.wait_dma("sync", s_ld)
    pb.barrier()



def setup_common(K):
    pb = K.pb
    cfg = K.cfg

    K.ident = pb.sbuf("c_ident", [128, 128], BF16)
    K.rr_f32 = pb.sbuf("c_rrf", [128, 4 * 1024], F32)
    K.rr_i32 = pb.sbuf("c_rri", [128, 1024], mybir.dt.int32)
    K.x_dram = K.dram_tmp("x_work", [cfg["TL"], cfg["D"]], F32)
    x_in = K.dram_in("x", [cfg["TL"], cfg["D"]], F32)
    s_init = pb.sem("c_init")
    s_initd = pb.sem("c_initd")
    pb.raw("gpsimd", lambda g: g.load_library(library_config.proxy))
    build_identity(pb, K.ident)
    pb.op("gpsimd", "nop", inc=(s_init, 1))
    pb.dma("sync", K.x_dram[:, :], x_in[:, :], s_initd)
    for e in ENGINES:
        pb.wait(e, s_init)
        pb.wait_dma(e, s_initd)


def finalize_out(K):
    pb = K.pb
    cfg = K.cfg
    out = K.nc.dram_tensor("out", [cfg["TL"], cfg["D"]], F32,
                           kind="ExternalOutput").ap()
    s_out = pb.sem("c_out")
    pb.barrier()
    pb.dma("sync", out[:, :], K.x_dram[:, :], s_out)
    pb.wait_dma("sync", s_out)


def _build_once(cfg, resolved):
    K = Kern(cfg, resolved)
    setup_common(K)
    for blk in cfg["blocks"]:
        K.pb.label = blk
        K.pb.push_scope()
        if blk == "l0_attn":
            emit_l0_attn(K)
        elif blk == "l0_ffn":
            emit_l0_ffn(K)
        elif blk == "mamba":
            emit_mamba(K)
        elif blk == "dec":
            emit_l1_ffn(K)
        else:
            raise ValueError(blk)
        K.pb.pop_scope()
    finalize_out(K)
    return K


def build_module_program(cfg):
    K1 = _build_once(cfg, None)
    K1.finish()
    resolved = K1.resolve_dma_totals()
    K2 = _build_once(cfg, resolved)
    nc = K2.finish()
    return K2, nc


FULL_CFG = dict(
    T=4096, TL=2048, D=1024, HL=8, HD=64, G=512,
    groups=[[0, 1], [2, 3], [4, 5], [6, 7]],
    blocks=["l0_attn", "l0_ffn", "mamba", "dec"],
)


def host_shards(cfg, inputs, ncores=8):
    T, TL, D, HL, HD = cfg["T"], cfg["TL"], cfg["D"], cfg["HL"], cfg["HD"]
    MY = HL * HD
    maps = []
    C = np.ascontiguousarray
    for c in range(ncores):
        b, m = c // 2, c % 2
        im = {}
        im["x"] = C(inputs["L0_attn_x"][b, m * TL:(m + 1) * TL, :])
        if "dec" in cfg["blocks"]:
            MY = HL * HD
            KCd = D // 128
            sl = slice(m * MY, (m + 1) * MY)
            for pre, wq, wk, wv, wo in (("1", "L1_ffn_wq1", "L1_ffn_wk1",
                                         "L1_ffn_wv1", "L1_ffn_wo1"),
                                        ("2", "L1_ffn_wq2", "L1_ffn_wk2",
                                         "L1_ffn_wv2", "L1_ffn_wo2")):
                im["d_w" + pre] = C(np.concatenate(
                    [inputs[wq][:, sl], inputs[wk][:, sl],
                     inputs[wv][:, sl]], axis=1))
                im["d_wo" + pre] = C(inputs[wo][sl, :])
            for nm in ("wg1", "wu1", "wg2", "wu2"):
                w = inputs["L1_ffn_" + nm]
                Dd, FFNd = w.shape
                KCd2, FCd = Dd // 128, FFNd // 128
                im["d_" + nm] = C(np.ascontiguousarray(
                    w.reshape(KCd2, 128, FCd, 128).transpose(2, 1, 0, 3)
                    .reshape(FCd, 128, Dd)))
            for nm in ("wd1", "wd2"):
                w = inputs["L1_ffn_" + nm]
                FFNd, Dd = w.shape
                FCd, KCd2 = FFNd // 128, Dd // 128
                im["d_" + nm] = C(np.ascontiguousarray(
                    w.reshape(FCd, 128, KCd2, 128).transpose(2, 1, 0, 3)
                    .reshape(KCd2, 128, FFNd)))
            im["d_ln11"] = C(inputs["L1_ffn_ln1_1"].reshape(1, -1))
            im["d_ln21"] = C(inputs["L1_ffn_ln2_1"].reshape(1, -1))
            im["d_ln12c"] = C(inputs["L1_ffn_ln1_2"].reshape(KCd, 128).T)
            im["d_ln22c"] = C(inputs["L1_ffn_ln2_2"].reshape(KCd, 128).T)
            im["d_pnw"] = C(inputs["L1_ffn_prior_norm_w"].reshape(1, -1))
            im["d_pw"] = C(inputs["L1_ffn_prior_w"])
            im["d_tpos"] = C(np.arange(m * TL, (m + 1) * TL,
                                       dtype=np.float32).reshape(1, -1))
        if "mamba" in cfg["blocks"]:
            DI = cfg.get("DI", 2 * D)
            DIL = DI // 2
            sl = slice(m * DIL, (m + 1) * DIL)
            im["m_Winu"] = C(inputs["L1_attn_W_in"][:, m * DIL:(m + 1) * DIL])
            im["m_Winz"] = C(inputs["L1_attn_W_in"][:, DI + m * DIL:DI + (m + 1) * DIL])
            im["m_convw"] = C(inputs["L1_attn_conv_w"][sl])
            im["m_convb"] = C(inputs["L1_attn_conv_b"][sl].reshape(-1, 1))
            im["m_Wx"] = C(inputs["L1_attn_W_x"][sl])
            im["m_Wdt"] = C(inputs["L1_attn_W_dt"][:, sl])
            im["m_bdt"] = C(inputs["L1_attn_b_dt"][sl].reshape(-1, 1))
            im["m_Alog"] = C(inputs["L1_attn_A_log"][sl])
            im["m_Dskip"] = C(inputs["L1_attn_Dskip"][sl].reshape(-1, 1))
            im["m_Wout"] = C(inputs["L1_attn_W_out"][sl])
            im["m_lng"] = C(inputs["L1_attn_ln_g"].reshape(1, -1))
            im["m_lnb"] = C(inputs["L1_attn_ln_b"].reshape(1, -1))
        if "l0_ffn" in cfg["blocks"]:
            im["rhs"] = C(inputs["L0_ffn_rhs"])
            im["rhsT"] = C(inputs["L0_ffn_rhs"].T)
        if "l0_attn" in cfg["blocks"]:
            wa = inputs["L0_attn_w_attn"]
            D0 = wa.shape[1]
            qkv = np.concatenate([wa[m * MY:(m + 1) * MY, :],
                                  wa[D0 + m * MY:D0 + (m + 1) * MY, :],
                                  wa[2 * D0 + m * MY:2 * D0 + (m + 1) * MY, :]],
                                 axis=0)
            im["wqkvT"] = C(qkv.T)
            im["wprojT"] = C(inputs["L0_attn_w_proj"][:, m * MY:(m + 1) * MY].T)
            im["deltaT"] = C(inputs["L0_attn_delta"][m * HL:(m + 1) * HL, :].T)
        maps.append(im)
    return maps


def kernel(**inputs):
    cfg = FULL_CFG
    K, nc = build_module_program(cfg)
    in_maps = host_shards(cfg, inputs, 8)
    res = run_bass_kernel_spmd(nc, in_maps, core_ids=list(range(8)))
    TL, D = cfg["TL"], cfg["D"]
    out = np.zeros((4, cfg["T"], D), dtype=np.float32)
    for c in range(8):
        b, m = c // 2, c % 2
        out[b, m * TL:(m + 1) * TL, :] = res.results[c]["out"]
    return out



def emit_round_halfaway(pb, K, x_ap, out_ap, itile, ftile, a1, a2):
    itile = itile[:, :]
    ftile = ftile[:, :]
    a1 = a1[:, :]
    a2 = a2[:, :]
    pb.op("vector", "tensor_copy", itile, x_ap)
    pb.drain("vector")
    pb.op("vector", "tensor_copy", ftile, itile)
    pb.drain("vector")
    pb.op("vector", "tensor_sub", a1, x_ap, ftile)
    pb.drain("vector")
    pb.op("vector", "tensor_scalar", a2, a1, 0.5, None, ALU.is_ge)
    pb.drain("vector")
    pb.op("vector", "tensor_add", ftile, ftile, a2)
    pb.drain("vector")
    pb.op("vector", "tensor_scalar", a2, a1, -0.5, None, ALU.is_le)
    pb.drain("vector")
    pb.op("vector", "tensor_sub", out_ap, ftile, a2)
    pb.drain("vector")


def emit_l0_ffn(K):
    cfg, pb = K.cfg, K.pb
    TL, D = cfg["TL"], cfg["D"]
    NT = TL // 128
    KC = D // 128
    eps = 1e-5
    Xd = K.x_dram
    K.dram_in("rhs", [D, D], F32)
    K.dram_in("rhsT", [D, D], F32)

    w_f32 = pb.sbuf("f0_wf", [128, KC * D], F32)
    q_rT = pb.sbuf("f0_qrt", [128, KC * D], BF16)
    q_r = pb.sbuf("f0_qr", [128, KC * D], BF16)
    rs = pb.sbuf("f0_rs", [128, KC], F32)
    itile = pb.sbuf("f0_it", [128, D], mybir.dt.int32)
    ft = pb.sbuf("f0_ft", [128, D], F32)
    a1 = pb.sbuf("f0_a1", [128, D], F32)
    a2 = pb.sbuf("f0_a2", [128, D], F32)
    xn = pb.sbuf("f0_xn", [128, D], F32)
    xq = pb.sbuf("f0_xq", [128, D], BF16)
    xqf = pb.sbuf("f0_xqf", [128, KC * 128], BF16)
    ls = pb.sbuf("f0_ls", [128, 1], F32)
    lsall = pb.sbuf("f0_lsall", [128, NT], F32)
    stats = pb.sbuf("f0_stats", [128, 8], F32)
    mean_c = pb.sbuf("f0_mean", [128, 1], F32)
    rstd_c = pb.sbuf("f0_rstd", [128, 1], F32)
    otok = pb.sbuf("f0_otok", [128, D], F32)
    res = pb.sbuf("f0_res", [128, D], BF16)
    sq = pb.sbuf("f0_sq", [128, D], F32)

    ps_t = pb.psum("f0_pst", [128, 1024], BF16)
    ps_a = pb.psum("f0_psa", [128, 512], F32)
    ps_b = pb.psum("f0_psb", [128, 512], F32)

    s_ld = pb.sem("f0_ld")
    s_v = pb.sem("f0_v")
    s_s = pb.sem("f0_s")
    s_t = pb.sem("f0_t")

    lv = None
    for r in range(KC):
        lv = pb.dma("sync", w_f32[:, r * D:(r + 1) * D],
                    K.inputs["rhsT"][r * 128:(r + 1) * 128, :], s_ld)
    pb.wait_dma("vector", s_ld, lv)
    for r in range(KC):
        sl = w_f32[:, r * D:(r + 1) * D]
        pb.op("vector", "reduce_max", rs[:, r:r + 1], sl, axis=AX.X,
              apply_absolute_value=True)
        pb.drain("vector")
        pb.op("vector", "tensor_scalar_max", rs[:, r:r + 1], rs[:, r:r + 1],
              1e-12)
        pb.drain("vector")
        pb.op("vector", "reciprocal", ls[:, 0:1], rs[:, r:r + 1])
        pb.drain("vector")
        pb.op("vector", "tensor_scalar", ls[:, 0:1], ls[:, 0:1], 127.0, None,
              ALU.mult)
        pb.drain("vector")
        pb.op("vector", "tensor_scalar", rs[:, r:r + 1], rs[:, r:r + 1],
              1.0 / 127.0, None, ALU.mult)
        pb.drain("vector")
        pb.op("vector", "tensor_scalar", sq[:, :], sl, ls[:, 0:1], None,
              ALU.mult)
        pb.drain("vector")
        emit_round_halfaway(pb, K, sq[:, :], sq[:, :], itile, ft, a1, a2)
        pb.op("vector", "tensor_copy", q_rT[:, r * D:(r + 1) * D], sq[:, :],
              inc=(s_v, 1))
    pb.wait("tensor", s_v)
    for r in range(KC):
        for c in range(KC):
            slot = c % 4
            dps = ps_t[:, slot * 128:slot * 128 + 128]
            tile_transpose(K, q_rT[:, r * D + c * 128:r * D + (c + 1) * 128],
                           dps, inc=(s_t, 1))
            pb.wait("vector", s_t)
            pb.op("vector", "tensor_copy",
                  q_r[:, c * D + r * 128:c * D + (r + 1) * 128], dps,
                  inc=(s_v, 1))
            pb.wait("tensor", s_v)
    pb.barrier()

    for t in range(NT):
        pb.wait_dma("sync", s_ld)
        pb.wait("sync", s_v)
        lv = pb.dma("sync", sq[:, :], Xd[t * 128:(t + 1) * 128, :], s_ld)
        pb.wait_dma("vector", s_ld, lv)
        ln_stats(pb, sq[:, :], stats, mean_c, rstd_c, D, eps, xn)
        pb.op("vector", "tensor_scalar", xn[:, :], sq[:, :], mean_c[:, 0:1],
              rstd_c[:, 0:1], ALU.subtract, ALU.mult)
        pb.drain("vector")
        pb.op("vector", "reduce_max", ls[:, 0:1], xn[:, :], axis=AX.X,
              apply_absolute_value=True)
        pb.drain("vector")
        pb.op("vector", "tensor_scalar_max", ls[:, 0:1], ls[:, 0:1], 1e-12)
        pb.drain("vector")
        pb.op("vector", "tensor_copy", lsall[:, t:t + 1], ls[:, 0:1])
        pb.op("vector", "reciprocal", ls[:, 0:1], ls[:, 0:1])
        pb.drain("vector")
        pb.op("vector", "tensor_scalar", xn[:, :], xn[:, :], ls[:, 0:1],
              127.0, ALU.mult, ALU.mult)
        pb.drain("vector")
        emit_round_halfaway(pb, K, xn[:, :], xn[:, :], itile, ft, a1, a2)
        pb.op("vector", "tensor_copy", xq[:, :], xn[:, :], inc=(s_v, 1))
        pb.wait("tensor", s_v)
        for c in range(KC):
            slot = c % 4
            dps = ps_t[:, slot * 128:slot * 128 + 128]
            tile_transpose(K, xq[:, c * 128:(c + 1) * 128], dps, inc=(s_t, 1))
            pb.wait("vector", s_t)
            pb.op("vector", "tensor_copy", xqf[:, c * 128:(c + 1) * 128],
                  dps, inc=(s_v, 1))
            pb.wait("tensor", s_v)
        for oc in range(KC):
            ps = ps_a if oc % 2 == 0 else ps_b
            for c in range(KC):
                stat = q_r[:, c * D + oc * 128:c * D + (oc + 1) * 128]
                pb.op("tensor", "ldweights", stat)
                pb.op("tensor", "matmul", ps[:, 0:128], stat,
                      xqf[:, c * 128:(c + 1) * 128], start=(c == 0),
                      stop=(c == KC - 1),
                      inc=(s_t, 1) if c == KC - 1 else None)
            pb.wait("vector", s_t)
            pb.op("vector", "tensor_scalar", res[:, oc * 128:(oc + 1) * 128]
                  if False else sq[:, oc * 128:(oc + 1) * 128],
                  ps[:, 0:128], rs[:, oc:oc + 1], None, ALU.mult,
                  inc=(s_v, 1))
            pb.wait("tensor", s_v)
        pb.drain("vector")
        pb.op("vector", "tensor_copy", res[:, :], sq[:, :], inc=(s_v, 1))
        pb.wait("tensor", s_v)
        for oc in range(KC):
            slot = oc % 4
            dps = ps_t[:, 512 + slot * 128:512 + slot * 128 + 128]
            tile_transpose(K, res[:, oc * 128:(oc + 1) * 128], dps,
                           inc=(s_t, 1))
            pb.wait("vector", s_t)
            pb.op("vector", "tensor_copy", otok[:, oc * 128:(oc + 1) * 128],
                  dps, inc=(s_v, 1))
            pb.wait("tensor", s_v)
        pb.drain("vector")
        pb.op("vector", "tensor_scalar", otok[:, :], otok[:, :],
              lsall[:, t:t + 1], 1.0 / 127.0, ALU.mult, ALU.mult)
        pb.drain("vector")
        ln_stats(pb, otok[:, :], stats, mean_c, rstd_c, D, eps, xn)
        pb.op("vector", "tensor_scalar", otok[:, :], otok[:, :],
              mean_c[:, 0:1], rstd_c[:, 0:1], ALU.subtract, ALU.mult)
        pb.drain("vector")
        pb.wait("sync", s_v)
        lv = pb.dma("sync", sq[:, :], Xd[t * 128:(t + 1) * 128, :], s_ld)
        pb.wait_dma("vector", s_ld, lv)
        pb.op("vector", "tensor_add", sq[:, :], sq[:, :], otok[:, :],
              inc=(s_v, 1))
        pb.wait("sync", s_v)
        pb.dma("sync", Xd[t * 128:(t + 1) * 128, :], sq[:, :], s_ld)
    pb.wait_dma("sync", s_ld)
    pb.barrier()



def emit_mamba(K):
    cfg, pb = K.cfg, K.pb
    T, TL, D = cfg["T"], cfg["TL"], cfg["D"]
    DI = cfg.get("DI", 2 * D)
    DIL = DI // 2
    MC = DIL // 128
    NT = TL // 128
    KC = D // 128
    DTR = cfg.get("DTR", 64)
    DS = cfg.get("DS", 16)
    W = 512
    TC = T // W
    eps_m = 1e-5
    eps = 1e-5
    Xd = K.x_dram

    K.dram_in("m_Winu", [D, DIL], F32)
    K.dram_in("m_Winz", [D, DIL], F32)
    K.dram_in("m_convw", [DIL, 4], F32)
    K.dram_in("m_convb", [DIL, 1], F32)
    K.dram_in("m_Wx", [DIL, DTR + 2 * DS], F32)
    K.dram_in("m_Wdt", [DTR, DIL], F32)
    K.dram_in("m_bdt", [DIL, 1], F32)
    K.dram_in("m_Alog", [DIL, DS], F32)
    K.dram_in("m_Dskip", [DIL, 1], F32)
    K.dram_in("m_Wout", [DIL, D], F32)
    K.dram_in("m_lng", [1, D], F32)
    K.dram_in("m_lnb", [1, D], F32)

    winu = pb.sbuf("m_winu", [128, KC * DIL], BF16)
    winz = pb.sbuf("m_winz", [128, KC * DIL], BF16)
    wout = pb.sbuf("m_wout", [128, MC * D], BF16)
    wx = pb.sbuf("m_wx", [128, MC * (DTR + 2 * DS)], BF16)
    wdt = pb.sbuf("m_wdt", [DTR, DIL], BF16)
    convw = pb.sbuf("m_cw", [128, MC * 4], F32)
    convb = pb.sbuf("m_cb", [128, MC], F32)
    bdt = pb.sbuf("m_bd", [128, MC], F32)
    Ask = pb.sbuf("m_A", [128, MC * DS], F32)
    dsk = pb.sbuf("m_ds", [128, MC], F32)
    hstate = pb.sbuf("m_h", [128, MC * DS], F32)
    tailb = pb.sbuf("m_tail", [128, MC * 3], BF16)

    sq = pb.sbuf("m_sq", [128, D], F32)
    xp = pb.sbuf("m_xp", [128, D], F32)
    xm_tok = pb.sbuf("m_xmtok", [128, D], BF16)
    stats = pb.sbuf("m_stats", [128, 8], F32)
    mean_c = pb.sbuf("m_mean", [128, 1], F32)
    rstd_c = pb.sbuf("m_rstd", [128, 1], F32)

    ubuf = pb.sbuf("m_ubuf", [128, MC * (W + 3)], BF16)
    ucb = pb.sbuf("m_ucb", [128, MC * W], BF16)
    zb = pb.sbuf("m_zb", [128, MC * W], BF16)
    dtb = pb.sbuf("m_dtb", [128, MC * W], BF16)
    wb = pb.sbuf("m_wb", [128, MC * W], BF16)
    yacc = pb.sbuf("m_yacc", [128, MC * W], F32)
    dbcp = pb.sbuf("m_dbcp", [DTR + 2 * DS, W], F32)
    bbc = pb.sbuf("m_bbc", [128, W], F32)
    cbc = pb.sbuf("m_cbc", [128, W], F32)
    brow0 = pb.sbuf("m_brow0", [1, W], F32)
    crow0 = pb.sbuf("m_crow0", [1, W], F32)
    identf = pb.sbuf("m_identf", [128, 128], F32)
    asc = pb.sbuf("m_asc", [128, 2 * W], BF16)
    bsc = pb.sbuf("m_bsc", [128, W], F32)
    hsc = pb.sbuf("m_hsc", [128, W], F32)
    tsc = pb.sbuf("m_tsc", [128, W], F32)
    xmg = pb.sbuf("m_xmg", [128, KC * W], BF16)
    ydone = pb.sbuf("m_ydone", [128, MC * W], BF16)
    dtr_bf = pb.sbuf("m_dtrbf", [DTR, W], BF16)
    ostage = pb.sbuf("m_ostage", [128, KC * W], BF16)
    sptmp = pb.sbuf("m_sptmp", [128, W], F32)

    ps_t = pb.psum("m_pst", [128, 1024], BF16)
    ps_a = pb.psum("m_psa", [128, 512], F32)
    ps_b = pb.psum("m_psb", [128, 512], F32)
    ps_c = pb.psum("m_psc", [128, 512], F32)

    dr_xm_in = K.dram_tmp("m_xm_in", [KC * 128, TL], BF16)
    dr_xm_all = K.dram_tmp("m_xm_all", [2, KC * 128, TL], BF16)
    dr_dbc_in = K.dram_tmp("m_dbc_in", [DTR + 2 * DS, W], F32)
    dr_dbc_out = K.dram_tmp("m_dbc_out", [DTR + 2 * DS, W], F32)
    dr_xp = K.dram_tmp("m_xp_d", [TL, D], F32)
    dr_mo_in = K.dram_tmp("m_mo_in", [T, D], BF16)
    dr_mo_my = K.dram_tmp("m_mo_my", [TL, D], BF16)

    s_ld = pb.sem("m_ld")
    s_v = pb.sem("m_v")
    s_s = pb.sem("m_s")
    s_t = pb.sem("m_t")
    s_g = pb.sem("m_g")
    s_cc = pb.sem("m_cc")

    pb.op("vector", "tensor_copy", identf[:, :], K.ident[:, :])
    pb.drain("vector")
    emit_weight_load(K, K.inputs["m_Winu"], winu, D, DIL)
    emit_weight_load(K, K.inputs["m_Winz"], winz, D, DIL)
    emit_weight_load(K, K.inputs["m_Wout"], wout, DIL, D)
    emit_weight_load(K, K.inputs["m_Wx"], wx, DIL, DTR + 2 * DS)
    gld = pb.sem("gld")
    pb.dma("gpsimd", wdt[:, :], K.inputs["m_Wdt"], gld)
    lv = None
    for nm, dst, wid in (("m_convw", convw, 4), ("m_convb", convb, 1),
                         ("m_bdt", bdt, 1), ("m_Alog", Ask, DS),
                         ("m_Dskip", dsk, 1)):
        for r in range(MC):
            lv = pb.dma("sync", dst[:, r * wid:(r + 1) * wid],
                        K.inputs[nm][r * 128:(r + 1) * 128, :], s_ld)
    pb.wait_dma("scalar", s_ld, lv)
    pb.wait_dma("vector", s_ld, lv)
    pb.op("scalar", "activation", Ask[:, :], Ask[:, :], AF.Exp, inc=(s_s, 1))
    pb.wait("vector", s_s)
    pb.op("vector", "tensor_scalar", Ask[:, :], Ask[:, :], -1.0, None,
          ALU.mult, inc=(s_v, 1))
    pb.op("vector", "memset", hstate[:, :], 0.0)
    pb.op("vector", "memset", tailb[:, :], 0.0)
    pb.drain("vector")
    pb.push_scope()
    grow = pb.sbuf("m_gr", [128, D], F32)
    brow = pb.sbuf("m_br", [128, D], F32)
    xfm = pb.sbuf("m_xfm", [128, KC * 128], BF16)
    pb.wait("gpsimd", s_v)
    lvg = pb.dma("sync", grow[0:1, :], K.inputs["m_lng"], s_ld)
    lvb = pb.dma("sync", brow[0:1, :], K.inputs["m_lnb"], s_ld)
    pb.wait_dma("gpsimd", s_ld, lvb)
    pb.op("gpsimd", "partition_broadcast", grow[:, :], grow[0:1, :],
          channels=128)
    pb.drain("gpsimd")
    pb.op("gpsimd", "partition_broadcast", brow[:, :], brow[0:1, :],
          channels=128, inc=(s_g, 1))
    pb.wait("vector", s_g)

    for t in range(NT):
        pb.wait_dma("sync", s_ld)
        pb.wait("sync", s_v)
        lv = pb.dma("sync", sq[:, :], Xd[t * 128:(t + 1) * 128, :], s_ld)
        pb.wait_dma("vector", s_ld, lv)
        ln_stats(pb, sq[:, :], stats, mean_c, rstd_c, D, eps, xp)
        pb.op("vector", "tensor_scalar", xp[:, :], sq[:, :], mean_c[:, 0:1],
              rstd_c[:, 0:1], ALU.subtract, ALU.mult, inc=(s_v, 1))
        pb.wait("sync", s_v)
        pb.dma("sync", dr_xp[t * 128:(t + 1) * 128, :], xp[:, :], s_ld)
        pb.drain("vector")
        ln_stats(pb, xp[:, :], stats, mean_c, rstd_c, D, eps_m, sq)
        pb.op("vector", "tensor_scalar", sq[:, :], xp[:, :], mean_c[:, 0:1],
              rstd_c[:, 0:1], ALU.subtract, ALU.mult)
        pb.drain("vector")
        pb.op("vector", "tensor_tensor", sq[:, :], sq[:, :], grow[:, :],
              ALU.mult)
        pb.drain("vector")
        pb.op("vector", "tensor_add", xm_tok[:, :], sq[:, :], brow[:, :],
              inc=(s_v, 1))
        pb.wait("tensor", s_v)
        for c in range(KC):
            slot = c % 4
            dps = ps_t[:, slot * 128:slot * 128 + 128]
            tile_transpose(K, xm_tok[:, c * 128:(c + 1) * 128], dps,
                           inc=(s_t, 1))
            pb.wait("vector", s_t)
            pb.op("vector", "tensor_copy", xfm[:, c * 128:(c + 1) * 128],
                  dps, inc=(s_v, 1))
            pb.wait("tensor", s_v)
        pb.wait("sync", s_v)
        for c in range(KC):
            pb.dma("sync",
                   dr_xm_in[c * 128:(c + 1) * 128, t * 128:(t + 1) * 128],
                   xfm[:, c * 128:(c + 1) * 128], s_ld)
    pb.raw("gpsimd", lambda g: None)
    pb.wait_dma("gpsimd", s_ld)
    pb.op("gpsimd", "collective_compute", "AllGather", ALU.bypass,
          replica_groups=cfg["groups"], ins=[dr_xm_in], outs=[dr_xm_all],
          inc=(s_cc, 1))
    pb.barrier()
    pb.pop_scope()

    pb.wait("sync", s_cc)
    pb.wait_dma("tensor", pb.sem("gld"))
    for tc in range(TC):
        slot = (tc * W) // TL
        col0 = (tc * W) % TL
        pb.wait("sync", s_t)
        pb.wait("sync", s_v)
        lv = None
        for c in range(KC):
            lv = pb.dma("sync", xmg[:, c * W:(c + 1) * W],
                        dr_xm_all[slot, c * 128:(c + 1) * 128,
                                  col0:col0 + W], s_ld)
        pb.wait_dma("tensor", s_ld, lv)
        for which, wsb, dst in (("u", winu, None), ("z", winz, None)):
            for mc in range(MC):
                for half in range(W // 512):
                    ps = ps_a if (mc + half) % 2 == 0 else ps_b
                    cols = slice(half * 512, (half + 1) * 512)
                    for c in range(KC):
                        stat = wsb[:, c * DIL + mc * 128:
                                   c * DIL + (mc + 1) * 128]
                        pb.op("tensor", "ldweights", stat)
                        pb.op("tensor", "matmul", ps[:, 0:512], stat,
                              xmg[:, c * W + half * 512:
                                  c * W + (half + 1) * 512],
                              start=(c == 0), stop=(c == KC - 1),
                              inc=(s_t, 1) if c == KC - 1 else None)
                    if which == "u":
                        pb.wait("vector", s_t)
                        pb.op("vector", "tensor_copy",
                              ubuf[:, mc * (W + 3) + 3 + half * 512:
                                   mc * (W + 3) + 3 + (half + 1) * 512],
                              ps[:, 0:512], inc=(s_v, 1))
                        pb.wait("tensor", s_v)
                    else:
                        pb.wait("scalar", s_t)
                        pb.op("scalar", "activation",
                              zb[:, mc * W + half * 512:
                                 mc * W + (half + 1) * 512],
                              ps[:, 0:512], AF.Silu, inc=(s_s, 1))
                        pb.wait("tensor", s_s)
        pb.wait("vector", s_s)
        for mc in range(MC):
            ub = ubuf[:, mc * (W + 3):mc * (W + 3) + W + 3]
            pb.op("vector", "tensor_copy", ub[:, 0:3],
                  tailb[:, mc * 3:(mc + 1) * 3])
        pb.drain("vector")
        for mc in range(MC):
            ub = ubuf[:, mc * (W + 3):mc * (W + 3) + W + 3]
            pb.op("vector", "tensor_scalar", sptmp[:, 0:W], ub[:, 0:W],
                  convw[:, mc * 4 + 0:mc * 4 + 1], None, ALU.mult)
            pb.drain("vector")
            for j in range(1, 4):
                pb.op("vector", "scalar_tensor_tensor", sptmp[:, 0:W],
                      ub[:, j:j + W], convw[:, mc * 4 + j:mc * 4 + j + 1],
                      sptmp[:, 0:W], ALU.mult, ALU.add)
                pb.drain("vector")
            pb.op("vector", "tensor_scalar", sptmp[:, 0:W], sptmp[:, 0:W],
                  convb[:, mc:mc + 1], None, ALU.add, inc=(s_v, 1))
            pb.wait("scalar", s_v)
            pb.op("scalar", "activation", ucb[:, mc * W:(mc + 1) * W],
                  sptmp[:, 0:W], AF.Silu, inc=(s_s, 1))
            pb.wait("vector", s_s)
            pb.op("vector", "tensor_copy", tailb[:, mc * 3:(mc + 1) * 3],
                  ub[:, W:W + 3])
            pb.drain("vector")
        pb.op("vector", "nop", inc=(s_v, 1))
        pb.wait("tensor", s_v)
        NB = DTR + 2 * DS
        for half in range(W // 512):
            for mc in range(MC):
                stat = wx[:, mc * NB:(mc + 1) * NB]
                pb.op("tensor", "ldweights", stat)
                pb.op("tensor", "matmul", ps_c[0:NB, half * 512 // 1:512]
                      if False else ps_c[0:NB, 0:512], stat,
                      ucb[:, mc * W + half * 512:mc * W + (half + 1) * 512],
                      start=(mc == 0), stop=(mc == MC - 1),
                      inc=(s_t, 1) if mc == MC - 1 else None)
            pb.wait("vector", s_t)
            pb.op("vector", "tensor_copy",
                  dbcp[:, half * 512:(half + 1) * 512], ps_c[0:NB, 0:512],
                  inc=(s_v, 1))
            pb.wait("tensor", s_v)
        pb.wait("sync", s_v)
        pb.dma("sync", dr_dbc_in[:, :], dbcp[:, :], s_ld)
        pb.raw("gpsimd", lambda g: None)
        pb.wait_dma("gpsimd", s_ld)
        pb.op("gpsimd", "collective_compute", "AllReduce", ALU.add,
              replica_groups=cfg["groups"], ins=[dr_dbc_in],
              outs=[dr_dbc_out], inc=(s_cc, 1))
        pb.wait("sync", s_cc)
        lv = pb.dma("sync", dbcp[:, :], dr_dbc_out[:, :], s_ld)
        pb.wait_dma("vector", s_ld, lv)
        pb.wait_dma("tensor", s_ld, lv)
        pb.wait_dma("gpsimd", s_ld, lv)
        pb.op("vector", "tensor_copy", dtr_bf[:, 0:W], dbcp[0:DTR, 0:W],
              inc=(s_v, 1))
        pb.wait("tensor", s_v)
        for mc in range(MC):
            stat = wdt[:, mc * 128:(mc + 1) * 128]
            pb.op("tensor", "ldweights", stat)
            for half in range(W // 512):
                pb.op("tensor", "matmul", ps_a[:, 0:512] if mc % 2 == 0
                      else ps_b[:, 0:512], stat,
                      dtr_bf[:, half * 512:(half + 1) * 512],
                      start=True, stop=True, inc=(s_t, 1))
            ps = ps_a if mc % 2 == 0 else ps_b
            pb.wait("vector", s_t)
            pb.op("vector", "tensor_scalar", sptmp[:, 0:W], ps[:, 0:W],
                  bdt[:, mc:mc + 1], None, ALU.add, inc=(s_v, 1))
            pb.wait("tensor", s_v)
            pb.wait("scalar", s_v)
            softplus(pb, sptmp[:, 0:W], dtb[:, mc * W:(mc + 1) * W],
                     inc=(s_s, 1))
            pb.wait("vector", s_s)
            softplus_finish(pb, dtb[:, mc * W:(mc + 1) * W], sptmp[:, 0:W],
                            dtb[:, mc * W:(mc + 1) * W])
            pb.drain("vector")
            pb.op("vector", "tensor_tensor", wb[:, mc * W:(mc + 1) * W],
                  dtb[:, mc * W:(mc + 1) * W], ucb[:, mc * W:(mc + 1) * W],
                  ALU.mult)
            pb.drain("vector")
        pb.op("vector", "memset", yacc[:, :], 0.0, inc=(s_v, 1))
        for s in range(DS):
            pb.wait("vector", s_g)
            pb.wait("tensor", s_g)
            pb.op("tensor", "matmul", ps_c[0:1, 0:W],
                  identf[0:NB, DTR + s:DTR + s + 1], dbcp[0:NB, 0:W],
                  start=True, stop=True, inc=(s_t, 1))
            pb.wait("vector", s_t)
            pb.op("vector", "tensor_copy", brow0[:, 0:W], ps_c[0:1, 0:W],
                  inc=(s_v, 1))
            pb.wait("tensor", s_v)
            pb.op("tensor", "matmul", ps_c[0:1, 0:W],
                  identf[0:NB, DTR + DS + s:DTR + DS + s + 1],
                  dbcp[0:NB, 0:W], start=True, stop=True, inc=(s_t, 1))
            pb.wait("vector", s_t)
            pb.op("vector", "tensor_copy", crow0[:, 0:W], ps_c[0:1, 0:W],
                  inc=(s_v, 1))
            pb.wait("tensor", s_v)
            pb.wait("gpsimd", s_v)
            pb.op("gpsimd", "partition_broadcast", bbc[:, 0:W],
                  brow0[:, 0:W], channels=128)
            pb.drain("gpsimd")
            pb.op("gpsimd", "partition_broadcast", cbc[:, 0:W],
                  crow0[:, 0:W], channels=128, inc=(s_g, 1))
            pb.wait("vector", s_g)
            pb.wait("scalar", s_g)
            for mc in range(MC):
                par = mc % 2
                aa = asc[:, par * W:(par + 1) * W]
                pb.wait("scalar", s_v)
                pb.op("scalar", "activation", aa,
                      dtb[:, mc * W:(mc + 1) * W], AF.Exp,
                      scale=Ask[:, mc * DS + s:mc * DS + s + 1],
                      inc=(s_s, 1))
                pb.wait("vector", s_s)
                pb.op("vector", "tensor_tensor", bsc[:, 0:W],
                      wb[:, mc * W:(mc + 1) * W], bbc[:, 0:W], ALU.mult)
                pb.drain("vector")
                pb.op("vector", "tensor_tensor_scan", hsc[:, 0:W], aa,
                      bsc[:, 0:W],
                      hstate[:, mc * DS + s:mc * DS + s + 1],
                      ALU.mult, ALU.add, inc=(s_v, 1))
                pb.drain("vector")
                pb.op("vector", "tensor_copy",
                      hstate[:, mc * DS + s:mc * DS + s + 1],
                      hsc[:, W - 1:W])
                pb.drain("vector")
                pb.op("vector", "tensor_tensor", tsc[:, 0:W], hsc[:, 0:W],
                      cbc[:, 0:W], ALU.mult)
                pb.drain("vector")
                pb.op("vector", "tensor_add", yacc[:, mc * W:(mc + 1) * W],
                      yacc[:, mc * W:(mc + 1) * W], tsc[:, 0:W])
                pb.drain("vector")
            pb.op("vector", "nop", inc=(s_v, 1))
        for mc in range(MC):
            pb.op("vector", "scalar_tensor_tensor",
                  yacc[:, mc * W:(mc + 1) * W], ucb[:, mc * W:(mc + 1) * W],
                  dsk[:, mc:mc + 1], yacc[:, mc * W:(mc + 1) * W],
                  ALU.mult, ALU.add)
            pb.drain("vector")
            pb.op("vector", "tensor_tensor", ydone[:, mc * W:(mc + 1) * W],
                  yacc[:, mc * W:(mc + 1) * W], zb[:, mc * W:(mc + 1) * W],
                  ALU.mult)
            pb.drain("vector")
        pb.op("vector", "nop", inc=(s_v, 1))
        pb.wait("tensor", s_v)
        pb.wait_dma("vector", s_ld)
        DH = D // 2
        for j in range(W // 128):
            for mc in range(MC):
                stat = ydone[:, mc * W + j * 128:mc * W + j * 128 + 128]
                pb.op("tensor", "ldweights", stat)
                pb.op("tensor", "matmul", ps_a[:, 0:DH], stat,
                      wout[:, mc * D:mc * D + DH], start=(mc == 0),
                      stop=(mc == MC - 1),
                      inc=(s_t, 1) if mc == MC - 1 else None)
                pb.op("tensor", "matmul", ps_b[:, 0:DH], stat,
                      wout[:, mc * D + DH:(mc + 1) * D], start=(mc == 0),
                      stop=(mc == MC - 1),
                      inc=(s_t, 1) if mc == MC - 1 else None)
            pb.wait("vector", s_t)
            pb.op("vector", "tensor_copy", ostage[:, j * D:j * D + DH],
                  ps_a[:, 0:DH], inc=(s_v, 1))
            pb.op("vector", "tensor_copy", ostage[:, j * D + DH:(j + 1) * D],
                  ps_b[:, 0:DH], inc=(s_v, 1))
            pb.wait("tensor", s_v)
        pb.wait("sync", s_v)
        for j in range(W // 128):
            pb.dma("sync",
                   dr_mo_in[tc * W + j * 128:tc * W + (j + 1) * 128, :],
                   ostage[:, j * D:(j + 1) * D], s_ld)
    obf = pb.sbuf("m_obf", [128, D], BF16)
    otok = pb.sbuf("m_otok", [128, D], F32)
    pb.raw("gpsimd", lambda g: None)
    pb.wait_dma("gpsimd", s_ld)
    pb.op("gpsimd", "collective_compute", "ReduceScatter", ALU.add,
          replica_groups=cfg["groups"], ins=[dr_mo_in], outs=[dr_mo_my],
          inc=(s_cc, 1))
    pb.wait("sync", s_cc)
    for t in range(NT):
        pb.wait("sync", s_v)
        pb.wait_dma("sync", s_ld)
        lvo = pb.dma("sync", obf[:, :], dr_mo_my[t * 128:(t + 1) * 128, :],
                     s_ld)
        lvx = pb.dma("sync", xp[:, :], dr_xp[t * 128:(t + 1) * 128, :], s_ld)
        pb.wait_dma("vector", s_ld, lvx)
        pb.op("vector", "tensor_copy", otok[:, :], obf[:, :], inc=(s_v, 1))
        pb.drain("vector")
        pb.op("vector", "tensor_add", otok[:, :], otok[:, :], xp[:, :])
        pb.drain("vector")
        ln_stats(pb, otok[:, :], stats, mean_c, rstd_c, D, eps, xm_tok)
        pb.op("vector", "tensor_scalar", otok[:, :], otok[:, :],
              mean_c[:, 0:1], rstd_c[:, 0:1], ALU.subtract, ALU.mult)
        pb.drain("vector")
        lvx = pb.dma("sync", sq[:, :], Xd[t * 128:(t + 1) * 128, :], s_ld)
        pb.wait_dma("vector", s_ld, lvx)
        pb.op("vector", "tensor_add", sq[:, :], sq[:, :], otok[:, :],
              inc=(s_v, 1))
        pb.wait("sync", s_v)
        pb.dma("sync", Xd[t * 128:(t + 1) * 128, :], sq[:, :], s_ld)
    pb.wait_dma("sync", s_ld)
    pb.barrier()



def rmsnorm_tile(pb, K, x_ap, out_ap, wrow_bcast, stats, sq_scr, D, s_v, lns,
                 eps=1e-6, inc=None):
    s_sq = stats[:, 1:2]
    rst = stats[:, 3:4]
    pb.op("vector", "scalar_tensor_tensor", sq_scr[:, :], x_ap, 1.0, x_ap,
          ALU.bypass, ALU.mult, accum_out=s_sq)
    pb.drain("vector")
    pb.op("vector", "tensor_scalar", rst, s_sq, 1.0 / D, eps, ALU.mult,
          ALU.add)
    pb.drain("vector")
    pb.op("vector", "reciprocal", rst, rst, inc=(s_v, 1))
    pb.wait("scalar", s_v)
    pb.op("scalar", "activation", rst, rst, AF.Sqrt, inc=(lns, 1))
    pb.wait("vector", lns)
    if wrow_bcast is not None:
        pb.op("vector", "tensor_scalar", sq_scr[:, :], x_ap, rst, None,
              ALU.mult)
        pb.drain("vector")
        pb.op("vector", "tensor_tensor", out_ap, sq_scr[:, :], wrow_bcast,
              ALU.mult, inc=inc)
        pb.drain("vector")
    else:
        pb.op("vector", "tensor_scalar", out_ap, x_ap, rst, None, ALU.mult,
              inc=inc)
        pb.drain("vector")


def emit_l1_ffn(K):
    cfg, pb = K.cfg, K.pb
    T, TL, D, HL, HD, G = (cfg["T"], cfg["TL"], cfg["D"], cfg["HL"],
                           cfg["HD"], cfg["G"])
    FFN = cfg.get("FFN", 4 * D)
    S2 = T // 4
    NT = TL // 128
    KC = D // 128
    MY = HL * HD
    MYC = MY // 128
    NG = T // G
    GT = G // 128
    VW = HL * (HD + 1)
    HALF = HD // 2
    eps = 1e-5
    Xd = K.x_dram

    K.dram_in("d_w1", [D, 3 * MY], F32)
    K.dram_in("d_wo1", [MY, D], F32)
    K.dram_in("d_w2", [D, 3 * MY], F32)
    K.dram_in("d_wo2", [MY, D], F32)
    K.dram_in("d_wg1", [FFN // 128, 128, D], F32)
    K.dram_in("d_wu1", [FFN // 128, 128, D], F32)
    K.dram_in("d_wd1", [D // 128, 128, FFN], F32)
    K.dram_in("d_wg2", [FFN // 128, 128, D], F32)
    K.dram_in("d_wu2", [FFN // 128, 128, D], F32)
    K.dram_in("d_wd2", [D // 128, 128, FFN], F32)
    K.dram_in("d_ln11", [1, D], F32)
    K.dram_in("d_ln21", [1, D], F32)
    K.dram_in("d_ln12c", [128, KC], F32)
    K.dram_in("d_ln22c", [128, KC], F32)
    K.dram_in("d_pnw", [1, D], F32)
    K.dram_in("d_pw", [D, 2 * D], F32)
    K.dram_in("d_tpos", [1, TL], F32)

    dr_hs = K.dram_tmp("d_hs", [TL, D], BF16)
    K.d_hs = dr_hs
    pb.push_scope()
    sq = pb.sbuf("d_sq", [128, D], F32)
    sq2 = pb.sbuf("d_sq2", [128, D], F32)
    xn_tok = pb.sbuf("d_xntok", [128, D], BF16)
    xfm = pb.sbuf("d_xfm", [128, KC * 128], BF16)
    stats = pb.sbuf("d_stats", [128, 8], F32)
    mean_c = pb.sbuf("d_mean", [128, 1], F32)
    rstd_c = pb.sbuf("d_rstd", [128, 1], F32)
    lnrow = pb.sbuf("d_lnrow", [128, D], F32)
    ln2row = pb.sbuf("d_ln2row", [128, D], F32)
    w1 = pb.sbuf("d_w1sb", [128, KC * 3 * MY], BF16)
    wo = pb.sbuf("d_wosb", [128, MYC * D], BF16)
    kst = pb.sbuf("d_kst", [128, MYC * T], BF16)
    vtok = pb.sbuf("d_vtok", [128, (T // 128) * VW], BF16)
    cpos = pb.sbuf("d_cpos", [128, T], BF16)
    spos = pb.sbuf("d_spos", [128, T], BF16)
    invf = pb.sbuf("d_invf", [HALF, 1], F32)
    qg = pb.sbuf("d_qg", [128, MYC * G], BF16)
    qrope = pb.sbuf("d_qrope", [128, MYC * G], BF16)
    qpad = pb.sbuf("d_qpad", [128, 2 * G], BF16)
    ptile = pb.sbuf("d_p", [128, 2 * G], BF16)
    tri = pb.sbuf("d_tri", [128, GT * G], BF16)
    ygrp = pb.sbuf("d_ygrp", [128, MYC * G], BF16)
    denb = pb.sbuf("d_denb", [HD, G], F32)
    den1 = pb.sbuf("d_den1", [1, G], F32)
    ostage = pb.sbuf("d_ostage", [128, KC * G], BF16)
    obf = pb.sbuf("d_obf", [128, D], BF16)
    otok = pb.sbuf("d_otok", [128, D], F32)
    xn_grp = pb.sbuf("d_xngrp", [128, KC * G], BF16)

    ps_t = pb.psum("d_pst", [128, 1024], BF16)
    ps_a = pb.psum("d_psa", [128, 512], F32)
    ps_b = pb.psum("d_psb", [128, 512], F32)
    ps_s0 = pb.psum("d_pss0", [128, 512], F32)
    ps_s1 = pb.psum("d_pss1", [128, 512], F32)
    ps_y = pb.psum("d_psy", [128, 512], F32)
    ps_y2 = pb.psum("d_psy2", [128, 512], F32)

    dr_h_in = K.dram_tmp("d_h_in", [KC * 128, TL], BF16)
    dr_h_all = K.dram_tmp("d_h_all", [2, KC * 128, TL], BF16)
    dr_oa_in = K.dram_tmp("d_oa_in", [T, D], BF16)
    dr_oa_my = K.dram_tmp("d_oa_my", [TL, D], BF16)
    dr_proc = K.dram_tmp("d_proc", [TL, D], F32)
    K.d_proc = dr_proc

    s_ld = pb.sem("d_ld")
    s_v = pb.sem("d_v")
    s_s = pb.sem("d_s")
    s_t = pb.sem("d_t")
    s_g = pb.sem("d_g")
    s_cc = pb.sem("d_cc")
    lns = pb.sem("ln_s")

    pb.push_scope()
    ang = pb.sbuf("d_ang", [HALF, 1024], F32)
    pb.op("gpsimd", "iota", invf[:, 0:1], pattern=[[1, 1]], base=0,
          channel_multiplier=1, allow_small_or_imprecise_dtypes=True,
          inc=(s_g, 1))
    pb.wait("scalar", s_g)
    pb.op("scalar", "activation", invf[:, 0:1], invf[:, 0:1], AF.Exp,
          scale=-math.log(10000.0) / HALF, inc=(s_s, 1))
    pb.wait("vector", s_s)
    for t0 in range(0, T, 1024):
        pb.op("gpsimd", "iota", ang[:, :], pattern=[[1, 1024]], base=t0,
              channel_multiplier=0, allow_small_or_imprecise_dtypes=True,
              inc=(s_g, 1))
        pb.wait("vector", s_g)
        pb.op("vector", "tensor_scalar", ang[:, :], ang[:, :], invf[:, 0:1],
              None, ALU.mult)
        pb.drain("vector")
        emit_sincos(K, ang, cpos[0:HALF, t0:t0 + 1024],
                    spos[0:HALF, t0:t0 + 1024], HALF, 1024)
        pb.wait("gpsimd", pb.sem("rr_s"))
    pb.op("vector", "nop", inc=(s_v, 1))
    pb.wait("sync", s_v)
    for rr in (32, 64, 96):
        pb.dma("sync", cpos[rr:rr + HALF, :], cpos[0:HALF, :], s_ld)
        lv = pb.dma("sync", spos[rr:rr + HALF, :], spos[0:HALF, :], s_ld)
    pb.wait_dma("vector", s_ld, lv)
    pb.pop_scope()
    for j in range(GT):
        sl = tri[:, j * G:(j + 1) * G]
        pb.op("gpsimd", "memset", sl, 1.0)
        pb.drain("gpsimd")
        pb.op("gpsimd", "affine_select", sl, sl, pattern=[[1, G]],
              base=-128 * j, channel_multiplier=-1, compare_op=ALU.is_ge,
              fill=0.0)
        pb.drain("gpsimd")
    pb.op("gpsimd", "nop", inc=(s_g, 1))
    lv = pb.dma("sync", lnrow[0:1, :], K.inputs["d_ln11"], s_ld)
    pb.wait_dma("gpsimd", s_ld, lv)
    pb.op("gpsimd", "partition_broadcast", lnrow[:, :], lnrow[0:1, :],
          channels=128, inc=(s_g, 1))
    pb.wait("vector", s_g)
    emit_weight_load(K, K.inputs["d_w1"], w1, D, 3 * MY)
    gld, wv = emit_weight_load(K, K.inputs["d_wo1"], wo, MY, D)

    for t in range(NT):
        pb.wait_dma("sync", s_ld)
        pb.wait("sync", s_v)
        lv = pb.dma("sync", sq[:, :], Xd[t * 128:(t + 1) * 128, :], s_ld)
        pb.wait_dma("vector", s_ld, lv)
        ln_stats(pb, sq[:, :], stats, mean_c, rstd_c, D, eps, sq2)
        pb.op("vector", "tensor_scalar", sq[:, :], sq[:, :], mean_c[:, 0:1],
              rstd_c[:, 0:1], ALU.subtract, ALU.mult)
        pb.drain("vector")
        pb.op("vector", "tensor_copy", xn_tok[:, :], sq[:, :], inc=(s_v, 1))
        pb.wait("sync", s_v)
        pb.dma("sync", dr_proc[t * 128:(t + 1) * 128, :], sq[:, :], s_ld)
        lvh = pb.dma("sync", dr_hs[t * 128:(t + 1) * 128, :], xn_tok[:, :],
                     s_ld)
        pb.wait_dma("vector", s_ld, lvh)
        rmsnorm_tile(pb, K, sq[:, :], xn_tok[:, :], lnrow[:, :], stats, sq2,
                     D, s_v, lns, inc=(s_v, 1))
        pb.wait("tensor", s_v)
        for c in range(KC):
            slot = c % 4
            dps = ps_t[:, slot * 128:slot * 128 + 128]
            tile_transpose(K, xn_tok[:, c * 128:(c + 1) * 128], dps,
                           inc=(s_t, 1))
            pb.wait("vector", s_t)
            pb.op("vector", "tensor_copy", xfm[:, c * 128:(c + 1) * 128],
                  dps, inc=(s_v, 1))
            pb.wait("tensor", s_v)
        pb.wait("sync", s_v)
        for c in range(KC):
            pb.dma("sync",
                   dr_h_in[c * 128:(c + 1) * 128, t * 128:(t + 1) * 128],
                   xfm[:, c * 128:(c + 1) * 128], s_ld)
    pb.raw("gpsimd", lambda g: None)
    pb.wait_dma("gpsimd", s_ld)
    pb.op("gpsimd", "collective_compute", "AllGather", ALU.bypass,
          replica_groups=cfg["groups"], ins=[dr_h_in], outs=[dr_h_all],
          inc=(s_cc, 1))

    pb.op("vector", "memset", vtok[:, :], 0.0)
    pb.drain("vector")
    last = (T // 128 - 1, HL - 1)
    for tc in range(T // 128):
        for h in range(HL):
            base = tc * VW + h * (HD + 1) + HD
            pb.op("vector", "memset", vtok[:, base:base + 1], 1.0,
                  inc=(s_v, 1) if (tc, h) == last else None)
    pb.wait("sync", s_cc)
    pb.wait_dma("tensor", gld, wv)
    pb.wait("tensor", s_v)

    def rope_chunk(dst, src, gcol0, glen, sgn_q=True):
        for r in (0, 64):
            x1 = src[r:r + HALF, 0:glen]
            x2 = src[r + HALF:r + 2 * HALF, 0:glen]
            cc1 = cpos[r:r + HALF, gcol0:gcol0 + glen]
            cc2 = cpos[r + HALF:r + 2 * HALF, gcol0:gcol0 + glen]
            ss1 = spos[r:r + HALF, gcol0:gcol0 + glen]
            ss2 = spos[r + HALF:r + 2 * HALF, gcol0:gcol0 + glen]
            d1 = dst[r:r + HALF, 0:glen]
            d2 = dst[r + HALF:r + 2 * HALF, 0:glen]
            pb.op("vector", "tensor_tensor", d1, x1, cc1, ALU.mult)
            pb.op("vector", "tensor_tensor", d2, x2, cc2, ALU.mult)
            pb.drain("vector")
            pb.op("vector", "tensor_tensor", sq[r:r + HALF, 0:glen], x2,
                  ss2, ALU.mult)
            pb.op("vector", "tensor_tensor",
                  sq[r + HALF:r + 2 * HALF, 0:glen], x1, ss1, ALU.mult)
            pb.drain("vector")
            pb.op("vector", "tensor_sub", d1, d1, sq[r:r + HALF, 0:glen])
            pb.op("vector", "tensor_add", d2, d2,
                  sq[r + HALF:r + 2 * HALF, 0:glen])
            pb.drain("vector")

    for kg in range(NG):
        slot = (kg * G) // TL
        col0 = (kg * G) % TL
        pb.wait("sync", s_t)
        lv = None
        for c in range(KC):
            lv = pb.dma("sync", xn_grp[:, c * G:(c + 1) * G],
                        dr_h_all[slot, c * 128:(c + 1) * 128,
                                 col0:col0 + G], s_ld)
        pb.wait_dma("tensor", s_ld, lv)
        for which, wofs in (("k", MY), ("v", 2 * MY)):
            for mc in range(MYC):
                ps = ps_a if (mc % 2 == 0) else ps_b
                for c in range(KC):
                    stat = w1[:, c * 3 * MY + wofs + mc * 128:
                              c * 3 * MY + wofs + (mc + 1) * 128]
                    pb.op("tensor", "ldweights", stat)
                    pb.op("tensor", "matmul", ps[:, 0:G], stat,
                          xn_grp[:, c * G:(c + 1) * G], start=(c == 0),
                          stop=(c == KC - 1),
                          inc=(s_t, 1) if c == KC - 1 else None)
                if which == "k":
                    pb.wait("vector", s_t)
                    pb.op("vector", "tensor_copy", xn_tok[:, 0:G],
                          ps[:, 0:G], inc=(s_v, 1))
                    pb.wait("tensor", s_v)
                    pb.drain("vector")
                    rope_chunk(kst[:, mc * T + kg * G:mc * T + (kg + 1) * G],
                               xn_tok[:, 0:G], kg * G, G)
                    pb.op("vector", "nop", inc=(s_v, 1))
                else:
                    pb.wait("vector", s_t)
                    pb.op("vector", "tensor_copy", xn_tok[:, 0:G],
                          ps[:, 0:G], inc=(s_v, 1))
                    pb.wait("tensor", s_v)
                    for q in range(GT):
                        slot2 = q % 4
                        dps = ps_t[:, 512 + slot2 * 128:
                                   512 + slot2 * 128 + 128]
                        tile_transpose(K, xn_tok[:, q * 128:(q + 1) * 128],
                                       dps, inc=(s_t, 1))
                        pb.wait("vector", s_t)
                        tc = kg * GT + q
                        for hh in range(128 // HD):
                            h = mc * (128 // HD) + hh
                            pb.op("vector", "tensor_copy",
                                  vtok[:, tc * VW + h * (HD + 1):
                                       tc * VW + h * (HD + 1) + HD],
                                  dps[:, hh * HD:(hh + 1) * HD],
                                  inc=(s_v, 1))
                        pb.wait("tensor", s_v)
    pb.barrier()

    pb.op("vector", "memset", qpad[:, :], 0.0)
    pb.drain("vector")
    K.att_pe_gate = {0: None, 1: None}
    K.att_exp_gate = {0: None, 1: None}
    K.att_ygate = {0: None, 1: None}
    for g in range(NG):
        nkt = (g + 1) * GT
        slot = (g * G) // TL
        col0 = (g * G) % TL
        pb.wait("sync", s_t)
        lv = None
        for c in range(KC):
            lv = pb.dma("sync", xn_grp[:, c * G:(c + 1) * G],
                        dr_h_all[slot, c * 128:(c + 1) * 128,
                                 col0:col0 + G], s_ld)
        pb.wait_dma("tensor", s_ld, lv)
        for mc in range(MYC):
            ps = ps_a if (mc % 2 == 0) else ps_b
            for c in range(KC):
                stat = w1[:, c * 3 * MY + mc * 128:c * 3 * MY + (mc + 1) * 128]
                pb.op("tensor", "ldweights", stat)
                pb.op("tensor", "matmul", ps[:, 0:G], stat,
                      xn_grp[:, c * G:(c + 1) * G], start=(c == 0),
                      stop=(c == KC - 1),
                      inc=(s_t, 1) if c == KC - 1 else None)
            pb.wait("vector", s_t)
            pb.op("vector", "tensor_copy", qg[:, mc * G:(mc + 1) * G],
                  ps[:, 0:G], inc=(s_v, 1))
            pb.wait("tensor", s_v)
            pb.drain("vector")
            rope_chunk(qrope[:, mc * G:(mc + 1) * G], qg[:, mc * G:(mc + 1) * G],
                       g * G, G)
            pb.op("vector", "nop", inc=(s_v, 1))
        pb.wait("tensor", s_v)
        for h in range(HL):
            mc0, r0 = divmod(h * HD, 128)
            sl = r0 // 64
            hb = h % 2
            yslot = h % 2
            psy = ps_y if yslot == 0 else ps_y2
            if K.att_pe_gate[hb] is not None:
                pb.wait("vector", s_t, K.att_pe_gate[hb])
            qpg = pb.op("vector", "tensor_copy",
                        qpad[r0:r0 + HD, sl * G:(sl + 1) * G],
                        qrope[r0:r0 + HD, mc0 * G:(mc0 + 1) * G],
                        inc=(s_v, 1))
            scd = {}
            expd = {}
            mskd = {}
            pvd = {}

            def emit_scores(kt):
                slot = kt % 2
                ps_sc = ps_s0 if slot == 0 else ps_s1
                if kt == 0:
                    vgate = qpg
                    if K.att_ygate[yslot] is not None:
                        vgate = max(vgate, K.att_ygate[yslot])
                    pb.wait("tensor", s_v, vgate)
                if K.att_exp_gate[slot] is not None:
                    pb.wait("tensor", s_s, K.att_exp_gate[slot])
                stat = kst[:, mc0 * T + kt * 128:mc0 * T + (kt + 1) * 128]
                pb.op("tensor", "ldweights", stat)
                scd[kt] = pb.op("tensor", "matmul", ps_sc[:, 0:G], stat,
                                qpad[:, sl * G:(sl + 1) * G],
                                start=True, stop=True, inc=(s_t, 1))

            def emit_exp_mask(kt):
                slot = kt % 2
                ps_sc = ps_s0 if slot == 0 else ps_s1
                pb.wait("scalar", s_t, scd[kt])
                expd[kt] = pb.op("scalar", "activation",
                                 ptile[:, slot * G:(slot + 1) * G],
                                 ps_sc[:, 0:G], AF.Exp,
                                 scale=1.0 / math.sqrt(HD), inc=(s_s, 1))
                K.att_exp_gate[slot] = expd[kt]
                diag = kt - g * GT
                if diag >= 0:
                    pb.wait("vector", s_s, expd[kt])
                    mskd[kt] = pb.op("vector", "tensor_tensor",
                                     ptile[:, slot * G:(slot + 1) * G],
                                     ptile[:, slot * G:(slot + 1) * G],
                                     tri[:, diag * G:(diag + 1) * G],
                                     ALU.mult, inc=(s_v, 1))
                else:
                    mskd[kt] = None

            def emit_pv(kt):
                slot = kt % 2
                if mskd[kt] is not None:
                    pb.wait("tensor", s_v, mskd[kt])
                else:
                    pb.wait("tensor", s_s, expd[kt])
                pb.op("tensor", "ldweights",
                      vtok[:, kt * VW + h * (HD + 1):
                           kt * VW + (h + 1) * (HD + 1)])
                pvd[kt] = pb.op("tensor", "matmul", psy[0:HD + 1, 0:G],
                                vtok[:, kt * VW + h * (HD + 1):
                                     kt * VW + (h + 1) * (HD + 1)],
                                ptile[:, slot * G:(slot + 1) * G],
                                start=(kt == 0), stop=(kt == nkt - 1),
                                inc=(s_t, 1))

            for kt in range(nkt):
                emit_scores(kt)
                emit_exp_mask(kt)
                if kt >= 1:
                    emit_pv(kt - 1)
            emit_pv(nkt - 1)
            K.att_pe_gate[hb] = scd[nkt - 1]
            pb.wait("vector", s_t, pvd[nkt - 1])
            rv = pb.op("vector", "reciprocal", den1[:, :],
                       psy[HD:HD + 1, 0:G], inc=(s_v, 1))
            pb.wait("gpsimd", s_v, rv)
            gv = pb.op("gpsimd", "partition_broadcast", denb[:, :],
                       den1[:, :], channels=HD, inc=(s_g, 1))
            pb.wait("vector", s_g, gv)
            K.att_ygate[yslot] = pb.op(
                "vector", "tensor_tensor",
                ygrp[r0:r0 + HD, mc0 * G:(mc0 + 1) * G], psy[0:HD, 0:G],
                denb[:, :], ALU.mult, inc=(s_v, 1))
        pb.wait("tensor", s_v)
        pb.wait_dma("vector", s_ld)
        DH = D // 2
        for j in range(G // 128):
            for c in range(MYC):
                stat = ygrp[:, c * G + j * 128:c * G + j * 128 + 128]
                pb.op("tensor", "ldweights", stat)
                pb.op("tensor", "matmul", ps_a[:, 0:DH], stat,
                      wo[:, c * D:c * D + DH], start=(c == 0),
                      stop=(c == MYC - 1),
                      inc=(s_t, 1) if c == MYC - 1 else None)
                pb.op("tensor", "matmul", ps_b[:, 0:DH], stat,
                      wo[:, c * D + DH:(c + 1) * D], start=(c == 0),
                      stop=(c == MYC - 1),
                      inc=(s_t, 1) if c == MYC - 1 else None)
            pb.wait("vector", s_t)
            pb.op("vector", "tensor_copy", ostage[:, j * D:j * D + DH],
                  ps_a[:, 0:DH], inc=(s_v, 1))
            pb.op("vector", "tensor_copy", ostage[:, j * D + DH:(j + 1) * D],
                  ps_b[:, 0:DH], inc=(s_v, 1))
            pb.wait("tensor", s_v)
        pb.wait("sync", s_v)
        for j in range(G // 128):
            pb.dma("sync",
                   dr_oa_in[g * G + j * 128:g * G + (j + 1) * 128, :],
                   ostage[:, j * D:(j + 1) * D], s_ld)
    pb.raw("gpsimd", lambda g: None)
    pb.wait_dma("gpsimd", s_ld)
    pb.op("gpsimd", "collective_compute", "ReduceScatter", ALU.add,
          replica_groups=cfg["groups"], ins=[dr_oa_in], outs=[dr_oa_my],
          inc=(s_cc, 1))
    pb.wait("sync", s_cc)
    for t in range(NT):
        pb.wait("sync", s_v)
        pb.wait_dma("sync", s_ld)
        lvo = pb.dma("sync", obf[:, :], dr_oa_my[t * 128:(t + 1) * 128, :],
                     s_ld)
        lvx = pb.dma("sync", sq[:, :], dr_proc[t * 128:(t + 1) * 128, :],
                     s_ld)
        pb.wait_dma("vector", s_ld, lvx)
        pb.op("vector", "tensor_copy", otok[:, :], obf[:, :], inc=(s_v, 1))
        pb.drain("vector")
        pb.op("vector", "tensor_add", sq[:, :], sq[:, :], otok[:, :],
              inc=(s_v, 1))
        pb.wait("sync", s_v)
        pb.dma("sync", dr_proc[t * 128:(t + 1) * 128, :], sq[:, :], s_ld)
    pb.wait_dma("sync", s_ld)
    pb.barrier()
    pb.pop_scope()
    pb.push_scope()
    K.pb.label = "dec_ffn1"
    emit_dec_ffn(K, dr_proc, "d_ln21", "d_wg1", "d_wu1", "d_wd1")
    pb.pop_scope()
    pb.push_scope()
    K.pb.label = "router"
    emit_router_dec2(K)
    pb.pop_scope()


def emit_dec_ffn(K, dr_x, ln_name, wg_name, wu_name, wd_name,
                 pre="f1"):
    cfg, pb = K.cfg, K.pb
    TL, D = cfg["TL"], cfg["D"]
    FFN = cfg.get("FFN", 4 * D)
    NT = TL // 128
    KC = D // 128
    FC = FFN // 128
    G2 = 512
    NGL = TL // G2
    GT2 = G2 // 128

    sq = pb.sbuf(pre + "_sq", [128, D], F32)
    sq2 = pb.sbuf(pre + "_sq2", [128, D], F32)
    stats = pb.sbuf(pre + "_stats", [128, 8], F32)
    lnrow = pb.sbuf(pre + "_lnrow", [128, D], F32)
    h2tok = pb.sbuf(pre + "_h2", [128, D], BF16)
    hfm = pb.sbuf(pre + "_hfm", [128, KC * G2], BF16)
    act = pb.sbuf(pre + "_act", [128, FC * G2], BF16)
    wgc = pb.sbuf(pre + "_wgc", [128, KC * 128 * 2 * 2], BF16)
    wdc = pb.sbuf(pre + "_wdc", [128, 2 * FFN], BF16)
    wstg = pb.sbuf(pre + "_wstg", [128, KC * 128 * 2], F32)
    wstgd = pb.sbuf(pre + "_wstgd", [128, FFN], F32)
    gtile = pb.sbuf(pre + "_gt", [128, G2], F32)
    otok = pb.sbuf(pre + "_otok", [128, D], F32)
    ofm = pb.sbuf(pre + "_ofm", [128, KC * G2], BF16)

    ps_t = pb.psum(pre + "_pst", [128, 1024], BF16)
    ps_a = pb.psum(pre + "_psa", [128, 512], F32)
    ps_b = pb.psum(pre + "_psb", [128, 512], F32)
    ps_c = pb.psum(pre + "_psc", [128, 512], F32)

    s_ld = pb.sem(pre + "_ld")
    s_v = pb.sem(pre + "_v")
    s_s = pb.sem(pre + "_s")
    s_t = pb.sem(pre + "_t")
    s_g = pb.sem(pre + "_g")
    gld = pb.sem("gld")
    lns = pb.sem("ln_s")

    lv = pb.dma("sync", lnrow[0:1, :], K.inputs[ln_name], s_ld)
    pb.wait_dma("gpsimd", s_ld, lv)
    pb.op("gpsimd", "partition_broadcast", lnrow[:, :], lnrow[0:1, :],
          channels=128, inc=(s_g, 1))
    pb.wait("vector", s_g)

    s_wl = pb.sem(pre + "_wl")
    s_ca = pb.sem(pre + "_ca")
    CH = KC * 128 * 2
    guards = []
    ldtok = {}
    caval_prev = {}
    dtok = {}
    dcav = {}
    dguards = []

    def issue_gu(gq):
        gi2, fc = divmod(gq, FC)
        if gi2 >= NGL:
            return
        if gq >= 1:
            pb.wait("sync", s_ca, caval_prev[gq - 1])
        pb.dma("sync", wstg[:, 0:KC * 128], K.inputs[wg_name][fc], s_wl)
        ldtok[gq] = pb.dma("sync", wstg[:, KC * 128:2 * KC * 128],
                           K.inputs[wu_name][fc], s_wl)

    def issue_wd(gq):
        gi2, oc = divmod(gq, KC)
        if gi2 >= NGL:
            return
        if gq >= 1:
            pb.wait("sync", s_ca, dcav[gq - 1])
        dtok[gq] = pb.dma("sync", wstgd[:, 0:FFN],
                          K.inputs[wd_name][oc], s_wl)

    issue_gu(0)
    for gi in range(NGL):
        for q in range(GT2):
            t = gi * GT2 + q
            pb.wait_dma("sync", s_ld)
            pb.wait("sync", s_v)
            lv = pb.dma("sync", sq[:, :], dr_x[t * 128:(t + 1) * 128, :],
                        s_ld)
            pb.wait_dma("vector", s_ld, lv)
            rmsnorm_tile(pb, K, sq[:, :], h2tok[:, :], lnrow[:, :], stats,
                         sq2, D, s_v, lns, inc=(s_v, 1))
            pb.wait("tensor", s_v)
            for c in range(KC):
                slot = c % 4
                dps = ps_t[:, slot * 128:slot * 128 + 128]
                tile_transpose(K, h2tok[:, c * 128:(c + 1) * 128], dps,
                               inc=(s_t, 1))
                pb.wait("vector", s_t)
                pb.op("vector", "tensor_copy",
                      hfm[:, c * G2 + q * 128:c * G2 + q * 128 + 128], dps,
                      inc=(s_v, 1))
                pb.wait("tensor", s_v)
        for fc in range(FC):
            gq = gi * FC + fc
            slot = gq % 2
            pb.wait_dma("scalar", s_wl, ldtok[gq])
            if gq >= 2:
                gsem, gval = guards[gq - 2]
                pb.wait("scalar", gsem, gval)
            caval_prev[gq] = pb.op(
                "scalar", "activation", wgc[:, slot * CH:(slot + 1) * CH],
                wstg[:, 0:CH], AF.Copy, inc=(s_ca, 1))
            pb.wait("tensor", s_ca, caval_prev[gq])
            issue_gu(gq + 1)
            if fc == FC - 1:
                issue_wd(gi * KC)
            for half in range(GT2 * 128 // 512):
                ps = ps_a
                psu = ps_b
                for c in range(KC):
                    stat = wgc[:, slot * CH + c * 128:
                               slot * CH + (c + 1) * 128]
                    pb.op("tensor", "ldweights", stat)
                    pb.op("tensor", "matmul", ps[:, 0:512], stat,
                          hfm[:, c * G2 + half * 512:c * G2 + (half + 1) * 512],
                          start=(c == 0), stop=(c == KC - 1),
                          inc=(s_t, 1) if c == KC - 1 else None)
                for c in range(KC):
                    stat = wgc[:, slot * CH + KC * 128 + c * 128:
                               slot * CH + KC * 128 + (c + 1) * 128]
                    pb.op("tensor", "ldweights", stat)
                    pb.op("tensor", "matmul", psu[:, 0:512], stat,
                          hfm[:, c * G2 + half * 512:c * G2 + (half + 1) * 512],
                          start=(c == 0), stop=(c == KC - 1),
                          inc=(s_t, 1) if c == KC - 1 else None)
                pb.wait("scalar", s_t)
                pb.op("scalar", "activation", gtile[:, 0:512], ps[:, 0:512],
                      AF.Silu, inc=(s_s, 1))
                pb.wait("vector", s_s)
                pb.op("vector", "tensor_tensor",
                      act[:, fc * G2 + half * 512:fc * G2 + (half + 1) * 512],
                      gtile[:, 0:512], psu[:, 0:512], ALU.mult, inc=(s_v, 1))
                pb.wait("tensor", s_v)
            guards.append((s_t, s_t.n))
        for oc in range(KC):
            gq = gi * KC + oc
            slot = gq % 2
            pb.wait_dma("scalar", s_wl, dtok[gq])
            if gq >= 2:
                gsem, gval = dguards[gq - 2]
                pb.wait("scalar", gsem, gval)
            dcav[gq] = pb.op(
                "scalar", "activation", wdc[:, slot * FFN:(slot + 1) * FFN],
                wstgd[:, 0:FFN], AF.Copy, inc=(s_ca, 1))
            pb.wait("tensor", s_ca, dcav[gq])
            if oc < KC - 1:
                issue_wd(gq + 1)
            for half in range(GT2 * 128 // 512):
                ps = ps_c
                for fc in range(FC):
                    stat = wdc[:, slot * FFN + fc * 128:
                               slot * FFN + (fc + 1) * 128]
                    pb.op("tensor", "ldweights", stat)
                    pb.op("tensor", "matmul", ps[:, 0:512], stat,
                          act[:, fc * G2 + half * 512:fc * G2 + (half + 1) * 512],
                          start=(fc == 0), stop=(fc == FC - 1),
                          inc=(s_t, 1) if fc == FC - 1 else None)
                pb.wait("vector", s_t)
                pb.op("vector", "tensor_copy",
                      ofm[:, oc * G2 + half * 512:oc * G2 + (half + 1) * 512],
                      ps[:, 0:512], inc=(s_v, 1))
                pb.wait("tensor", s_v)
            dguards.append((s_t, s_t.n))
        for q in range(GT2):
            t = gi * GT2 + q
            pb.wait("tensor", s_v)
            for c in range(KC):
                slot = c % 4
                dps = ps_t[:, 512 + slot * 128:512 + slot * 128 + 128]
                tile_transpose(K, ofm[:, c * G2 + q * 128:c * G2 + q * 128 + 128],
                               dps, inc=(s_t, 1))
                pb.wait("vector", s_t)
                pb.op("vector", "tensor_copy",
                      otok[:, c * 128:(c + 1) * 128], dps, inc=(s_v, 1))
                pb.wait("tensor", s_v)
            pb.wait_dma("sync", s_ld)
            pb.wait("sync", s_v)
            lv = pb.dma("sync", sq[:, :], dr_x[t * 128:(t + 1) * 128, :],
                        s_ld)
            pb.wait_dma("vector", s_ld, lv)
            pb.op("vector", "tensor_add", sq[:, :], sq[:, :], otok[:, :],
                  inc=(s_v, 1))
            pb.wait("sync", s_v)
            pb.dma("sync", dr_x[t * 128:(t + 1) * 128, :], sq[:, :], s_ld)
    pb.wait_dma("sync", s_ld)
    pb.barrier()


def psum_sel(K, fc):
    return K._gather_ps[fc % 4]


def emit_router_dec2(K):
    cfg, pb = K.cfg, K.pb
    T, TL, D, HL, HD, G = (cfg["T"], cfg["TL"], cfg["D"], cfg["HL"],
                           cfg["HD"], cfg["G"])
    FFN = cfg.get("FFN", 4 * D)
    S2 = T // 4
    SC = S2 // 128
    W2H = min(512, S2)
    NT = TL // 128
    KC = D // 128
    MY = HL * HD
    MYC = MY // 128
    HALF = HD // 2
    VW = HL * (HD + 1)
    G2 = min(512, S2)
    NG2 = S2 // G2
    GT2 = G2 // 128
    FC = FFN // 128
    eps = 1e-5
    Xd = K.x_dram
    dr_proc = K.d_proc
    dr_hs = K.d_hs

    hs_t = pb.sbuf("r_hst", [128, D], BF16)
    sq = pb.sbuf("r_sq", [128, D], F32)
    sq2 = pb.sbuf("r_sq2", [128, D], F32)
    stats = pb.sbuf("r_stats", [128, 8], F32)
    mean_c = pb.sbuf("r_mean", [128, 1], F32)
    rstd_c = pb.sbuf("r_rstd", [128, 1], F32)
    onescol = pb.sbuf("r_ones", [128, 1], F32)
    packC = pb.sbuf("r_packC", [128, TL], F32)
    myslot = packC[0:1, 0:TL]
    sc1 = pb.sbuf("r_sc1", [1, 8], F32)
    gth = pb.sbuf("r_gth", [1, 4], F32)
    slotcols = pb.sbuf("r_scol", [128, NT], F32)
    gcols = pb.sbuf("r_gcol", [128, NT], BF16)
    basecols = pb.sbuf("r_bcol", [128, NT], BF16)
    iotacol = pb.sbuf("r_iota", [128, 1], BF16)
    icf = pb.sbuf("r_icf", [128, 1], F32)
    gcf = pb.sbuf("r_gcf", [128, NT], F32)
    tpf = pb.sbuf("r_tpf", [128, NT], F32)
    selfm = pb.sbuf("r_selfm", [128, KC * S2], BF16)
    x2fm = pb.sbuf("r_x2fm", [128, KC * S2], BF16)
    h2fm = pb.sbuf("r_h2fm", [128, KC * S2], BF16)
    scb = pb.sbuf("r_scb", [128, S2], F32)
    scores_row = packC[64:65, 0:S2]
    pos_row = packC[96:97, 0:S2]
    otok = pb.sbuf("r_otok", [128, D], F32)
    rbf = pb.sbuf("r_rbf", [128, D], BF16)
    f32a = pb.sbuf("r_f32a", [128, 512], F32)
    f32b = pb.sbuf("r_f32b", [128, 512], F32)

    ps_t = pb.psum("r_pst", [128, 1024], BF16)
    ps_a = pb.psum("r_psa", [128, 512], F32)
    ps_b = pb.psum("r_psb", [128, 512], F32)
    ps_s0 = pb.psum("r_pss0", [128, 512], F32)
    ps_s1 = pb.psum("r_pss1", [128, 512], F32)
    ps_y = pb.psum("r_psy", [128, 512], F32)
    ps_y2r = pb.psum("r_psy2", [128, 512], F32)
    K._gather_ps = [ps_a, ps_b, ps_s0, ps_s1]

    dr_rt_in = K.dram_tmp("r_rt_in", [1, 2 * TL], F32)
    dr_rt_all = K.dram_tmp("r_rt_all", [2, 1, 2 * TL], F32)
    dr_slot = K.dram_tmp("r_slotd", [1, T], F32)
    dr_g = K.dram_tmp("r_g", [1, T], F32)
    dr_sel_in = K.dram_tmp("r_sel_in", [D, S2], BF16)
    dr_sel_out = K.dram_tmp("r_sel_out", [D, S2], BF16)
    dr_rows_in = K.dram_tmp("r_rows_in", [4, S2], F32)
    dr_rows_out = K.dram_tmp("r_rows_out", [4, S2], F32)
    dr_o2_in = K.dram_tmp("r_o2_in", [D, S2], BF16)
    dr_o2_out = K.dram_tmp("r_o2_out", [D, S2], BF16)
    dr_procb = K.dram_tmp("r_procb", [TL, D], BF16)

    s_ld = pb.sem("r_ld")
    s_v = pb.sem("r_v")
    s_s = pb.sem("r_s")
    s_t = pb.sem("r_t")
    s_g = pb.sem("r_g")
    s_cc = pb.sem("r_cc")
    lns = pb.sem("ln_s")
    gld = pb.sem("gld")

    pb.push_scope()
    packA = pb.sbuf("r_packA", [128, T], F32)
    packB = pb.sbuf("r_packB", [128, T], F32)
    grow = packA[0:1, 0:T]
    mrow = packA[32:33, 0:T]
    slotrow = packA[64:65, 0:T]
    zerorow = packA[96:97, 0:T]
    zrow = packB[0:1, 0:T]
    onesrow = packB[32:33, 0:T]
    tmprow = packB[64:65, 0:T]
    dstrow = packB[64:65, 0:TL]
    dchrow = packB[96:97, 0:TL]
    pb.op("vector", "memset", onescol[:, :], 1.0)
    pb.op("vector", "memset", zerorow[:, :], 0.0)
    pb.op("vector", "memset", onesrow[0:1, :], 1.0)
    pb.drain("vector")
    tpin = K.inputs["d_tpos"]
    lv = pb.dma("sync", tpf[:, :],
                tpin.rearrange("o (c p) -> (o p) c", p=128), s_ld,
                allow_slow_non_contiguous=True)
    pb.wait_dma("vector", s_ld, lv)

    pb.push_scope()
    pnrow = pb.sbuf("r_pnrow", [128, D], F32)
    pw = pb.sbuf("r_pw", [128, KC * 2 * D], BF16)
    res_fm = pb.sbuf("r_resfm", [128, KC * 512], BF16)
    ph_fm = pb.sbuf("r_phfm", [128, KC * 512], BF16)
    mlv = pb.sbuf("r_mlv", [128, 2 * KC * 512], BF16)

    lv = pb.dma("sync", pnrow[0:1, :], K.inputs["d_pnw"], s_ld)
    pb.wait_dma("gpsimd", s_ld, lv)
    pb.op("gpsimd", "partition_broadcast", pnrow[:, :], pnrow[0:1, :],
          channels=128, inc=(s_g, 1))
    pb.wait("vector", s_g)
    emit_weight_load(K, K.inputs["d_pw"], pw, D, 2 * D)
    pb.raw("tensor", lambda e: None)
    pb.wait_dma("tensor", gld)

    for gi in range(TL // 512):
        for q in range(4):
            t = gi * 4 + q
            pb.wait_dma("sync", s_ld)
            pb.wait("sync", s_v)
            lv = pb.dma("sync", sq[:, :], dr_proc[t * 128:(t + 1) * 128, :],
                        s_ld)
            lv = pb.dma("sync", hs_t[:, :], dr_hs[t * 128:(t + 1) * 128, :],
                        s_ld)
            pb.wait_dma("vector", s_ld, lv)
            pb.op("vector", "tensor_copy", rbf[:, :], sq[:, :], inc=(s_v, 1))
            pb.wait("sync", s_v)
            lvpb = pb.dma("sync", dr_procb[t * 128:(t + 1) * 128, :],
                          rbf[:, :], s_ld)
            pb.op("vector", "tensor_sub", sq[:, :], sq[:, :], hs_t[:, :])
            pb.drain("vector")
            pb.wait_dma("vector", s_ld, lvpb)
            pb.op("vector", "tensor_copy", rbf[:, :], sq[:, :], inc=(s_v, 1))
            pb.wait("tensor", s_v)
            for c in range(KC):
                slot = c % 4
                dps = ps_t[:, slot * 128:slot * 128 + 128]
                tile_transpose(K, rbf[:, c * 128:(c + 1) * 128], dps,
                               inc=(s_t, 1))
                pb.wait("vector", s_t)
                pb.op("vector", "tensor_copy",
                      res_fm[:, c * 512 + q * 128:c * 512 + q * 128 + 128],
                      dps, inc=(s_v, 1))
                pb.wait("tensor", s_v)
            pb.op("vector", "tensor_copy", sq[:, :], hs_t[:, :])
            pb.drain("vector")
            rmsnorm_tile(pb, K, sq[:, :], rbf[:, :], pnrow[:, :], stats,
                         otok, D, s_v, lns, eps=1e-6, inc=(s_v, 1))
            pb.wait("tensor", s_v)
            for c in range(KC):
                slot = c % 4
                dps = ps_t[:, 512 + slot * 128:512 + slot * 128 + 128]
                tile_transpose(K, rbf[:, c * 128:(c + 1) * 128], dps,
                               inc=(s_t, 1))
                pb.wait("vector", s_t)
                pb.op("vector", "tensor_copy",
                      ph_fm[:, c * 512 + q * 128:c * 512 + q * 128 + 128],
                      dps, inc=(s_v, 1))
                pb.wait("tensor", s_v)
        for mc in range(2 * KC):
            ps = ps_a if mc % 2 == 0 else ps_b
            for c in range(KC):
                stat = pw[:, c * 2 * D + mc * 128:c * 2 * D + (mc + 1) * 128]
                pb.op("tensor", "ldweights", stat)
                pb.op("tensor", "matmul", ps[:, 0:512], stat,
                      ph_fm[:, c * 512:(c + 1) * 512], start=(c == 0),
                      stop=(c == KC - 1),
                      inc=(s_t, 1) if c == KC - 1 else None)
            pb.wait("vector", s_t)
            pb.op("vector", "tensor_copy", mlv[:, mc * 512:(mc + 1) * 512],
                  ps[:, 0:512], inc=(s_v, 1))
            pb.wait("tensor", s_v)
        pb.wait("scalar", s_v)
        for c in range(KC):
            pb.op("scalar", "activation", f32a[:, :],
                  res_fm[:, c * 512:(c + 1) * 512], AF.Square, inc=(s_s, 1))
            pb.wait("tensor", s_s)
            pb.op("tensor", "matmul", ps_y[0:1, 0:512], onescol[:, 0:1],
                  f32a[:, :], start=(c == 0), stop=(c == KC - 1),
                  inc=(s_t, 1))
            pb.wait("scalar", s_t)
        pb.wait("vector", s_t)
        pb.op("vector", "tensor_scalar",
              dstrow[0:1, gi * 512:(gi + 1) * 512], ps_y[0:1, 0:512],
              1.0 / D, None, ALU.mult, inc=(s_v, 1))
        pb.wait("tensor", s_v)
        for c in range(KC):
            pb.wait("vector", s_s)
            pb.op("vector", "tensor_sub", f32a[:, :],
                  res_fm[:, c * 512:(c + 1) * 512],
                  mlv[:, c * 512:(c + 1) * 512], inc=(s_v, 1))
            pb.wait("scalar", s_v)
            pb.op("scalar", "activation", f32a[:, :], f32a[:, :], AF.Square,
                  inc=(s_s, 1))
            pb.op("scalar", "activation", f32b[:, :],
                  mlv[:, (KC + c) * 512:(KC + c + 1) * 512], AF.Exp,
                  scale=-1.0, inc=(s_s, 1))
            pb.wait("vector", s_s)
            pb.op("vector", "tensor_scalar", f32a[:, :], f32a[:, :], 1.0,
                  None, ALU.add)
            pb.drain("vector")
            pb.op("vector", "tensor_tensor", f32a[:, :], f32a[:, :],
                  f32b[:, :], ALU.mult)
            pb.drain("vector")
            pb.op("vector", "tensor_add", f32a[:, :], f32a[:, :],
                  mlv[:, (KC + c) * 512:(KC + c + 1) * 512])
            pb.drain("vector")
            pb.op("vector", "tensor_scalar", f32a[:, :], f32a[:, :], -1.0,
                  None, ALU.add, inc=(s_v, 1))
            pb.wait("tensor", s_v)
            pb.op("tensor", "matmul", ps_y[0:1, 0:512], onescol[:, 0:1],
                  f32a[:, :], start=(c == 0), stop=(c == KC - 1),
                  inc=(s_t, 1))
            pb.wait("vector", s_t)
        pb.op("vector", "tensor_scalar",
              dchrow[0:1, gi * 512:(gi + 1) * 512], ps_y[0:1, 0:512],
              0.5 / D, None, ALU.mult, inc=(s_v, 1))
        pb.wait("tensor", s_v)
        pb.wait("scalar", s_v)
    pb.pop_scope()

    pb.op("vector", "nop", inc=(s_v, 1))
    pb.wait("sync", s_v)
    pb.dma("sync", dr_rt_in[0:1, 0:TL], dstrow[:, :], s_ld)
    pb.dma("sync", dr_rt_in[0:1, TL:2 * TL], dchrow[:, :], s_ld)
    pb.raw("gpsimd", lambda g: None)
    pb.wait_dma("gpsimd", s_ld)
    pb.op("gpsimd", "collective_compute", "AllGather", ALU.bypass,
          replica_groups=cfg["groups"], ins=[dr_rt_in], outs=[dr_rt_all],
          inc=(s_cc, 1))
    pb.wait("sync", s_cc)
    for sl in range(2):
        pb.dma("sync", zrow[0:1, sl * TL:(sl + 1) * TL],
               dr_rt_all[sl, 0:1, 0:TL], s_ld)
        lv = pb.dma("sync", grow[0:1, sl * TL:(sl + 1) * TL],
                    dr_rt_all[sl, 0:1, TL:2 * TL], s_ld)
    pb.wait_dma("vector", s_ld, lv)
    pb.op("vector", "reduce_sum", sc1[0:1, 0:1], zrow[0:1, :], axis=AX.X)
    pb.op("vector", "reduce_sum", sc1[0:1, 1:2], grow[0:1, :], axis=AX.X)
    pb.drain("vector")
    pb.op("vector", "tensor_scalar", sc1[0:1, 0:1], sc1[0:1, 0:1], 1.0 / T,
          None, ALU.mult)
    pb.op("vector", "tensor_scalar", sc1[0:1, 1:2], sc1[0:1, 1:2], 1.0 / T,
          None, ALU.mult)
    pb.drain("vector")
    pb.op("vector", "tensor_add", zrow[0:1, :], zrow[0:1, :], grow[0:1, :])
    pb.drain("vector")
    pb.op("vector", "tensor_scalar", zrow[0:1, :], zrow[0:1, :],
          sc1[0:1, 0:1], None, ALU.subtract)
    pb.drain("vector")
    pb.op("vector", "tensor_scalar", zrow[0:1, :], zrow[0:1, :],
          sc1[0:1, 1:2], None, ALU.subtract, inc=(s_v, 1))
    pb.wait("scalar", s_v)
    pb.op("scalar", "activation", grow[0:1, :], zrow[0:1, :], AF.Sigmoid,
          inc=(s_s, 1))
    pb.wait("vector", s_s)
    pb.op("vector", "memset", gth[0:1, 0:1], 0.0)
    pb.op("vector", "memset", gth[0:1, 1:2], 1.0)
    pb.drain("vector")
    for it in range(42):
        pb.op("vector", "tensor_add", gth[0:1, 2:3], gth[0:1, 0:1],
              gth[0:1, 1:2])
        pb.drain("vector")
        pb.op("vector", "tensor_scalar", gth[0:1, 2:3], gth[0:1, 2:3], 0.5,
              None, ALU.mult)
        pb.drain("vector")
        pb.op("vector", "tensor_scalar", zrow[0:1, :], grow[0:1, :],
              gth[0:1, 2:3], 0.0, ALU.is_ge, ALU.add,
              accum_out=sc1[0:1, 2:3])
        pb.drain("vector")
        pb.op("vector", "tensor_scalar", gth[0:1, 3:4], sc1[0:1, 2:3],
              float(S2), None, ALU.is_ge)
        pb.drain("vector")
        pb.op("vector", "tensor_sub", sc1[0:1, 3:4], gth[0:1, 2:3],
              gth[0:1, 0:1])
        pb.drain("vector")
        pb.op("vector", "tensor_tensor", sc1[0:1, 3:4], sc1[0:1, 3:4],
              gth[0:1, 3:4], ALU.mult)
        pb.drain("vector")
        pb.op("vector", "tensor_add", gth[0:1, 0:1], gth[0:1, 0:1],
              sc1[0:1, 3:4])
        pb.drain("vector")
        pb.op("vector", "tensor_sub", sc1[0:1, 3:4], gth[0:1, 2:3],
              gth[0:1, 1:2])
        pb.drain("vector")
        pb.op("vector", "tensor_scalar", sc1[0:1, 4:5], gth[0:1, 3:4], -1.0,
              1.0, ALU.mult, ALU.add)
        pb.drain("vector")
        pb.op("vector", "tensor_tensor", sc1[0:1, 3:4], sc1[0:1, 3:4],
              sc1[0:1, 4:5], ALU.mult)
        pb.drain("vector")
        pb.op("vector", "tensor_add", gth[0:1, 1:2], gth[0:1, 1:2],
              sc1[0:1, 3:4])
        pb.drain("vector")
    pb.op("vector", "tensor_scalar", mrow[0:1, :], grow[0:1, :],
          gth[0:1, 0:1], None, ALU.is_ge)
    pb.drain("vector")
    pb.op("vector", "tensor_tensor_scan", slotrow[0:1, :], onesrow[0:1, :],
          mrow[0:1, :], 0.0, ALU.mult, ALU.add)
    pb.drain("vector")
    pb.op("vector", "tensor_scalar", tmprow[0:1, :], mrow[0:1, :], 1.0e5,
          -1.0e5, ALU.mult, ALU.add)
    pb.drain("vector")
    pb.op("vector", "tensor_tensor", slotrow[0:1, :], slotrow[0:1, :],
          tmprow[0:1, :], ALU.add)
    pb.drain("vector")
    pb.op("vector", "tensor_scalar", slotrow[0:1, :], slotrow[0:1, :], -1.0,
          None, ALU.add, inc=(s_v, 1))
    pb.wait("sync", s_v)
    pb.dma("sync", dr_slot[:, :], slotrow[0:1, :], s_ld)
    pb.dma("sync", dr_g[:, :], grow[0:1, :], s_ld)
    pb.wait_dma("sync", s_ld)

    def bfr(buf, dr, cols=False):
        def bf(e, buf=buf, dr=dr, cols=cols):
            mm = e.partition_id() % 2
            if cols:
                src = dr[0:1, bass.ts(mm, TL)].rearrange(
                    "o (c p) -> (o p) c", p=128)
            else:
                src = dr[0:1, bass.ts(mm, TL)]
            return (buf, src)
        return bf
    pb.dma_dyn("sync", s_ld, bfr(myslot[0:1, :], dr_slot))
    pb.dma_dyn("sync", s_ld, bfr(slotcols[:, :], dr_slot, cols=True))
    lvv = pb.dma_dyn("sync", s_ld, bfr(gcf[:, :], dr_g, cols=True))
    pb.wait_dma("vector", s_ld, lvv)
    pb.wait_dma("gpsimd", s_ld, lvv)
    pb.op("vector", "tensor_copy", gcols[:, :], gcf[:, :])
    pb.drain("vector")
    pb.op("gpsimd", "iota", icf[:, 0:1], pattern=[[1, 1]], base=0,
          channel_multiplier=1, allow_small_or_imprecise_dtypes=True,
          inc=(s_g, 1))
    pb.wait("vector", s_g)
    pb.op("vector", "tensor_scalar", tpf[:, :], tpf[:, :], icf[:, 0:1],
          None, ALU.subtract)
    pb.drain("vector")
    pb.op("vector", "tensor_copy", basecols[:, :], tpf[:, :])
    pb.drain("vector")
    pb.op("vector", "tensor_copy", iotacol[:, 0:1], icf[:, 0:1])
    pb.drain("vector")

    if cfg.get("dbg_d"):
        dbs = pb.sem("dbgd_st")
        o = K.nc.dram_tensor("dbg_proc", [TL, D], F32,
                             kind="ExternalOutput").ap()
        pb.dma("sync", o[:, :], dr_proc[:, :], dbs)
        for nm, buf in (("dbg_g", grow), ("dbg_m", mrow),
                        ("dbg_slot", slotrow)):
            o = K.nc.dram_tensor(nm, [1, T], F32, kind="ExternalOutput").ap()
            pb.dma("sync", o[0:1, :], buf[0:1, :], dbs)
        o = K.nc.dram_tensor("dbg_th", [1, 4], F32,
                             kind="ExternalOutput").ap()
        pb.dma("sync", o[0:1, :], gth[0:1, :], dbs)
        pb.wait_dma("sync", dbs)
    pb.push_scope()
    jbc = pb.sbuf("r_jbc", [128, S2], F32)
    jrow = pb.sbuf("r_jrow", [1, S2], F32)
    pt = pb.sbuf("r_pt", [128, NT * S2], BF16)
    ptile_b = pb.sbuf("r_ptileb", [128, 1024], BF16)
    selp = pb.sbuf("r_selp", [128, KC * S2], BF16)

    pb.op("gpsimd", "iota", jrow[0:1, :], pattern=[[1, S2]], base=0,
          channel_multiplier=0, allow_small_or_imprecise_dtypes=True)
    pb.drain("gpsimd")
    pb.op("gpsimd", "partition_broadcast", jbc[:, :], jrow[0:1, :],
          channels=128, inc=(s_g, 1))
    pb.wait("vector", s_g)
    for tc in range(NT):
        pb.op("vector", "tensor_scalar", pt[:, tc * S2:(tc + 1) * S2],
              jbc[:, :], slotcols[:, tc:tc + 1], None, ALU.is_equal)
        pb.drain("vector")
    pb.op("vector", "nop", inc=(s_v, 1))
    pb.wait("tensor", s_v)
    NFG = max(1, KC // 4)
    s_ptb = pb.sem("r_ptb")
    it = 0
    guards = []
    for half in range(S2 // W2H):
        for fcg in range(NFG):
            fcs = [fcg * 4 + i for i in range(min(4, KC))]
            for tc in range(NT):
                slot = it % 2
                if it >= 2:
                    gsem, gval = guards[it - 2]
                    pb.wait("sync", gsem, gval)
                lv = pb.dma("sync",
                            ptile_b[:, slot * 512:slot * 512 +
                                    len(fcs) * 128],
                            dr_procb[tc * 128:(tc + 1) * 128,
                                     fcs[0] * 128:(fcs[-1] + 1) * 128],
                            s_ld)
                pb.wait_dma("tensor", s_ld, lv)
                for i, fc in enumerate(fcs):
                    stat = ptile_b[:, slot * 512 + i * 128:
                                   slot * 512 + (i + 1) * 128]
                    last = (i == len(fcs) - 1)
                    pb.op("tensor", "ldweights", stat)
                    pb.op("tensor", "matmul", psum_sel(K, fc)[:, 0:W2H],
                          stat,
                          pt[:, tc * S2 + half * W2H:
                             tc * S2 + (half + 1) * W2H],
                          start=(tc == 0), stop=(tc == NT - 1),
                          inc=(s_t, 1) if tc == NT - 1 else
                          ((s_ptb, 1) if last else None))
                sem = s_t if tc == NT - 1 else s_ptb
                guards.append((sem, sem.n))
                it += 1
            for fc in fcs:
                pb.wait("vector", s_t)
                pb.op("vector", "tensor_copy",
                      selp[:, fc * S2 + half * W2H:
                           fc * S2 + (half + 1) * W2H],
                      psum_sel(K, fc)[:, 0:W2H], inc=(s_v, 1))
            pb.wait("tensor", s_v)
    for half in range(S2 // W2H):
        for tc in range(NT):
            pb.op("tensor", "ldweights", gcols[:, tc:tc + 1])
            pb.op("tensor", "matmul", ps_y[0:1, 0:W2H], gcols[:, tc:tc + 1],
                  pt[:, tc * S2 + half * W2H:tc * S2 + (half + 1) * W2H],
                  start=(tc == 0), stop=(tc == NT - 1),
                  inc=(s_t, 1) if tc == NT - 1 else None)
        pb.wait("vector", s_t)
        pb.op("vector", "tensor_copy",
              scores_row[0:1, half * W2H:(half + 1) * W2H],
              ps_y[0:1, 0:W2H], inc=(s_v, 1))
        pb.wait("tensor", s_v)
    for half in range(S2 // W2H):
        for tc in range(NT):
            pb.op("tensor", "ldweights", basecols[:, tc:tc + 1])
            pb.op("tensor", "matmul", ps_y[0:1, 0:W2H],
                  basecols[:, tc:tc + 1],
                  pt[:, tc * S2 + half * W2H:tc * S2 + (half + 1) * W2H],
                  start=(tc == 0), stop=False)
        for tc in range(NT):
            pb.op("tensor", "ldweights", iotacol[:, 0:1])
            pb.op("tensor", "matmul", ps_y[0:1, 0:W2H], iotacol[:, 0:1],
                  pt[:, tc * S2 + half * W2H:tc * S2 + (half + 1) * W2H],
                  start=False, stop=(tc == NT - 1),
                  inc=(s_t, 1) if tc == NT - 1 else None)
        pb.wait("vector", s_t)
        pb.op("vector", "tensor_copy",
              pos_row[0:1, half * W2H:(half + 1) * W2H],
              ps_y[0:1, 0:W2H], inc=(s_v, 1))
        pb.wait("tensor", s_v)
    pb.wait("sync", s_v)
    for fc in range(KC):
        pb.dma("sync", dr_sel_in[fc * 128:(fc + 1) * 128, :],
               selp[:, fc * S2:(fc + 1) * S2], s_ld)
    pb.dma("sync", dr_rows_in[0:1, :], scores_row[0:1, :], s_ld)
    pb.dma("sync", dr_rows_in[1:2, :], pos_row[0:1, :], s_ld)
    pb.dma("sync", dr_rows_in[2:3, :], zerorow[0:1, 0:S2], s_ld)
    pb.dma("sync", dr_rows_in[3:4, :], zerorow[0:1, 0:S2], s_ld)
    pb.raw("gpsimd", lambda g: None)
    pb.wait_dma("gpsimd", s_ld)
    pb.op("gpsimd", "collective_compute", "AllReduce", ALU.add,
          replica_groups=cfg["groups"], ins=[dr_sel_in], outs=[dr_sel_out],
          inc=(s_cc, 1))
    pb.op("gpsimd", "collective_compute", "AllReduce", ALU.add,
          replica_groups=cfg["groups"], ins=[dr_rows_in],
          outs=[dr_rows_out], inc=(s_cc, 1))
    pb.wait("sync", s_cc)
    for fc in range(KC):
        pb.dma("sync", selfm[:, fc * S2:(fc + 1) * S2],
               dr_sel_out[fc * 128:(fc + 1) * 128, :], s_ld)
    pb.dma("sync", scores_row[0:1, :], dr_rows_out[0:1, :], s_ld)
    lv = pb.dma("sync", pos_row[0:1, :], dr_rows_out[1:2, :], s_ld)
    pb.wait_dma("vector", s_ld, lv)
    pb.wait_dma("scalar", s_ld, lv)
    pb.wait_dma("gpsimd", s_ld, lv)
    pb.wait_dma("tensor", s_ld, lv)
    if cfg.get("dbg_d"):
        dbs = pb.sem("dbgd_st")
        o = K.nc.dram_tensor("dbg_pt", [128, NT * S2], BF16,
                             kind="ExternalOutput").ap()
        pb.dma("sync", o[:, :], pt[:, :], dbs)
        o = K.nc.dram_tensor("dbg_scol", [128, NT], F32,
                             kind="ExternalOutput").ap()
        pb.dma("sync", o[:, :], slotcols[:, :], dbs)
        o = K.nc.dram_tensor("dbg_sel", [KC * 128, S2], BF16,
                             kind="ExternalOutput").ap()
        for fc in range(KC):
            pb.dma("sync", o[fc * 128:(fc + 1) * 128, :],
                   selfm[:, fc * S2:(fc + 1) * S2], dbs)
        o = K.nc.dram_tensor("dbg_rows", [2, S2], F32,
                             kind="ExternalOutput").ap()
        pb.dma("sync", o[0:1, :], scores_row[0:1, :], dbs)
        pb.dma("sync", o[1:2, :], pos_row[0:1, :], dbs)
        pb.wait_dma("sync", dbs)
    pb.pop_scope()
    pb.pop_scope()
    pb.barrier()
    K.pb.label = "dec2"
    emit_dec2(K, selfm, (scores_row, pos_row), locals())


def emit_dec2(K, selfm, rows_p, env):
    cfg, pb = K.cfg, K.pb
    T, TL, D, HL, HD = cfg["T"], cfg["TL"], cfg["D"], cfg["HL"], cfg["HD"]
    FFN = cfg.get("FFN", 4 * D)
    S2 = T // 4
    SC = S2 // 128
    NT = TL // 128
    KC = D // 128
    MY = HL * HD
    MYC = MY // 128
    HALF = HD // 2
    VW = HL * (HD + 1)
    G2 = min(512, S2)
    W2H = G2
    NG2 = S2 // G2
    GT2 = G2 // 128
    FC = FFN // 128
    eps = 1e-5
    Xd = K.x_dram
    dr_proc = K.d_proc
    ps_t, ps_a, ps_b = env["ps_t"], env["ps_a"], env["ps_b"]
    ps_s0, ps_s1, ps_y = env["ps_s0"], env["ps_s1"], env["ps_y"]
    ps_t2y = env["ps_y2r"]
    s_ld, s_v, s_s, s_t = env["s_ld"], env["s_v"], env["s_s"], env["s_t"]
    s_g, s_cc, lns, gld = env["s_g"], env["s_cc"], env["lns"], env["gld"]
    sq, sq2, onescol = env["sq"], env["sq2"], env["onescol"]
    myslot, otok, stats = env["myslot"], env["otok"], env["stats"]
    mean_c, rstd_c = env["mean_c"], env["rstd_c"]
    dr_o2_in, dr_o2_out = env["dr_o2_in"], env["dr_o2_out"]
    zerorow = env["zerorow"]

    pb.push_scope()
    w2 = pb.sbuf("r2_w2", [128, KC * 3 * MY], BF16)
    wo2 = pb.sbuf("r2_wo2", [128, MYC * D], BF16)
    x2fm = env["x2fm"]
    h2fm = env["h2fm"]
    q2 = pb.sbuf("r2_q2", [128, MYC * S2], BF16)
    qpad2 = pb.sbuf("r2_qpad", [128, 2 * G2], BF16)
    k2 = pb.sbuf("r2_k2", [128, MYC * S2], BF16)
    v2tok = pb.sbuf("r2_v2t", [128, SC * VW], BF16)
    ang2 = pb.sbuf("r2_ang2", [HALF, S2], F32)
    c2 = pb.sbuf("r2_c2", [128, S2], BF16)
    s2t = pb.sbuf("r2_s2", [128, S2], BF16)
    invf2 = pb.sbuf("r2_invf2", [HALF, 1], F32)
    rr_row = pb.sbuf("r2_rrrow", [1, S2], F32)
    rrb = pb.sbuf("r2_rrb", [128, min(512, S2)], F32)
    ln12c = pb.sbuf("r2_ln12", [128, KC], F32)
    ln22c = pb.sbuf("r2_ln22", [128, KC], F32)
    ptile = pb.sbuf("r2_p", [128, 2 * G2], BF16)
    tri2 = pb.sbuf("r2_tri", [128, GT2 * G2], BF16)
    ygrp = pb.sbuf("r2_ygrp", [128, MYC * G2], BF16)
    denb = pb.sbuf("r2_denb", [HD, G2], F32)
    den1 = pb.sbuf("r2_den1", [1, G2], F32)
    o2st = pb.sbuf("r2_o2st", [128, 2 * S2], BF16)
    scb = env["scb"]
    ropesc = pb.sbuf("r2_ropesc", [128, S2], BF16)

    lv = pb.dma("sync", ln12c[:, :], K.inputs["d_ln12c"], s_ld)
    lv = pb.dma("sync", ln22c[:, :], K.inputs["d_ln22c"], s_ld)
    emit_weight_load(K, K.inputs["d_w2"], w2, D, 3 * MY)
    emit_weight_load(K, K.inputs["d_wo2"], wo2, MY, D)
    pb.wait_dma("vector", s_ld, lv)

    pb.op("vector", "tensor_copy", rr_row[0:1, 0:S2], rows_p[1][0:1, :],
          inc=(s_v, 1))
    pb.op("gpsimd", "iota", invf2[:, 0:1], pattern=[[1, 1]], base=0,
          channel_multiplier=1, allow_small_or_imprecise_dtypes=True)
    pb.drain("gpsimd")
    pb.raw("gpsimd", lambda g: None)
    pb.wait("gpsimd", s_v)
    pb.op("gpsimd", "partition_broadcast", ang2[:, :], rr_row[0:1, 0:S2],
          channels=HALF, inc=(s_g, 1))
    pb.wait("vector", s_g)
    pb.wait("scalar", s_g)
    pb.op("scalar", "activation", invf2[:, 0:1], invf2[:, 0:1], AF.Exp,
          scale=-math.log(10000.0) / HALF, inc=(s_s, 1))
    pb.wait("vector", s_s)
    pb.wait("vector", s_g)
    pb.op("vector", "tensor_scalar", ang2[:, :], ang2[:, :], invf2[:, 0:1],
          None, ALU.mult)
    pb.drain("vector")
    emit_sincos(K, ang2[0:HALF, :], c2[0:HALF, :], s2t[0:HALF, :], HALF,
                S2)
    pb.op("vector", "nop", inc=(s_v, 1))
    pb.wait("sync", s_v)
    for rr in (32, 64, 96):
        pb.dma("sync", c2[rr:rr + HALF, :], c2[0:HALF, :], s_ld)
        lvc2 = pb.dma("sync", s2t[rr:rr + HALF, :], s2t[0:HALF, :], s_ld)
    pb.wait_dma("vector", s_ld, lvc2)
    for j in range(GT2):
        sl = tri2[:, j * G2:(j + 1) * G2]
        pb.op("gpsimd", "memset", sl, 1.0)
        pb.drain("gpsimd")
        pb.op("gpsimd", "affine_select", sl, sl, pattern=[[1, G2]],
              base=-128 * j, channel_multiplier=-1, compare_op=ALU.is_ge,
              fill=0.0)
        pb.drain("gpsimd")
    pb.op("gpsimd", "nop", inc=(s_g, 1))
    pb.wait("vector", s_g)
    pb.wait("vector", s_g)
    pb.op("vector", "tensor_copy", rr_row[0:1, 0:S2], rows_p[0][0:1, :],
          inc=(s_v, 1))
    pb.raw("gpsimd", lambda g: None)
    pb.wait("gpsimd", s_v)
    pb.op("gpsimd", "partition_broadcast", scb[:, :], rr_row[0:1, 0:S2],
          channels=128, inc=(s_g, 1))
    pb.wait("vector", s_g)

    for half in range(S2 // W2H):
        for c in range(KC):
            pb.wait("scalar", s_v)
            pb.op("scalar", "activation", env["f32a"][:, 0:W2H],
                  selfm[:, c * S2 + half * W2H:c * S2 + (half + 1) * W2H],
                  AF.Square, inc=(s_s, 1))
            pb.wait("tensor", s_s)
            pb.op("tensor", "matmul", ps_y[0:1, 0:W2H], onescol[:, 0:1],
                  env["f32a"][:, 0:W2H], start=(c == 0), stop=(c == KC - 1),
                  inc=(s_t, 1))
            pb.wait("scalar", s_t)
        pb.wait("vector", s_t)
        pb.op("vector", "tensor_scalar", rr_row[0:1, 0:W2H],
              ps_y[0:1, 0:W2H], 1.0 / D, 1e-6, ALU.mult, ALU.add)
        pb.drain("vector")
        pb.op("vector", "reciprocal", rr_row[0:1, 0:W2H],
              rr_row[0:1, 0:W2H], inc=(s_v, 1))
        pb.wait("scalar", s_v)
        pb.op("scalar", "activation", rr_row[0:1, 0:W2H],
              rr_row[0:1, 0:W2H], AF.Sqrt, inc=(lns, 1))
        pb.wait("gpsimd", lns)
        pb.op("gpsimd", "partition_broadcast", rrb[:, 0:W2H],
              rr_row[0:1, 0:W2H], channels=128, inc=(s_g, 1))
        pb.wait("vector", s_g)
        for c in range(KC):
            pb.op("vector", "tensor_tensor", env["f32a"][:, 0:W2H],
                  selfm[:, c * S2 + half * W2H:c * S2 + (half + 1) * W2H],
                  rrb[:, 0:W2H], ALU.mult)
            pb.drain("vector")
            pb.op("vector", "tensor_scalar",
                  h2fm[:, c * S2 + half * W2H:c * S2 + (half + 1) * W2H],
                  env["f32a"][:, 0:W2H], ln12c[:, c:c + 1], None, ALU.mult)
            pb.drain("vector")
        pb.op("vector", "nop", inc=(s_v, 1))
    pb.wait("tensor", s_v)
    pb.wait_dma("tensor", gld)

    def rope_fm(dst, src, mc, width):
        for r in (0, 64):
            x1 = src[r:r + HALF, 0:width]
            x2 = src[r + HALF:r + 2 * HALF, 0:width]
            cc1 = c2[r:r + HALF, 0:width]
            cc2 = c2[r + HALF:r + 2 * HALF, 0:width]
            ss1 = s2t[r:r + HALF, 0:width]
            ss2 = s2t[r + HALF:r + 2 * HALF, 0:width]
            d1 = dst[r:r + HALF, 0:width]
            d2 = dst[r + HALF:r + 2 * HALF, 0:width]
            pb.op("vector", "tensor_tensor", d1, x1, cc1, ALU.mult)
            pb.op("vector", "tensor_tensor", d2, x2, cc2, ALU.mult)
            pb.drain("vector")
            pb.op("vector", "tensor_tensor", ropesc[r:r + HALF, 0:width],
                  x2, ss2, ALU.mult)
            pb.op("vector", "tensor_tensor",
                  ropesc[r + HALF:r + 2 * HALF, 0:width], x1, ss1, ALU.mult)
            pb.drain("vector")
            pb.op("vector", "tensor_sub", d1, d1,
                  ropesc[r:r + HALF, 0:width])
            pb.op("vector", "tensor_add", d2, d2,
                  ropesc[r + HALF:r + 2 * HALF, 0:width])
            pb.drain("vector")

    pb.op("vector", "memset", v2tok[:, :], 0.0)
    pb.drain("vector")
    for tc in range(SC):
        for h in range(HL):
            base = tc * VW + h * (HD + 1) + HD
            pb.op("vector", "memset", v2tok[:, base:base + 1], 1.0)
    pb.drain("vector")
    pb.op("vector", "nop", inc=(s_v, 1))
    pb.wait("tensor", s_v)
    for which, wofs, dstb in (("q", 0, q2), ("k", MY, k2), ("v", 2 * MY, None)):
        for mc in range(MYC):
            for half in range(S2 // W2H):
                ps = ps_a if (mc + half) % 2 == 0 else ps_b
                for c in range(KC):
                    stat = w2[:, c * 3 * MY + wofs + mc * 128:
                              c * 3 * MY + wofs + (mc + 1) * 128]
                    pb.op("tensor", "ldweights", stat)
                    pb.op("tensor", "matmul", ps[:, 0:W2H], stat,
                          h2fm[:, c * S2 + half * W2H:
                               c * S2 + (half + 1) * W2H],
                          start=(c == 0), stop=(c == KC - 1),
                          inc=(s_t, 1) if c == KC - 1 else None)
                pb.wait("vector", s_t)
                if which in ("q", "k"):
                    pb.op("vector", "tensor_copy",
                          sq[:, 0:W2H], ps[:, 0:W2H], inc=(s_v, 1))
                    pb.wait("tensor", s_v)
                    pb.drain("vector")
                    dst = dstb[:, mc * S2 + half * W2H:
                               mc * S2 + (half + 1) * W2H]
                    for r in (0, 64):
                        x1 = sq[r:r + HALF, 0:W2H]
                        x2 = sq[r + HALF:r + 2 * HALF, 0:W2H]
                        cc1 = c2[r:r + HALF, half * W2H:(half + 1) * W2H]
                        cc2 = c2[r + HALF:r + 2 * HALF,
                                 half * W2H:(half + 1) * W2H]
                        ss1 = s2t[r:r + HALF, half * W2H:(half + 1) * W2H]
                        ss2 = s2t[r + HALF:r + 2 * HALF,
                                  half * W2H:(half + 1) * W2H]
                        d1 = dst[r:r + HALF, 0:W2H]
                        d2 = dst[r + HALF:r + 2 * HALF, 0:W2H]
                        pb.op("vector", "tensor_tensor", d1, x1, cc1,
                              ALU.mult)
                        pb.op("vector", "tensor_tensor", d2, x2, cc2,
                              ALU.mult)
                        pb.drain("vector")
                        pb.op("vector", "tensor_tensor",
                              ropesc[r:r + HALF, 0:W2H], x2, ss2, ALU.mult)
                        pb.op("vector", "tensor_tensor",
                              ropesc[r + HALF:r + 2 * HALF, 0:W2H], x1, ss1,
                              ALU.mult)
                        pb.drain("vector")
                        pb.op("vector", "tensor_sub", d1, d1,
                              ropesc[r:r + HALF, 0:W2H])
                        pb.op("vector", "tensor_add", d2, d2,
                              ropesc[r + HALF:r + 2 * HALF, 0:W2H])
                        pb.drain("vector")
                    pb.op("vector", "nop", inc=(s_v, 1))
                    pb.wait("tensor", s_v)
                else:
                    pb.wait("vector", s_t)
                    pb.op("vector", "tensor_copy", ropesc[:, 0:W2H],
                          ps[:, 0:W2H], inc=(s_v, 1))
                    pb.wait("tensor", s_v)
                    for q in range(W2H // 128):
                        slot = q % 4
                        dps = ps_t[:, 512 + slot * 128:512 + slot * 128 + 128]
                        tile_transpose(K, ropesc[:, q * 128:(q + 1) * 128],
                                       dps, inc=(s_t, 1))
                        pb.wait("vector", s_t)
                        tc = half * 4 + q
                        for hh in range(128 // HD):
                            h = mc * (128 // HD) + hh
                            pb.op("vector", "tensor_copy",
                                  v2tok[:, tc * VW + h * (HD + 1):
                                       tc * VW + h * (HD + 1) + HD],
                                  dps[:, hh * HD:(hh + 1) * HD],
                                  inc=(s_v, 1))
                        pb.wait("tensor", s_v)
    pb.barrier()

    pb.op("vector", "memset", qpad2[:, :], 0.0)
    pb.drain("vector")
    K.att_pe_gate = {0: None, 1: None}
    K.att_exp_gate = {0: None, 1: None}
    K.att_ygate = {0: None, 1: None}
    for g in range(NG2):
        nkt = (g + 1) * GT2
        for h in range(HL):
            mc0, r0 = divmod(h * HD, 128)
            sl = r0 // 64
            hb = h % 2
            yslot = h % 2
            psy = ps_y if yslot == 0 else ps_t2y
            if K.att_pe_gate[hb] is not None:
                pb.wait("vector", s_t, K.att_pe_gate[hb])
            qpg = pb.op("vector", "tensor_copy",
                        qpad2[r0:r0 + HD, sl * G2:(sl + 1) * G2],
                        q2[r0:r0 + HD,
                           mc0 * S2 + g * G2:mc0 * S2 + (g + 1) * G2],
                        inc=(s_v, 1))
            scd = {}
            expd = {}
            mskd = {}
            pvd = {}

            def emit_scores(kt):
                slot = kt % 2
                ps_sc = ps_s0 if slot == 0 else ps_s1
                if kt == 0:
                    vgate = qpg
                    if K.att_ygate[yslot] is not None:
                        vgate = max(vgate, K.att_ygate[yslot])
                    pb.wait("tensor", s_v, vgate)
                if K.att_exp_gate[slot] is not None:
                    pb.wait("tensor", s_s, K.att_exp_gate[slot])
                stat = k2[:, mc0 * S2 + kt * 128:mc0 * S2 + (kt + 1) * 128]
                pb.op("tensor", "ldweights", stat)
                scd[kt] = pb.op("tensor", "matmul", ps_sc[:, 0:G2], stat,
                                qpad2[:, sl * G2:(sl + 1) * G2],
                                start=True, stop=True, inc=(s_t, 1))

            def emit_exp_mask(kt):
                slot = kt % 2
                ps_sc = ps_s0 if slot == 0 else ps_s1
                pb.wait("scalar", s_t, scd[kt])
                expd[kt] = pb.op("scalar", "activation",
                                 ptile[:, slot * G2:(slot + 1) * G2],
                                 ps_sc[:, 0:G2], AF.Exp,
                                 scale=1.0 / math.sqrt(HD), inc=(s_s, 1))
                K.att_exp_gate[slot] = expd[kt]
                diag = kt - g * GT2
                if diag >= 0:
                    pb.wait("vector", s_s, expd[kt])
                    mskd[kt] = pb.op("vector", "tensor_tensor",
                                     ptile[:, slot * G2:(slot + 1) * G2],
                                     ptile[:, slot * G2:(slot + 1) * G2],
                                     tri2[:, diag * G2:(diag + 1) * G2],
                                     ALU.mult, inc=(s_v, 1))
                else:
                    mskd[kt] = None

            def emit_pv(kt):
                slot = kt % 2
                if mskd[kt] is not None:
                    pb.wait("tensor", s_v, mskd[kt])
                else:
                    pb.wait("tensor", s_s, expd[kt])
                pb.op("tensor", "ldweights",
                      v2tok[:, kt * VW + h * (HD + 1):
                            kt * VW + (h + 1) * (HD + 1)])
                pvd[kt] = pb.op("tensor", "matmul", psy[0:HD + 1, 0:G2],
                                v2tok[:, kt * VW + h * (HD + 1):
                                      kt * VW + (h + 1) * (HD + 1)],
                                ptile[:, slot * G2:(slot + 1) * G2],
                                start=(kt == 0), stop=(kt == nkt - 1),
                                inc=(s_t, 1))

            for kt in range(nkt):
                emit_scores(kt)
                emit_exp_mask(kt)
                if kt >= 1:
                    emit_pv(kt - 1)
            emit_pv(nkt - 1)
            K.att_pe_gate[hb] = scd[nkt - 1]
            pb.wait("vector", s_t, pvd[nkt - 1])
            rv = pb.op("vector", "reciprocal", den1[0:1, 0:G2],
                       psy[HD:HD + 1, 0:G2], inc=(s_v, 1))
            pb.wait("gpsimd", s_v, rv)
            gv = pb.op("gpsimd", "partition_broadcast", denb[:, 0:G2],
                       den1[0:1, 0:G2], channels=HD, inc=(s_g, 1))
            pb.wait("vector", s_g, gv)
            K.att_ygate[yslot] = pb.op(
                "vector", "tensor_tensor",
                ygrp[r0:r0 + HD, mc0 * G2:(mc0 + 1) * G2],
                psy[0:HD, 0:G2], denb[:, 0:G2], ALU.mult, inc=(s_v, 1))
        pb.wait("tensor", s_v)
        for oc in range(KC):
            ps = ps_a if oc % 2 == 0 else ps_b
            for c in range(MYC):
                stat = wo2[:, c * D + oc * 128:c * D + (oc + 1) * 128]
                pb.op("tensor", "ldweights", stat)
                pb.op("tensor", "matmul", ps[:, 0:G2], stat,
                      ygrp[:, c * G2:(c + 1) * G2], start=(c == 0),
                      stop=(c == MYC - 1),
                      inc=(s_t, 1) if c == MYC - 1 else None)
            pb.wait("vector", s_t)
            if g > 0 or oc >= 2:
                pb.wait_dma("vector", s_ld)
            osl = o2st[:, (oc % 2) * S2:(oc % 2) * S2 + G2]
            pb.op("vector", "tensor_copy", osl, ps[:, 0:G2], inc=(s_v, 1))
            pb.wait("sync", s_v)
            pb.dma("sync",
                   dr_o2_in[oc * 128:(oc + 1) * 128, g * G2:(g + 1) * G2],
                   osl, s_ld)
            pb.wait("tensor", s_v)
    pb.raw("gpsimd", lambda g: None)
    pb.wait_dma("gpsimd", s_ld)
    pb.op("gpsimd", "collective_compute", "AllReduce", ALU.add,
          replica_groups=cfg["groups"], ins=[dr_o2_in], outs=[dr_o2_out],
          inc=(s_cc, 1))
    pb.wait("sync", s_cc)
    for oc in range(KC):
        if oc >= 2:
            pb.wait("sync", s_v)
        osl = o2st[:, (oc % 2) * S2:(oc % 2 + 1) * S2]
        lv = pb.dma("sync", osl, dr_o2_out[oc * 128:(oc + 1) * 128, :], s_ld)
        pb.wait_dma("vector", s_ld, lv)
        pb.op("vector", "tensor_add", x2fm[:, oc * S2:(oc + 1) * S2],
              selfm[:, oc * S2:(oc + 1) * S2], osl, inc=(s_v, 1))
    pb.op("vector", "nop", inc=(s_v, 1))
    if cfg.get("dbg_d"):
        dbs2 = pb.sem("dbgd2_st")
        o = K.nc.dram_tensor("dbg_x2", [KC * 128, S2], BF16,
                             kind="ExternalOutput").ap()
        pb.wait("sync", s_v)
        for fc in range(KC):
            pb.dma("sync", o[fc * 128:(fc + 1) * 128, :],
                   x2fm[:, fc * S2:(fc + 1) * S2], dbs2)
        o = K.nc.dram_tensor("dbg_q2", [128, 4 * S2], BF16,
                             kind="ExternalOutput").ap()
        pb.dma("sync", o[:, :], q2[:, 0:4 * S2], dbs2)
        o = K.nc.dram_tensor("dbg_k2", [128, 4 * S2], BF16,
                             kind="ExternalOutput").ap()
        pb.dma("sync", o[:, :], k2[:, 0:4 * S2], dbs2)
        o = K.nc.dram_tensor("dbg_c2", [128, S2], BF16,
                             kind="ExternalOutput").ap()
        pb.dma("sync", o[:, :], c2[:, :], dbs2)
        o = K.nc.dram_tensor("dbg_s2t", [128, S2], BF16,
                             kind="ExternalOutput").ap()
        pb.dma("sync", o[:, :], s2t[:, :], dbs2)
        o = K.nc.dram_tensor("dbg_h1", [KC * 128, S2], BF16,
                             kind="ExternalOutput").ap()
        for fc in range(KC):
            pb.dma("sync", o[fc * 128:(fc + 1) * 128, :],
                   h2fm[:, fc * S2:(fc + 1) * S2], dbs2)
        pb.wait_dma("sync", dbs2)
    for half in range(S2 // W2H):
        for c in range(KC):
            pb.wait("scalar", s_v)
            pb.op("scalar", "activation", env["f32a"][:, 0:W2H],
                  x2fm[:, c * S2 + half * W2H:c * S2 + (half + 1) * W2H],
                  AF.Square, inc=(s_s, 1))
            pb.wait("tensor", s_s)
            pb.op("tensor", "matmul", ps_y[0:1, 0:W2H], onescol[:, 0:1],
                  env["f32a"][:, 0:W2H], start=(c == 0), stop=(c == KC - 1),
                  inc=(s_t, 1))
            pb.wait("scalar", s_t)
        pb.wait("vector", s_t)
        pb.op("vector", "tensor_scalar", rr_row[0:1, 0:W2H],
              ps_y[0:1, 0:W2H], 1.0 / D, 1e-6, ALU.mult, ALU.add)
        pb.drain("vector")
        pb.op("vector", "reciprocal", rr_row[0:1, 0:W2H], rr_row[0:1, 0:W2H],
              inc=(s_v, 1))
        pb.wait("scalar", s_v)
        pb.op("scalar", "activation", rr_row[0:1, 0:W2H], rr_row[0:1, 0:W2H],
              AF.Sqrt, inc=(lns, 1))
        pb.wait("gpsimd", lns)
        pb.op("gpsimd", "partition_broadcast", rrb[:, 0:W2H],
              rr_row[0:1, 0:W2H], channels=128, inc=(s_g, 1))
        pb.wait("vector", s_g)
        for c in range(KC):
            pb.op("vector", "tensor_tensor", env["f32a"][:, 0:W2H],
                  x2fm[:, c * S2 + half * W2H:c * S2 + (half + 1) * W2H],
                  rrb[:, 0:W2H], ALU.mult)
            pb.drain("vector")
            pb.op("vector", "tensor_scalar",
                  h2fm[:, c * S2 + half * W2H:c * S2 + (half + 1) * W2H],
                  env["f32a"][:, 0:W2H], ln22c[:, c:c + 1], None, ALU.mult)
            pb.drain("vector")
        pb.op("vector", "nop", inc=(s_v, 1))
    pb.wait("tensor", s_v)
    pb.pop_scope()
    K.pb.label = "dec2_ffn"
    emit_dec2_ffn_scatter(K, env)


def emit_dec2_ffn_scatter(K, env):
    cfg, pb = K.cfg, K.pb
    T, TL, D = cfg["T"], cfg["TL"], cfg["D"]
    FFN = cfg.get("FFN", 4 * D)
    S2 = T // 4
    SC = S2 // 128
    W2H = min(512, S2)
    NT = TL // 128
    KC = D // 128
    FC = FFN // 128
    eps = 1e-5
    Xd = K.x_dram
    dr_proc = K.d_proc
    ps_t, ps_a, ps_b = env["ps_t"], env["ps_a"], env["ps_b"]
    ps_y = env["ps_y"]
    s_ld, s_v, s_s, s_t = env["s_ld"], env["s_v"], env["s_s"], env["s_t"]
    s_g, lns, gld = env["s_g"], env["lns"], env["gld"]
    sq, sq2 = env["sq"], env["sq2"]
    stats, mean_c, rstd_c = env["stats"], env["mean_c"], env["rstd_c"]
    otok, myslot = env["otok"], env["myslot"]
    selfm = env["selfm"]
    x2fm = env["x2fm"]
    h2fm = env["h2fm"]
    scb = env["scb"]

    pb.push_scope()
    act2 = pb.sbuf("r3_act2", [128, FC * W2H], BF16)
    wgc2 = pb.sbuf("r3_wgc2", [128, KC * 128 * 2 * 2], BF16)
    wdc2 = pb.sbuf("r3_wdc2", [128, 2 * FFN], BF16)
    wstg2 = pb.sbuf("r3_wstg2", [128, KC * 128 * 2], F32)
    wstgd2 = pb.sbuf("r3_wstgd2", [128, FFN], F32)
    gtile2 = pb.sbuf("r3_gt2", [128, W2H], F32)
    NG3 = S2 // W2H
    CH = KC * 128 * 2
    s_wl = pb.sem("r3_wl")
    s_ca = pb.sem("r3_ca")
    guards = []
    ldtok = {}
    caval = {}
    dtok = {}
    dcav = {}
    dguards = []

    def issue_gu(gq):
        gi2, fc = divmod(gq, FC)
        if gi2 >= NG3:
            return
        if gq >= 1:
            pb.wait("sync", s_ca, caval[gq - 1])
        pb.dma("sync", wstg2[:, 0:KC * 128], K.inputs["d_wg2"][fc], s_wl)
        ldtok[gq] = pb.dma("sync", wstg2[:, KC * 128:2 * KC * 128],
                           K.inputs["d_wu2"][fc], s_wl)

    def issue_wd(gq):
        gi2, oc = divmod(gq, KC)
        if gi2 >= NG3:
            return
        if gq >= 1:
            pb.wait("sync", s_ca, dcav[gq - 1])
        dtok[gq] = pb.dma("sync", wstgd2[:, 0:FFN],
                          K.inputs["d_wd2"][oc], s_wl)

    issue_gu(0)
    for g2 in range(NG3):
        for fc in range(FC):
            gq = g2 * FC + fc
            slot = gq % 2
            pb.wait_dma("scalar", s_wl, ldtok[gq])
            if gq >= 2:
                gsem, gval = guards[gq - 2]
                pb.wait("scalar", gsem, gval)
            caval[gq] = pb.op(
                "scalar", "activation", wgc2[:, slot * CH:(slot + 1) * CH],
                wstg2[:, 0:CH], AF.Copy, inc=(s_ca, 1))
            pb.wait("tensor", s_ca, caval[gq])
            issue_gu(gq + 1)
            if fc == FC - 1:
                issue_wd(g2 * KC)
            for c in range(KC):
                stat = wgc2[:, slot * CH + c * 128:slot * CH + (c + 1) * 128]
                pb.op("tensor", "ldweights", stat)
                pb.op("tensor", "matmul", ps_a[:, 0:W2H], stat,
                      h2fm[:, c * S2 + g2 * W2H:c * S2 + (g2 + 1) * W2H],
                      start=(c == 0), stop=(c == KC - 1),
                      inc=(s_t, 1) if c == KC - 1 else None)
            for c in range(KC):
                stat = wgc2[:, slot * CH + KC * 128 + c * 128:
                            slot * CH + KC * 128 + (c + 1) * 128]
                pb.op("tensor", "ldweights", stat)
                pb.op("tensor", "matmul", ps_b[:, 0:W2H], stat,
                      h2fm[:, c * S2 + g2 * W2H:c * S2 + (g2 + 1) * W2H],
                      start=(c == 0), stop=(c == KC - 1),
                      inc=(s_t, 1) if c == KC - 1 else None)
            guards.append((s_t, s_t.n))
            pb.wait("scalar", s_t)
            pb.op("scalar", "activation", gtile2[:, 0:W2H], ps_a[:, 0:W2H],
                  AF.Silu, inc=(s_s, 1))
            pb.wait("vector", s_s)
            pb.op("vector", "tensor_tensor", act2[:, fc * W2H:(fc + 1) * W2H],
                  gtile2[:, 0:W2H], ps_b[:, 0:W2H], ALU.mult, inc=(s_v, 1))
            pb.wait("tensor", s_v)
        for oc in range(KC):
            gq = g2 * KC + oc
            slot = gq % 2
            pb.wait_dma("scalar", s_wl, dtok[gq])
            if gq >= 2:
                gsem, gval = dguards[gq - 2]
                pb.wait("scalar", gsem, gval)
            dcav[gq] = pb.op(
                "scalar", "activation", wdc2[:, slot * FFN:(slot + 1) * FFN],
                wstgd2[:, 0:FFN], AF.Copy, inc=(s_ca, 1))
            pb.wait("tensor", s_ca, dcav[gq])
            if oc < KC - 1:
                issue_wd(gq + 1)
            for fc in range(FC):
                stat = wdc2[:, slot * FFN + fc * 128:
                            slot * FFN + (fc + 1) * 128]
                pb.op("tensor", "ldweights", stat)
                pb.op("tensor", "matmul", ps_a[:, 0:W2H], stat,
                      act2[:, fc * W2H:(fc + 1) * W2H], start=(fc == 0),
                      stop=(fc == FC - 1),
                      inc=(s_t, 1) if fc == FC - 1 else None)
            dguards.append((s_t, s_t.n))
            pb.wait("vector", s_t)
            colr = slice(oc * S2 + g2 * W2H, oc * S2 + (g2 + 1) * W2H)
            pb.op("vector", "tensor_add", env["f32a"][:, 0:W2H],
                  ps_a[:, 0:W2H], x2fm[:, colr])
            pb.drain("vector")
            pb.op("vector", "tensor_sub", env["f32a"][:, 0:W2H],
                  env["f32a"][:, 0:W2H], selfm[:, colr])
            pb.drain("vector")
            pb.op("vector", "tensor_tensor", x2fm[:, colr],
                  env["f32a"][:, 0:W2H],
                  scb[:, g2 * W2H:(g2 + 1) * W2H], ALU.mult, inc=(s_v, 1))
            pb.wait("tensor", s_v)
    pb.pop_scope()

    pb.push_scope()
    slotbc = pb.sbuf("r4_slbc", [128, TL], F32)
    pmat = pb.sbuf("r4_pmat", [128, SC * TL], BF16)
    deltT = pb.sbuf("r4_dT", [128, SC * D], BF16)
    oadd = pb.sbuf("r4_oadd", [128, KC * TL], BF16)
    jcol = pb.sbuf("r4_jcol", [128, 1], F32)

    pb.raw("gpsimd", lambda g: None)
    pb.wait("gpsimd", s_v)
    pb.op("gpsimd", "partition_broadcast", slotbc[:, :], myslot[0:1, :],
          channels=128, inc=(s_g, 1))
    pb.wait("vector", s_g)
    for jc in range(SC):
        pb.wait("vector", s_g)
        pb.op("gpsimd", "iota", jcol[:, 0:1], pattern=[[1, 1]],
              base=jc * 128, channel_multiplier=1,
              allow_small_or_imprecise_dtypes=True, inc=(s_g, 1))
        pb.wait("vector", s_g)
        pb.op("vector", "tensor_scalar", pmat[:, jc * TL:(jc + 1) * TL],
              slotbc[:, :], jcol[:, 0:1], None, ALU.is_equal, inc=(s_v, 1))
        pb.wait("gpsimd", s_v)
    pb.wait("tensor", s_v)
    for jc in range(SC):
        for c in range(KC):
            slot = c % 4
            dps = ps_t[:, slot * 128:slot * 128 + 128]
            tile_transpose(K, x2fm[:, c * S2 + jc * 128:
                                   c * S2 + (jc + 1) * 128], dps,
                           inc=(s_t, 1))
            pb.wait("vector", s_t)
            pb.op("vector", "tensor_copy",
                  deltT[:, jc * D + c * 128:jc * D + (c + 1) * 128], dps,
                  inc=(s_v, 1))
            pb.wait("tensor", s_v)
    for c in range(KC):
        for half in range(TL // 512):
            ps = ps_a if half % 2 == 0 else ps_b
            for jc in range(SC):
                stat = deltT[:, jc * D + c * 128:jc * D + (c + 1) * 128]
                pb.op("tensor", "ldweights", stat)
                pb.op("tensor", "matmul", ps[:, 0:512], stat,
                      pmat[:, jc * TL + half * 512:jc * TL + (half + 1) * 512],
                      start=(jc == 0), stop=(jc == SC - 1),
                      inc=(s_t, 1) if jc == SC - 1 else None)
            pb.wait("vector", s_t)
            pb.op("vector", "tensor_copy",
                  oadd[:, c * TL + half * 512:c * TL + (half + 1) * 512],
                  ps[:, 0:512], inc=(s_v, 1))
            pb.wait("tensor", s_v)
    for t in range(NT):
        pb.wait("tensor", s_v)
        for c in range(KC):
            slot = c % 4
            dps = ps_t[:, 512 + slot * 128:512 + slot * 128 + 128]
            tile_transpose(K, oadd[:, c * TL + t * 128:c * TL + t * 128 + 128],
                           dps, inc=(s_t, 1))
            pb.wait("vector", s_t)
            pb.op("vector", "tensor_copy", otok[:, c * 128:(c + 1) * 128],
                  dps, inc=(s_v, 1))
            pb.wait("tensor", s_v)
        pb.wait_dma("sync", s_ld)
        lv = pb.dma("sync", sq[:, :], dr_proc[t * 128:(t + 1) * 128, :],
                    s_ld)
        pb.wait_dma("vector", s_ld, lv)
        pb.op("vector", "tensor_add", otok[:, :], otok[:, :], sq[:, :])
        pb.drain("vector")
        ln_stats(pb, otok[:, :], stats, mean_c, rstd_c, D, eps, sq)
        pb.op("vector", "tensor_scalar", otok[:, :], otok[:, :],
              mean_c[:, 0:1], rstd_c[:, 0:1], ALU.subtract, ALU.mult,
              inc=(s_v, 1))
        pb.wait("sync", s_v)
        lv = pb.dma("sync", sq[:, :], Xd[t * 128:(t + 1) * 128, :], s_ld)
        pb.wait_dma("vector", s_ld, lv)
        pb.op("vector", "tensor_add", sq[:, :], sq[:, :], otok[:, :],
              inc=(s_v, 1))
        pb.wait("sync", s_v)
        pb.dma("sync", Xd[t * 128:(t + 1) * 128, :], sq[:, :], s_ld)
    pb.wait_dma("sync", s_ld)
    pb.pop_scope()
    pb.barrier()
